# version 18
# baseline (speedup 1.0000x reference)
import numpy as np
import concourse.bass as bass
import concourse.mybir as mybir
from concourse.bass_utils import run_bass_kernel_spmd

F32 = mybir.dt.float32
BF16 = mybir.dt.bfloat16
AF = mybir.ActivationFunctionType
ALU = mybir.AluOpType
AX = mybir.AxisListType

T = 2048
D = 1024
NT = 16
NC = 8
HD = 64
NE = 8
H0 = 2048
F1 = 4096
ENGINES = ("sync", "gpsimd", "tensor", "vector", "scalar")

GROUPS = [[0, 1, 2, 3, 4, 5]]

SUB_KIND = {0: "f", 1: "a", 2: "f", 3: "g", 4: "t", 5: "g"}
SUB_ALPHA = {0: 0.5, 1: 1.0, 2: 0.5, 3: 1.0, 4: 1.0, 5: 1.0}



def host_prep(inp):
    c = lambda a: np.ascontiguousarray(np.asarray(a), dtype=np.float32)
    out = {}
    _ = inp["L1_ffn_router_w"], inp["L1_ffn_router_b"]

    def per_ptile(wq, wk, wv):
        return np.stack([np.concatenate(
            [wq[:, 128*p:128*p+128], wk[:, 128*p:128*p+128], wv[:, 128*p:128*p+128]],
            axis=1) for p in range(8)])

    Wm = c(inp["L0_ffn_qkv_w"]).T
    out["f_wqkv"] = c(per_ptile(Wm[:, 0:D], Wm[:, D:2*D], Wm[:, 2*D:3*D]))
    bq = c(inp["L0_ffn_qkv_b"])
    bb = np.stack([np.stack([bq[128*p:128*p+128], bq[D+128*p:D+128*p+128],
                             bq[2*D+128*p:2*D+128*p+128]]) for p in range(8)])
    out["f_bqkv"] = c(bb)
    rot = lambda v: np.concatenate([v[32:64], v[0:32], v[96:128], v[64:96]])
    out["f_bqkrot"] = c(np.stack([np.stack([rot(bb[p, 0]), rot(bb[p, 1])])
                                  for p in range(8)]))
    out["f_wout"] = c(c(inp["L0_ffn_out_w"]).T)
    out["f_outb"] = c(inp["L0_ffn_out_b"]).reshape(1, D)
    out["f_n1w"] = c(inp["L0_ffn_norm1_w"]).reshape(1, D)
    out["f_n2w"] = c(inp["L0_ffn_norm2_w"]).reshape(1, D)
    out["f_gateT"] = c(c(inp["L0_ffn_gate_w"]).T)
    out["f_w1T"] = c(np.transpose(c(inp["L0_ffn_w1"]), (0, 2, 1)))
    out["f_w2T"] = c(np.transpose(c(inp["L0_ffn_w2"]), (0, 2, 1)))
    out["f_w3T"] = c(np.transpose(c(inp["L0_ffn_w3"]), (0, 2, 1)))
    out["f_b1T"] = c(np.transpose(c(inp["L0_ffn_b1"]).reshape(8, 16, 128), (0, 2, 1)))
    out["f_b2T"] = c(np.transpose(c(inp["L0_ffn_b2"]).reshape(8, 16, 128), (0, 2, 1)))
    out["f_b3"] = c(inp["L0_ffn_b3"])

    W = c(inp["L0_attn_w_qkv"])
    out["a_w"] = c(per_ptile(W[:, 0:D], W[:, D:2*D], W[:, 2*D:3*D]))
    out["a_wout"] = c(inp["L0_attn_w_out"])
    out["a_gamma"] = c(inp["L0_attn_gamma"]).reshape(1, D)

    out["t_w"] = c(per_ptile(c(inp["L1_attn_Wq"]), c(inp["L1_attn_Wk"]),
                             c(inp["L1_attn_Wv"])))
    out["t_wr"] = c(inp["L1_attn_Wr"])

    w1 = c(inp["L1_ffn_w1"][0])
    out["g_w1"] = c(np.transpose(w1.reshape(D, 32, 128), (1, 0, 2)))
    out["g_b1T"] = c(c(inp["L1_ffn_b1"][0]).reshape(32, 128).T)
    out["g_w2"] = c(inp["L1_ffn_w2"][0])
    out["g_b2"] = c(inp["L1_ffn_b2"][0]).reshape(1, D)
    out["g_gam"] = c(inp["L1_ffn_gamma"]).reshape(1, D)
    out["g_bet"] = c(inp["L1_ffn_beta"]).reshape(1, D)

    out["c_ident"] = np.eye(128, dtype=np.float32)
    out["c_ones_col"] = np.ones((128, 1), dtype=np.float32)
    out["c_ones_row"] = np.ones((1, 128), dtype=np.float32)
    invf = 1.0 / 10000.0 ** (np.arange(0, HD, 2, dtype=np.float64) / HD)
    fr = np.arange(T, dtype=np.float64)[None, :] * invf[:, None]
    c64 = np.concatenate([np.cos(fr), np.cos(fr)], axis=0)
    s64 = np.concatenate([np.sin(fr), np.sin(fr)], axis=0)
    ssgn = np.concatenate([-s64[0:32], s64[32:64]], axis=0)
    out["ropeC"] = c(np.concatenate([c64, c64], axis=0))
    out["ropeS"] = c(np.concatenate([ssgn, ssgn], axis=0))
    aa = np.arange(128)[:, None]
    cc = np.arange(1920)[None, :]
    out["stair"] = c(np.where(aa + 896 <= cc, 1.0, np.exp(-50.0)))
    return out



class Sem:
    def __init__(self, nc, name):
        self.h = nc.alloc_semaphore(name)
        self.n = 0

    def inc(self, k=1):
        self.n += k
        return self.n


class Bld:
    def __init__(self, nc):
        self.nc = nc
        self.ops = {e: [] for e in ENGINES}
        self.WX = Sem(nc, "WX")
        self.WG = Sem(nc, "WG")
        self.WSC = Sem(nc, "WSC")
        self.WS0 = Sem(nc, "WS0")
        self.WS1 = Sem(nc, "WS1")
        self.T = Sem(nc, "T")
        self.V = Sem(nc, "V")
        self.S = Sem(nc, "S")

    def op(self, eng, fn, waits=(), inc=None, dma=False):
        amt = 16 if dma else 1
        dma_sems = (self.WX, self.WG, self.WSC, self.WS0, self.WS1)
        waits = list(waits)
        wd = {}
        for s, v in waits:
            if v and v > 0:
                if s in dma_sems:
                    v = s.n
                wd[s.h] = max(wd.get(s.h, 0), v)
        if dma and inc in dma_sems and inc.n > 0:
            wd[inc.h] = max(wd.get(inc.h, 0), inc.n)
        tgt = inc.inc(amt) if inc is not None else None
        ih = inc.h if inc is not None else None

        def emitter(e, _wd=wd, _fn=fn, _ih=ih, _amt=amt):
            for h, v in _wd.items():
                e.wait_ge(h, v)
            ins = _fn(e)
            if _ih is not None:
                if ins is None:
                    ins = e.nop()
                ins.then_inc(_ih, _amt)
        self.ops[eng].append(emitter)
        return tgt

    def finalize(self):
        with self.nc.Block() as blk:
            for eng in ENGINES:
                def run(e, _l=self.ops[eng]):
                    for f in _l:
                        f(e)
                getattr(blk, eng)(run)



def build_program(subs, debug=None):
    nc = bass.Bass("TRN2", target_bir_lowering=False)
    b = Bld(nc)
    WX, WG, WSC, TT_, V_, S_ = b.WX, b.WG, b.WSC, b.T, b.V, b.S

    need = {SUB_KIND[s] for s in subs}
    dr = {}

    def dram(name, shape):
        dr[name] = nc.dram_tensor(name, list(shape), F32, kind="ExternalInput").ap()

    dram("xin", (T, D))
    xout = nc.dram_tensor("xout", [T, D], F32, kind="ExternalOutput").ap()
    xmid = [nc.dram_tensor(f"xmid{i}", [T, D], F32, kind="Internal").ap()
            for i in range(len(subs) - 1)]
    xs_seq = [dr["xin"]] + xmid + [xout]

    dram("c_ident", (128, 128))
    dram("c_ones_col", (128, 1))
    dram("c_ones_row", (1, 128))
    if "f" in need:
        for n, s in [("f_wqkv", (8, D, 384)), ("f_bqkv", (8, 3, 128)),
                     ("f_bqkrot", (8, 2, 128)), ("f_wout", (D, D)), ("f_outb", (1, D)),
                     ("f_n1w", (1, D)), ("f_n2w", (1, D)), ("f_gateT", (D, 8)),
                     ("f_w1T", (8, D, H0)), ("f_w2T", (8, D, H0)), ("f_w3T", (8, H0, D)),
                     ("f_b1T", (8, 128, 16)), ("f_b2T", (8, 128, 16)), ("f_b3", (8, D)),
                     ("ropeC", (128, T)), ("ropeS", (128, T))]:
            dram(n, s)
    if "a" in need:
        for n, s in [("a_w", (8, D, 384)), ("a_wout", (D, D)), ("a_gamma", (1, D))]:
            dram(n, s)
    if "t" in need:
        for n, s in [("t_w", (8, D, 384)), ("t_wr", (D, D)), ("stair", (128, 1920))]:
            dram(n, s)
    if "g" in need:
        for n, s in [("g_w1", (32, D, 128)), ("g_b1T", (128, 32)), ("g_w2", (F1, D)),
                     ("g_b2", (1, D)), ("g_gam", (1, D)), ("g_bet", (1, D))]:
            dram(n, s)

    sb = nc.alloc_sbuf_tensor
    AZ = sb("AZ", [128, NT * D], F32)
    AXNT = sb("AXNT", [128, NC * T], BF16)
    ABIG = sb("ABIG", [128, 16384], BF16)
    AW = sb("AW", [128, 10368], BF16)
    ATAB = sb("ATAB", [128, 4096], BF16) if ("f" in need or "t" in need) else None
    AF32 = sb("AF32", [128, 4096], F32)
    ASM = sb("ASM", [128, 4096], BF16)
    SROW = sb("SROW", [128, 128], F32)
    SCOMB = sb("SCOMB", [128, 128], F32)
    SMOE = sb("SMOE", [128, 48], F32)
    SBIAS = sb("SBIAS", [128, 64], F32)
    SGATE = sb("SGATE", [128, 64], BF16) if "f" in need else None
    B1L = sb("B1L", [128, 32], F32) if "g" in need else None
    IDENT = sb("IDENT", [128, 128], BF16)
    ONESC = sb("ONESC", [128, 1], BF16)
    ONESR = sb("ONESR", [1, 128], F32)
    ONESRB = sb("ONESRB", [1, 128], BF16)
    ROWT = sb("ROWT", [1, 1024], F32)
    RROW = sb("RROW", [1, 1024], BF16)
    GROW = sb("GROW", [1, 1024], F32)
    GROWB = sb("GROWB", [1, 1024], BF16)
    NRMB = sb("NRMB", [128, 1024], F32)
    NRMB2 = sb("NRMB2", [128, 1024], F32)
    SQD = sb("SQD", [128, 1024], BF16)

    PA = nc.alloc_psum_tensor("PA", [128, 2048], F32)
    PB = nc.alloc_psum_tensor("PB", [128, 1024], F32)
    PC = nc.alloc_psum_tensor("PC", [128, 512], F32)
    PT = nc.alloc_psum_tensor("PT", [128, 1024], BF16)

    EPS5 = SBIAS[:, 0:1]
    EPS6 = SBIAS[:, 1:2]
    EPS24 = SBIAS[:, 2:3]
    CNEG = SBIAS[:, 3:4]
    BQ, BK = SBIAS[:, 4:5], SBIAS[:, 5:6]
    BQR, BKR = SBIAS[:, 6:7], SBIAS[:, 7:8]
    B1C = SBIAS[:, 16:32]
    B2C = SBIAS[:, 32:48]
    XNTI = [ASM[:, 2048:3072], ASM[:, 3072:4096]]
    QT = ASM[:, 0:2048]
    KT = ASM[:, 2048:4096]

    def col128(ap1d):
        return ap1d.rearrange("(q a) -> q a", a=1)

    st = {"bar_t": 0, "bar_v": 0, "bar_s": 0, "bar_g": 0,
          "zdone": 0, "pb_v": 0, "grow_t": 0,
          "ztile_v": {i: 0 for i in range(NT)},
          "xnt_tile_s": {i: 0 for i in range(NT)},
          "xnt_last_s": 0,
          "af32_free_v": 0}

    def bars():
        return [(TT_, st["bar_t"]), (V_, st["bar_v"]), (S_, st["bar_s"])]

    def set_bars():
        st["bar_t"], st["bar_v"], st["bar_s"] = TT_.n, V_.n, S_.n

    b.op("gpsimd", lambda e: e.dma_start(IDENT[:], dr["c_ident"]), inc=WG, dma=True)
    b.op("gpsimd", lambda e: e.dma_start(ONESC[:], dr["c_ones_col"]), inc=WG, dma=True)
    b.op("sync", lambda e: e.dma_start(ONESR[:], dr["c_ones_row"]), inc=WX, dma=True)
    b.op("gpsimd", lambda e: e.dma_start(ONESRB[:], dr["c_ones_row"]), inc=WG, dma=True)
    if "f" in need:
        b.op("gpsimd", lambda e: e.dma_start(
            SGATE[:].rearrange("p (j g) -> p j g", j=NC),
            dr["f_gateT"].rearrange("(j q) g -> q j g", q=128)), inc=WG, dma=True)
    if "g" in need:
        b.op("sync", lambda e: e.dma_start(B1L[:], dr["g_b1T"]), inc=WX, dma=True)
    consts_g = WG.n
    consts_x = WX.n
    cm = b.op("vector", lambda e: e.memset(SBIAS[:, 0:16], 0.0), inc=V_)
    cm = b.op("vector", lambda e: e.memset(EPS5, 1e-5), waits=[(V_, cm)], inc=V_)
    cm = b.op("vector", lambda e: e.memset(EPS6, 1e-6), waits=[(V_, cm)], inc=V_)
    cm = b.op("vector", lambda e: e.memset(EPS24, 1e-24), waits=[(V_, cm)], inc=V_)
    cm = b.op("vector", lambda e: e.memset(SBIAS[:, 8:9], float(np.log(0.1275))),
              waits=[(V_, cm)], inc=V_)
    consts_v = V_.n
    st["bar_v"] = consts_v


    def row_bcast(dst, src, width=D, scale=None, add=None, post_scale=None):
        wx = b.op("sync", lambda e: e.dma_start(GROW[:, 0:width], src),
                  waits=[(TT_, st["grow_t"])] + bars(), inc=WX, dma=True)
        vv = 0
        if scale is not None:
            vv = b.op("vector", lambda e: e.tensor_scalar_mul(
                GROW[:, 0:width], GROW[:, 0:width], scale), waits=[(WX, wx)], inc=V_)
        if add is not None:
            vv = b.op("vector", lambda e: e.tensor_scalar_add(
                GROW[:, 0:width], GROW[:, 0:width], add), waits=[(WX, wx)], inc=V_)
        if post_scale is not None:
            vv = b.op("vector", lambda e: e.tensor_scalar_mul(
                GROW[:, 0:width], GROW[:, 0:width], post_scale),
                waits=[(WX, wx), (V_, vv)], inc=V_)
        tv = b.op("tensor", lambda e: [
            e.matmul(PB[:, 512*q:512*(q+1)], ONESR[:], GROW[0:1, 512*q:512*(q+1)],
                     start=True, stop=True) for q in range(width // 512)][-1],
            waits=[(WX, max(wx, consts_x)), (V_, vv), (V_, st["pb_v"]),
                   (TT_, TT_.n)], inc=TT_)
        st["grow_t"] = tv
        vv = b.op("vector", lambda e: e.tensor_copy(dst[:, 0:width], PB[:, 0:width]),
                  waits=[(TT_, tv), (V_, st["bar_v"])], inc=V_)
        st["pb_v"] = vv
        return vv

    def emit_ln(xin_ap):
        wx = b.op("sync", lambda e: e.dma_start(
            AZ[:].rearrange("p (i d) -> p i d", i=NT),
            xin_ap.rearrange("(i p) d -> p i d", p=128)),
            waits=bars() + [(b.WS0, b.WS0.n), (b.WS1, b.WS1.n)], inc=WX, dma=True)
        sqp = 0
        for i in range(NT):
            b.op("vector", lambda e, i=i: e.reduce_sum(
                SROW[:, i:i+1], AZ[:, D*i:D*(i+1)], axis=AX.X),
                waits=[(WX, wx)] + bars(), inc=V_)
            sqp = b.op("scalar", lambda e, i=i: e.activation(
                SQD[:], AZ[:, D*i:D*(i+1)], AF.Square,
                accum_out=SROW[:, 16+i:17+i]),
                waits=[(WX, wx), (S_, sqp)] + bars(), inc=S_)
        v_st, s_st = V_.n, S_.n
        c1 = b.op("vector", lambda e: e.tensor_scalar_mul(
            SROW[:, 32:48], SROW[:, 0:16], 1.0 / D),
            waits=[(V_, v_st), (S_, s_st)], inc=V_)
        c2 = b.op("vector", lambda e: e.tensor_mul(
            SROW[:, 48:64], SROW[:, 32:48], SROW[:, 32:48]),
            waits=[(V_, c1)], inc=V_)
        vv = b.op("vector", lambda e: e.scalar_tensor_tensor(
            SROW[:, 48:64], SROW[:, 16:32], 1.0 / D, SROW[:, 48:64],
            ALU.mult, ALU.subtract), waits=[(V_, c2)], inc=V_)
        m1 = b.op("scalar", lambda e: e.activation(SROW[:, 96:112], SROW[:, 48:64],
             AF.Ln, bias=EPS5), waits=[(V_, vv), (V_, consts_v)], inc=S_)
        sv = b.op("scalar", lambda e: e.activation(SROW[:, 64:80], SROW[:, 96:112],
                  AF.Exp, scale=-0.5), waits=[(S_, m1)], inc=S_)
        nm = b.op("vector", lambda e: e.tensor_scalar_mul(
            SROW[:, 80:96], SROW[:, 32:48], -1.0), waits=[(V_, c1)], inc=V_)
        for i in range(NT):
            tsm = b.op("vector", lambda e, i=i: e.tensor_scalar(
                AZ[:, D*i:D*(i+1)], AZ[:, D*i:D*(i+1)],
                SROW[:, 80+i:81+i], SROW[:, 64+i:65+i],
                ALU.add, ALU.mult), waits=[(S_, sv), (V_, nm)], inc=V_)
            st["ztile_v"][i] = tsm
        st["zdone"] = V_.n
        set_bars()

    def emit_finale(xin_ap, xo_ap, alpha, fuse_ln=False):
        fin_v = {}
        ws_after = {}
        WS = [b.WS0, b.WS1]
        sqp = 0
        afv = st["af32_free_v"]
        for i in range(NT):
            par = i % 2
            xld = AF32[:, 1024*par:1024*(par+1)]
            w_extra = [(b.WS0, b.WS0.n), (b.WS1, b.WS1.n)] if i < 2 else []
            wx = b.op("sync", lambda e, i=i, xld=xld: e.dma_start(
                xld, xin_ap[128*i:128*(i+1), :]),
                waits=[(V_, afv), (V_, fin_v.get(i-2, 0))] + w_extra,
                inc=WX, dma=True)
            vv = b.op("vector", lambda e, i=i, xld=xld: e.scalar_tensor_tensor(
                AZ[:, D*i:D*(i+1)], AZ[:, D*i:D*(i+1)], alpha, xld,
                ALU.mult, ALU.add),
                waits=[(WX, wx), (WS[par], WS[par].n),
                       (V_, st["ztile_v"][i])], inc=V_)
            fin_v[i] = vv
            ws_after[i] = b.op("sync", lambda e, i=i: e.dma_start(
                xo_ap[128*i:128*(i+1), :], AZ[:, D*i:D*(i+1)]), waits=[(V_, vv)],
                inc=WS[par], dma=True)
            if fuse_ln:
                b.op("vector", lambda e, i=i: e.reduce_sum(
                    SROW[:, i:i+1], AZ[:, D*i:D*(i+1)], axis=AX.X),
                    waits=[(V_, vv)], inc=V_)
                sqp = b.op("scalar", lambda e, i=i: e.activation(
                    SQD[:], AZ[:, D*i:D*(i+1)], AF.Square,
                    accum_out=SROW[:, 16+i:17+i]),
                    waits=[(V_, vv), (S_, sqp)], inc=S_)
        st["bar_v"] = V_.n
        if not fuse_ln:
            st["zdone"] = 0
            return 0
        v_st, s_st = V_.n, S_.n
        c1 = b.op("vector", lambda e: e.tensor_scalar_mul(
            SROW[:, 32:48], SROW[:, 0:16], 1.0 / D),
            waits=[(V_, v_st), (S_, s_st)], inc=V_)
        c2 = b.op("vector", lambda e: e.tensor_mul(
            SROW[:, 48:64], SROW[:, 32:48], SROW[:, 32:48]),
            waits=[(V_, c1)], inc=V_)
        vv = b.op("vector", lambda e: e.scalar_tensor_tensor(
            SROW[:, 48:64], SROW[:, 16:32], 1.0 / D, SROW[:, 48:64],
            ALU.mult, ALU.subtract), waits=[(V_, c2)], inc=V_)
        m1 = b.op("scalar", lambda e: e.activation(SROW[:, 96:112], SROW[:, 48:64],
             AF.Ln, bias=EPS5), waits=[(V_, vv), (V_, consts_v)], inc=S_)
        sv = b.op("scalar", lambda e: e.activation(SROW[:, 64:80], SROW[:, 96:112],
                  AF.Exp, scale=-0.5), waits=[(S_, m1)], inc=S_)
        nm = b.op("vector", lambda e: e.tensor_scalar_mul(
            SROW[:, 80:96], SROW[:, 32:48], -1.0), waits=[(V_, c1)], inc=V_)
        for i in range(NT):
            par = i % 2
            tsm = b.op("vector", lambda e, i=i: e.tensor_scalar(
                AZ[:, D*i:D*(i+1)], AZ[:, D*i:D*(i+1)],
                SROW[:, 80+i:81+i], SROW[:, 64+i:65+i],
                ALU.add, ALU.mult),
                waits=[(S_, sv), (V_, nm), (WS[par], ws_after[i])], inc=V_)
            st["ztile_v"][i] = tsm
        st["zdone"] = V_.n
        set_bars()
        return 0

    def emit_to_xnt(mode, nrm_v=0, nrmb=None):
        if nrmb is None:
            nrmb = NRMB
        zt = st["ztile_v"]
        sv = 0
        if mode != "none":
            sqp = 0
            for i in range(NT):
                sqp = b.op("scalar", lambda e, i=i: e.activation(
                    SQD[:], AZ[:, D*i:D*(i+1)], AF.Square,
                    accum_out=SROW[:, i:i+1]),
                    waits=[(V_, zt[i]), (S_, sqp)], inc=S_)
            s_st = S_.n
            if mode == "rms_w":
                vv = b.op("vector", lambda e: e.tensor_scalar_mul(
                    SROW[:, 16:32], SROW[:, 0:16], 1.0 / D), waits=[(S_, s_st)], inc=V_)
                m1 = b.op("scalar", lambda e: e.activation(SROW[:, 96:112], SROW[:, 16:32],
                     AF.Ln, bias=EPS6), waits=[(V_, vv), (V_, consts_v)], inc=S_)
                sv = b.op("scalar", lambda e: e.activation(
                    SROW[:, 32:48], SROW[:, 96:112], AF.Exp, scale=-0.5),
                    waits=[(S_, m1)], inc=S_)
            else:
                vv = b.op("vector", lambda e: e.tensor_scalar(
                    SROW[:, 16:32], SROW[:, 0:16], EPS24, None, ALU.max),
                    waits=[(S_, s_st), (V_, consts_v)], inc=V_)
                m1 = b.op("scalar", lambda e: e.activation(SROW[:, 96:112], SROW[:, 16:32],
                     AF.Ln), waits=[(V_, vv)], inc=S_)
                sv = b.op("scalar", lambda e: e.activation(SROW[:, 32:48], SROW[:, 96:112],
                     AF.Exp, scale=-0.5), waits=[(S_, m1)], inc=S_)
        prev_evac = {}
        last_prev = st["xnt_last_s"]
        for i in range(NT):
            par = i % 2
            xt = XNTI[par]
            guard = prev_evac.get(i-2, last_prev)
            if mode == "none":
                vv = b.op("scalar", lambda e, xt=xt, i=i: e.copy(
                    xt, AZ[:, D*i:D*(i+1)]), waits=[(V_, zt[i])], inc=S_)
                vw = (S_, vv)
            else:
                vv = b.op("vector", lambda e, i=i, xt=xt, nrmb=nrmb:
                          e.scalar_tensor_tensor(
                    xt, AZ[:, D*i:D*(i+1)], SROW[:, 32+i:33+i], nrmb[:],
                    ALU.mult, ALU.mult),
                    waits=[(S_, sv), (V_, nrm_v), (V_, zt[i]), (S_, guard)], inc=V_)
                vw = (V_, vv)
            tv = b.op("tensor", lambda e, xt=xt: [
                e.transpose(PT[:, 128*j:128*(j+1)], xt[:, 128*j:128*(j+1)], IDENT[:])
                for j in range(8)][-1],
                waits=[vw, (S_, prev_evac.get(i-1, last_prev)), (WG, consts_g)],
                inc=TT_)
            prev_evac[i] = b.op("scalar", lambda e, i=i: e.copy(
                AXNT[:].rearrange("p (j c) -> p j c", j=NC)[:, :, 128*i:128*(i+1)],
                PT[:].rearrange("p (j c) -> p j c", j=NC)),
                waits=[(TT_, tv)], inc=S_)
            st["xnt_tile_s"][i] = prev_evac[i]
        st["xnt_s"] = prev_evac[NT-1]
        st["xnt_last_s"] = prev_evac[NT-1]
        set_bars()

    def emit_attention(kind):
        wname = {"f": "f_wqkv", "a": "a_w", "t": "t_w"}[kind]
        woname = {"f": "f_wout", "a": "a_wout", "t": "t_wr"}[kind]
        xnt_ts = dict(st["xnt_tile_s"])
        causal = kind == "t"
        rope = kind == "f"
        VAUG = AW[:, 8192:10272]
        VAUG4 = VAUG.rearrange("p (k h c) -> p k h c", k=NT, c=65)
        EB = [AW[:, 6144:7168], AW[:, 7168:8192]]

        st["pend_evac"] = None
        ones_v = b.op("vector", lambda e: e.memset(
            VAUG.rearrange("p (h c) -> p h c", c=65)[:, :, 64:65], 1.0),
            waits=bars(), inc=V_)

        def flush_evac():
            nonlocal b_evac_v, rb_v, ot_v
            pe_ = st.get("pend_evac")
            if pe_ is None:
                return
            pp_, hl_, qh_, dm, as0, as1 = pe_
            st["pend_evac"] = None
            base_ = 64 * hl_
            tvs = []
            for q in range(2):
                tvr = b.op("tensor", lambda e, q=q: e.matmul(
                    PC[0:64, 0:512], ONESRB[0:1, 0:64],
                    RROW[0:1, 512*q:512*(q+1)], start=True, stop=True),
                    waits=[(S_, dm), (V_, rb_v), (WG, consts_g),
                           (V_, st.get("csqm_l", 0))], inc=TT_)
                rb_v = b.op("vector", lambda e, q=q: e.tensor_copy(
                    AF32[0:64, 512*q:512*(q+1)], PC[0:64, 0:512]),
                    waits=[(TT_, tvr)], inc=V_)
            st["rb_t"] = TT_.n
            if debug == "tagot":
                b_evac_v = b.op("vector",
                                lambda e, pp_=pp_, base_=base_, qh_=qh_: e.memset(
                    ABIG[:, 2048*pp_:2048*(pp_+1)][base_:base_+64,
                                                   1024*qh_:1024*(qh_+1)],
                    float(100*pp_ + 10*(base_//64) + qh_ + 1)),
                    waits=[(V_, rb_v), (TT_, csq_t)], inc=V_)
            else:
                b_evac_v = b.op("vector",
                                lambda e, pp_=pp_, base_=base_, qh_=qh_: e.tensor_mul(
                    ABIG[:, 2048*pp_:2048*(pp_+1)][base_:base_+64,
                                                   1024*qh_:1024*(qh_+1)],
                    PB[0:64, 0:1024], AF32[0:64, 0:1024]),
                    waits=[(V_, rb_v), (TT_, csq_t)], inc=V_)
            ot_v = b_evac_v
        tab_g = st.get("tab_g", 0)
        if rope and st.get("tab") != "rope":
            tab_g = b.op("gpsimd", lambda e: e.dma_start(
                ATAB[:, 0:2048], dr["ropeC"]), waits=bars(), inc=WG, dma=True)
            tab_g = b.op("gpsimd", lambda e: e.dma_start(
                ATAB[:, 2048:4096], dr["ropeS"]), inc=WG, dma=True)
            st["tab"] = "rope"
            st["tab_g"] = tab_g
        if causal and st.get("tab") != "stair":
            tab_g = b.op("gpsimd", lambda e: e.dma_start(
                ATAB[:, 0:1920], dr["stair"]), waits=bars(), inc=WG, dma=True)
            st["tab"] = "stair"
            st["tab_g"] = tab_g
        wslab_t = {0: st["bar_t"], 1: st["bar_t"]}
        vaug_t = 0
        e_t = {0: 0, 1: 0}
        b_evac_v = 0
        rb_v = 0
        cneg_s = 0
        asm_t = st["bar_t"]
        bias_v = 0
        vb_t = 0
        csqm_v = 0
        pa_s = 0
        ot_v = 0
        zv0 = st["zdone"]

        for p in range(8):
            par = p % 2
            WSL = AW[:, 3072*par:3072*(par+1)]
            WSLv = WSL.rearrange("p (j f) -> p j f", j=NC)
            wg = b.op("gpsimd", lambda e, p=p, WSL=WSL: e.dma_start(
                WSL.rearrange("p (j f) -> p j f", j=NC),
                dr[wname][p].rearrange("(j q) f -> q j f", q=128)),
                waits=[(TT_, wslab_t[par])], inc=WG, dma=True)
            wb = 0
            if rope:
                for (dst, src) in ((BQ, dr["f_bqkv"][p, 0]), (BK, dr["f_bqkv"][p, 1]),
                                   (BQR, dr["f_bqkrot"][p, 0]),
                                   (BKR, dr["f_bqkrot"][p, 1])):
                    wb = b.op("sync", lambda e, dst=dst, src=src: e.dma_start(
                        dst, col128(src)), waits=[(V_, bias_v)], inc=WX, dma=True)
            qk_v = 0
            sq_s = {}
            DUMP = ABIG[:, 2048*p:2048*(p+1)]

            def rope_or_copy(DST, bcol, bcolr, sqm, tv):
                if rope:
                    b.op("vector", lambda e, bcol=bcol: e.scalar_tensor_tensor(
                        AF32[:, 0:2048], PA[:], bcol, ATAB[:, 0:2048],
                        ALU.add, ALU.mult),
                        waits=[(TT_, tv), (WX, wb), (TT_, asm_t),
                               (V_, b_evac_v), (WG, tab_g), (S_, sqm)], inc=V_)
                    r2 = 0
                    for (od, sd) in ((0, 32), (32, 0), (64, 96), (96, 64)):
                        r2 = b.op("vector", lambda e, od=od, sd=sd, bcolr=bcolr:
                             e.scalar_tensor_tensor(
                                 AF32[od:od+32, 2048:4096], PA[sd:sd+32, :],
                                 bcolr[od:od+32, :], ATAB[od:od+32, 2048:4096],
                                 ALU.add, ALU.mult), inc=V_)
                    return b.op("vector", lambda e, DST=DST: e.tensor_add(
                        DST, AF32[:, 0:2048], AF32[:, 2048:4096]),
                        waits=[(V_, r2)], inc=V_)
                return b.op("vector", lambda e, DST=DST: e.tensor_copy(DST, PA[:]),
                            waits=[(TT_, tv), (TT_, asm_t), (V_, b_evac_v),
                                   (S_, sqm)], inc=V_)

            def emit_v_half(h0, h1, gv):
                for tt in range(h0, h1):
                    ph = tt % 2
                    tv = b.op("tensor", lambda e, tt=tt, ph=ph, WSLv=WSLv,
                              last=(kind != "f"): [
                        e.matmul(PB[:, 512*ph:512*ph+128],
                                 AXNT[:, T*j+128*tt:T*j+128*(tt+1)],
                                 WSLv[:, j, 256:384],
                                 start=(j == 0), stop=(last and j == NC-1))
                        for j in range(NC)][-1],
                        waits=[(S_, vev.get(tt-2, 0)), (WG, wg),
                               (S_, xnt_ts[tt]),
                               (V_, ones_v), (TT_, vaug_t), (V_, st["pb_v"]),
                               (V_, b_evac_v)], inc=TT_)
                    if kind == "f":
                        tv = b.op("tensor", lambda e, ph=ph: e.matmul(
                            PB[:, 512*ph:512*ph+128], ONESRB[:], GROWB[0:1, 0:128],
                            start=False, stop=True), waits=[(WG, gv)], inc=TT_)
                    vev[tt] = b.op("scalar", lambda e, tt=tt, ph=ph: e.copy(
                        VAUG4[:, tt, :, 0:64],
                        PB[:, 512*ph:512*ph+128].rearrange("p (h c) -> p h c", h=2)),
                        waits=[(TT_, tv), (V_, ones_v)], inc=S_)

            def emit_csq(which, sqm):
                cm = 0
                for q in range(4):
                    tv2 = b.op("tensor", lambda e, q=q, DUMP=DUMP: e.matmul(
                        PC[0:1, 0:512], ONESC[:], DUMP[:, 512*q:512*(q+1)],
                        start=True, stop=True),
                        waits=[(S_, sqm), (V_, st.get("csqm_l", 0)), (V_, rb_v),
                               (WG, consts_g)], inc=TT_)
                    cm = b.op("vector", lambda e, which=which, q=q: e.reduce_max(
                        SROW[0:1, 96+4*which+q:97+4*which+q], PC[0:1, 0:512],
                        axis=AX.X), waits=[(TT_, tv2)], inc=V_)
                    st["csqm_l"] = cm
                return cm

            gv = 0
            if kind == "f":
                gv = b.op("gpsimd", lambda e, p=p: e.dma_start(
                    GROWB[0:1, 0:128],
                    dr["f_bqkv"][p, 2].rearrange("(a f) -> a f", a=1)),
                    waits=[(TT_, vb_t)], inc=WG, dma=True)
            vev = {}
            for q in range(4):
                tvq = b.op("tensor", lambda e, WSLv=WSLv, q=q: [
                    e.matmul(PA[:, 512*q:512*(q+1)], WSLv[:, j, 0:128],
                             AXNT[:, T*j+512*q:T*j+512*(q+1)],
                             start=(j == 0), stop=(j == NC-1))
                    for j in range(NC)][-1],
                    waits=[(WG, wg), (S_, xnt_ts[4*q+3]), (TT_, asm_t),
                           (S_, pa_s), (V_, rb_v)], inc=TT_)
            flush_evac()
            sq_s[0] = b.op(
                "scalar",
                (lambda e, DUMP=DUMP: e.activation(DUMP, PA[:], AF.Square, bias=BQ))
                if rope else
                (lambda e, DUMP=DUMP: e.activation(DUMP, PA[:], AF.Square)),
                waits=[(TT_, tvq), (WX, wb), (S_, cneg_s), (V_, ot_v),
                       (TT_, st["bar_t"]), (TT_, st.get("sqd_t", 0))], inc=S_)
            qk_v = rope_or_copy(QT, BQ, BQR, sq_s[0], tvq)
            bias_v = qk_v
            emit_v_half(0, 8, gv)
            cmq = emit_csq(0, sq_s[0])
            for q in range(4):
                tvk = b.op("tensor", lambda e, WSLv=WSLv, q=q: [
                    e.matmul(PA[:, 512*q:512*(q+1)], WSLv[:, j, 128:256],
                             AXNT[:, T*j+512*q:T*j+512*(q+1)],
                             start=(j == 0), stop=(j == NC-1))
                    for j in range(NC)][-1],
                    waits=[(WG, wg), (S_, xnt_ts[4*q+3]), (S_, sq_s[0]),
                           (V_, qk_v)], inc=TT_)
            sq_s[1] = b.op("scalar", (lambda e, DUMP=DUMP: e.activation(
                DUMP, PA[:], AF.Square, bias=BK)) if rope else
                (lambda e, DUMP=DUMP: e.activation(DUMP, PA[:], AF.Square)),
                waits=[(TT_, tvk), (WX, wb), (S_, sq_s[0]), (V_, cmq)], inc=S_)
            qk_v = rope_or_copy(KT, BK, BKR, sq_s[1], tvk)
            bias_v = qk_v
            emit_v_half(8, NT, gv)
            emit_csq(1, sq_s[1])
            v_ready = vev[NT-1]
            if kind == "f":
                vb_t = TT_.n
            csq_t = TT_.n
            st["sqd_t"] = TT_.n
            if debug == "qt" and p == 0:
                last = b.op("gpsimd", lambda e: e.dma_start(
                    xout.rearrange("(a p) d -> p a d", p=128)[:, 0:4, :],
                    ASM[:, 0:4096].rearrange("p (a d) -> p a d", a=4)),
                    waits=[(V_, qk_v)], inc=WG, dma=True)
                b.op("sync", lambda e: e.wait_ge(b.WG.h, last))
                b.finalize()
                return "OT_DUMP"
            vv = b.op("vector", lambda e: e.reduce_max(
                SROW[0:1, 104:105], SROW[0:1, 96:100], axis=AX.X),
                waits=[(V_, st.get("csqm_l", 0))], inc=V_)
            vv = b.op("vector", lambda e: e.reduce_max(
                SROW[0:1, 105:106], SROW[0:1, 100:104], axis=AX.X),
                waits=[(V_, st.get("csqm_l", 0))], inc=V_)
            b.op("scalar", lambda e: e.activation(SROW[0:1, 106:107],
                 SROW[0:1, 104:105], AF.Ln), waits=[(V_, vv)], inc=S_)
            sv = b.op("scalar", lambda e: e.activation(SROW[0:1, 107:108],
                      SROW[0:1, 105:106], AF.Ln), inc=S_)
            vv = b.op("vector", lambda e: e.tensor_add(
                SROW[0:1, 106:107], SROW[0:1, 106:107], SROW[0:1, 107:108]),
                waits=[(S_, sv)], inc=V_)
            sv = b.op("scalar", lambda e: e.activation(SROW[0:1, 108:109],
                      SROW[0:1, 106:107], AF.Exp, scale=0.5,
                      bias=SBIAS[0:1, 8:9]), waits=[(V_, vv)], inc=S_)
            tv = b.op("tensor", lambda e: e.matmul(
                PC[:, 0:1], ONESR[:], SROW[0:1, 108:109], start=True, stop=True),
                waits=[(S_, sv), (V_, st.get("csqm_l", 0))], inc=TT_)
            cneg_v = b.op("vector", lambda e: e.tensor_scalar_mul(
                CNEG, PC[:, 0:1], -1.0), waits=[(TT_, tv), (S_, cneg_s)], inc=V_)
            wslab_t[par] = TT_.n

            for hl in range(2):
                base = 64 * hl
                for qh in range(2):
                    kts = [kt for kt in range(NT)
                           if not (causal and 128*kt > 1024*qh + 1023)]
                    a_s = {0: pa_s, 1: pa_s}
                    pend_av = None
                    first = True

                    def emit_av(pend, last):
                        nonlocal vaug_t
                        kt, ap_, E, av_w, fst = pend
                        tv = b.op("tensor", lambda e, kt=kt, hl=hl, E=E,
                                  fst=fst, last=last: [
                            e.matmul(PB[0:65, 512*q:512*(q+1)],
                                     VAUG[:, 130*kt+65*hl:130*kt+65*hl+65],
                                     E[:, 512*q:512*(q+1)],
                                     start=fst, stop=last) for q in range(2)][-1],
                            waits=[av_w, (S_, v_ready), (V_, b_evac_v)], inc=TT_)
                        vaug_t = tv
                        e_t[ap_] = tv

                    for idx, kt in enumerate(kts):
                        ap_ = idx % 2
                        A = PA[:, 1024*ap_:1024*(ap_+1)]
                        E = EB[ap_]
                        tv = b.op("tensor",
                                  lambda e, kt=kt, qh=qh, base=base, A=A: [
                            e.matmul(A[:, 512*q:512*(q+1)],
                                     KT[base:base+64, 128*kt:128*(kt+1)],
                                     QT[base:base+64,
                                        1024*qh+512*q:1024*qh+512*(q+1)],
                                     start=True, stop=True) for q in range(2)][-1],
                            waits=[(S_, a_s[ap_]), (V_, qk_v)], inc=TT_)
                        asm_t = tv
                        sv = b.op("scalar", lambda e, A=A, E=E: e.activation(
                            E, A, AF.Exp, bias=CNEG, scale=0.125),
                            waits=[(TT_, tv), (V_, cneg_v), (TT_, e_t[ap_]),
                                   (TT_, st.get("rb_t", 0))], inc=S_)
                        a_s[ap_] = sv
                        cneg_s = sv
                        pa_s = sv
                        av_w = (S_, sv)
                        if causal:
                            s_ = 128*kt - 1024*qh
                            if 0 <= s_ <= 896:
                                mv = b.op("vector", lambda e, E=E, s_=s_: e.tensor_mul(
                                    E, E, ATAB[:, 896-s_:1920-s_]),
                                    waits=[(S_, sv), (WG, tab_g)], inc=V_)
                                av_w = (V_, mv)
                        if idx == 1 and st.get("pend_evac") is not None:
                            flush_evac()
                        if pend_av is not None:
                            emit_av(pend_av, False)
                        pend_av = (kt, ap_, E, av_w, first)
                        first = False
                    emit_av(pend_av, True)
                    if debug == "att1" and p == 0 and hl == 0 and qh == 0:
                        cv = b.op("vector", lambda e: e.tensor_copy(
                            AF32[0:65, 0:1024], PB[0:65, 0:1024]),
                            waits=[(TT_, vaug_t)], inc=V_)
                        last = b.op("sync", lambda e: e.dma_start(
                            xout[0:65, 0:1024], AF32[0:65, 0:1024]),
                            waits=[(V_, cv)], inc=WX, dma=True)
                        b.op("sync", lambda e: e.wait_ge(b.WX.h, last))
                        b.finalize()
                        return "OT_DUMP"
                    m1 = b.op("scalar", lambda e: e.activation(
                        ROWT[:, 0:1024], PB[64:65, 0:1024], AF.Ln),
                        waits=[(TT_, vaug_t)], inc=S_)
                    dm = b.op("scalar", lambda e: e.activation(
                        RROW[:, 0:1024], ROWT[:, 0:1024], AF.Exp, scale=-1.0),
                        waits=[(S_, m1), (TT_, st.get("rb_t", 0))], inc=S_)
                    st["pend_evac"] = (p, hl, qh, dm, a_s[0], a_s[1])
        flush_evac()
        st["af32_free_v"] = V_.n

        if debug in ("ot", "tagot", "tagb"):
            ot_all = ot_v
            last = b.op("gpsimd", lambda e: e.dma_start(
                xout.rearrange("(a p) d -> p a d", p=128),
                ABIG[:].rearrange("p (a d) -> p a d", a=16)),
                waits=[(V_, ot_all)] + [(TT_, TT_.n), (S_, S_.n)], inc=WG, dma=True)
            b.op("sync", lambda e: e.wait_ge(b.WG.h, last))
            b.finalize()
            return "OT_DUMP"
        wgo = b.op("gpsimd", lambda e: e.dma_start(
            AW[:, 0:8192].rearrange("p (j f) -> p j f", j=NC),
            dr[woname].rearrange("(j q) f -> q j f", q=128)),
            waits=[(TT_, asm_t), (TT_, wslab_t[0]), (TT_, wslab_t[1]),
                   (TT_, TT_.n)], inc=WG, dma=True)
        gv = 0
        if kind == "f":
            gv = b.op("gpsimd", lambda e: e.dma_start(GROWB[0:1, 0:1024], dr["f_outb"]),
                      waits=[(TT_, vb_t), (TT_, st.get("growb_t", 0))], inc=WG, dma=True)
        opev = {}
        for tt in range(NT):
            ph = tt % 2
            PAH = PA[:, 1024*ph:1024*(ph+1)]
            tv = b.op("tensor", lambda e, tt=tt, PAH=PAH, isf=(kind == "f"): [
                e.matmul(PAH[:, 512*q:512*(q+1)],
                         ABIG[:, 2048*j+128*tt:2048*j+128*(tt+1)],
                         AW[:, 1024*j+512*q:1024*j+512*(q+1)],
                         start=(j == 0), stop=(not isf) and (j == NC-1))
                for j in range(NC) for q in range(2)][-1],
                waits=[(WG, wgo), (V_, ot_v), (V_, opev.get(tt-2, 0)),
                       (S_, pa_s), (V_, rb_v), (V_, b_evac_v)], inc=TT_)
            if kind == "f":
                tv = b.op("tensor", lambda e, PAH=PAH: [
                    e.matmul(PAH[:, 512*q:512*(q+1)], ONESRB[:],
                             GROWB[0:1, 512*q:512*(q+1)], start=False, stop=True)
                    for q in range(2)][-1], waits=[(WG, gv)], inc=TT_)
                st["growb_t"] = tv
            if kind == "f":
                opev[tt] = b.op("vector", lambda e, tt=tt, PAH=PAH: e.tensor_add(
                    AZ[:, D*tt:D*(tt+1)], AZ[:, D*tt:D*(tt+1)], PAH),
                    waits=[(TT_, tv), (V_, zv0)], inc=V_)
            else:
                opev[tt] = b.op("vector", lambda e, tt=tt, PAH=PAH: e.tensor_copy(
                    AZ[:, D*tt:D*(tt+1)], PAH), waits=[(TT_, tv), (V_, zv0)], inc=V_)
            st["ztile_v"][tt] = opev[tt]
        st["zdone"] = V_.n
        set_bars()

    def emit_moe():
        xnt_ts = dict(st["xnt_tile_s"])
        zv0 = st["zdone"]
        gl_v = 0
        for tt in range(NT):
            tv = b.op("tensor", lambda e, tt=tt: [
                e.matmul(PC[:, 8*tt:8*(tt+1)],
                         AXNT[:, T*j+128*tt:T*j+128*(tt+1)], SGATE[:, 8*j:8*(j+1)],
                         start=(j == 0), stop=(j == NC-1)) for j in range(NC)][-1],
                waits=[(S_, xnt_ts[tt]), (WG, consts_g), (TT_, st["bar_t"]),
                       (V_, gl_v)], inc=TT_)
            gl_v = b.op("vector", lambda e, tt=tt: e.tensor_copy(
                SCOMB[:, 8*tt:8*(tt+1)], PC[:, 8*tt:8*(tt+1)]),
                waits=[(TT_, tv), (V_, st["bar_v"])], inc=V_)
        comb_v = 0
        for tt in range(NT):
            LG = SCOMB[:, 8*tt:8*(tt+1)]
            m = b.op("vector", lambda e, LG=LG: e.reduce_max(
                SMOE[:, 0:1], LG, axis=AX.X), waits=[(V_, gl_v), (S_, st["bar_s"])],
                inc=V_)
            m = b.op("vector", lambda e, LG=LG: e.tensor_scalar(
                SMOE[:, 8:16], LG, SMOE[:, 0:1], None, ALU.is_ge),
                waits=[(V_, m)], inc=V_)
            m = b.op("vector", lambda e, LG=LG: e.scalar_tensor_tensor(
                SMOE[:, 16:24], SMOE[:, 8:16], -1e30, LG, ALU.mult, ALU.add),
                waits=[(V_, m)], inc=V_)
            m = b.op("vector", lambda e: e.reduce_max(
                SMOE[:, 1:2], SMOE[:, 16:24], axis=AX.X), waits=[(V_, m)], inc=V_)
            m2 = b.op("vector", lambda e: e.tensor_scalar(
                SMOE[:, 24:32], SMOE[:, 16:24], SMOE[:, 1:2], None, ALU.is_ge),
                waits=[(V_, m)], inc=V_)
            vv = b.op("vector", lambda e: e.tensor_sub(
                SMOE[:, 1:2], SMOE[:, 1:2], SMOE[:, 0:1]), waits=[(V_, m2)], inc=V_)
            sv = b.op("scalar", lambda e: e.activation(
                SMOE[:, 2:3], SMOE[:, 1:2], AF.Exp), waits=[(V_, vv)], inc=S_)
            m = b.op("vector", lambda e: e.tensor_scalar_add(
                SMOE[:, 3:4], SMOE[:, 2:3], 1.0), waits=[(S_, sv)], inc=V_)
            m = b.op("vector", lambda e: e.reciprocal(SMOE[:, 4:5], SMOE[:, 3:4]),
                waits=[(V_, m)], inc=V_)
            m = b.op("vector", lambda e: e.tensor_mul(
                SMOE[:, 5:6], SMOE[:, 2:3], SMOE[:, 4:5]), waits=[(V_, m)], inc=V_)
            m = b.op("vector", lambda e: e.tensor_scalar_mul(
                SMOE[:, 24:32], SMOE[:, 24:32], SMOE[:, 5:6]), waits=[(V_, m)], inc=V_)
            comb_v = b.op("vector", lambda e, LG=LG: e.scalar_tensor_tensor(
                LG, SMOE[:, 8:16], SMOE[:, 4:5], SMOE[:, 24:32],
                ALU.mult, ALU.add), waits=[(V_, m)], inc=V_)
        w12_t = {0: st["bar_t"], 1: st["bar_t"]}
        w3_t = {0: st["bar_t"], 1: st["bar_t"]}
        h_v = 0
        h_t = st["bar_t"]
        sil_s = {0: 0, 1: 0}
        ab_s = {0: st["bar_s"], 1: st["bar_s"]}
        ab_v = 0
        y_t = {0: 0, 1: 0}
        y_v = {0: 0, 1: 0}
        bias_x = 0
        W1C = [AW[:, 0:2048].rearrange("p (j h) -> p j h", j=NC),
               AW[:, 2048:4096].rearrange("p (j h) -> p j h", j=NC)]
        W2C = [AW[:, 4096:6144].rearrange("p (j h) -> p j h", j=NC),
               AW[:, 6144:8192].rearrange("p (j h) -> p j h", j=NC)]
        W3Q = [ABIG[:, 8192:12288].rearrange("p (k d) -> p k d", k=4),
               ABIG[:, 12288:16384].rearrange("p (k d) -> p k d", k=4)]
        aw_t0 = st["bar_t"]
        for ee in range(NE):
            bx = b.op("sync", lambda e, ee=ee: e.dma_start(B1C, dr["f_b1T"][ee]),
                      waits=[(S_, ab_s[0]), (S_, ab_s[1]), (V_, ab_v)], inc=WX,
                      dma=True)
            bx = b.op("sync", lambda e, ee=ee: e.dma_start(B2C, dr["f_b2T"][ee]),
                      inc=WX, dma=True)
            gv = b.op("gpsimd", lambda e, ee=ee: e.dma_start(
                GROWB[0:1, 0:1024], dr["f_b3"][ee].rearrange("(a f) -> a f", a=1)),
                waits=[(TT_, st.get("growb_t", 0))], inc=WG, dma=True)
            for hq in range(4):
                wg3 = b.op("gpsimd", lambda e, ee=ee, hq=hq: e.dma_start(
                    W3Q[hq % 2], dr["f_w3T"][ee][512*hq:512*(hq+1), :]
                    .rearrange("(k q) d -> q k d", q=128)),
                    waits=[(TT_, w3_t[hq % 2]), (S_, st["bar_s"])], inc=WG, dma=True)
                for hc in range(2):
                    cpar = (2*hq + hc) % 2
                    wg1 = b.op("gpsimd", lambda e, ee=ee, hq=hq, hc=hc, cpar=cpar:
                               e.dma_start(
                        W1C[cpar], dr["f_w1T"][ee][:, 512*hq+256*hc:512*hq+256*(hc+1)]
                        .rearrange("(j q) h -> q j h", q=128)),
                        waits=[(TT_, w12_t[cpar]), (TT_, aw_t0)], inc=WG, dma=True)
                    wg2 = b.op("gpsimd", lambda e, ee=ee, hq=hq, hc=hc, cpar=cpar:
                               e.dma_start(
                        W2C[cpar], dr["f_w2T"][ee][:, 512*hq+256*hc:512*hq+256*(hc+1)]
                        .rearrange("(j q) h -> q j h", q=128)), inc=WG, dma=True)
                    for ht2 in range(2):
                        ht = 2*hc + ht2
                        bcol1 = B1C[:, 4*hq+ht:4*hq+ht+1]
                        bcol2 = B2C[:, 4*hq+ht:4*hq+ht+1]
                        for th in range(2):
                            ph = (2*ht + th) % 2
                            A = PA[:, 1024*ph:1024*(ph+1)]
                            SSI = ASM[:, 1024*ph:1024*(ph+1)]
                            ta = b.op("tensor", lambda e, ht2=ht2, th=th, A=A,
                                      cpar=cpar, wg1=wg1: [
                                e.matmul(A[:, 512*q:512*(q+1)],
                                         W1C[cpar][:, j, 128*ht2:128*(ht2+1)],
                                         AXNT[:, T*j+1024*th+512*q:T*j+1024*th+512*(q+1)],
                                         start=(j == 0), stop=(j == NC-1))
                                for q in range(2) for j in range(NC)][-1],
                                waits=[(WG, wg1), (S_, xnt_ts[8*th+7]),
                                       (S_, ab_s[ph]), (V_, y_v[ph])], inc=TT_)
                            tb = b.op("tensor", lambda e, ht2=ht2, th=th,
                                      cpar=cpar, wg2=wg2: [
                                e.matmul(PB[:, 512*q:512*(q+1)],
                                         W2C[cpar][:, j, 128*ht2:128*(ht2+1)],
                                         AXNT[:, T*j+1024*th+512*q:T*j+1024*th+512*(q+1)],
                                         start=(j == 0), stop=(j == NC-1))
                                for q in range(2) for j in range(NC)][-1],
                                waits=[(WG, wg2), (V_, ab_v), (V_, st["pb_v"])], inc=TT_)
                            w12_t[cpar] = tb
                            sv = b.op("scalar", lambda e, A=A, SSI=SSI, bcol1=bcol1:
                                      e.activation(SSI, A, AF.Silu, bias=bcol1),
                                      waits=[(TT_, ta), (WX, bx), (V_, h_v)], inc=S_)
                            ab_s[ph] = sv
                            h_v = b.op("vector", lambda e, ht=ht, th=th, SSI=SSI,
                                       bcol2=bcol2: e.scalar_tensor_tensor(
                                ABIG[:, 2048*ht+1024*th:2048*ht+1024*(th+1)],
                                PB[:], bcol2, SSI, ALU.add, ALU.mult),
                                waits=[(TT_, tb), (S_, sv), (WX, bx), (TT_, h_t)],
                                inc=V_)
                            ab_v = h_v
                for ts_ in range(NT):
                    ph = ts_ % 2
                    Y = PA[:, 0:1024] if ph == 0 else PB[:, 0:1024]
                    tv = b.op("tensor", lambda e, ts_=ts_, Y=Y, hq=hq: [
                        e.matmul(Y[:, 512*q:512*(q+1)],
                                 ABIG[:, 2048*k+128*ts_:2048*k+128*(ts_+1)],
                                 W3Q[hq % 2][:, k, 512*q:512*(q+1)],
                                 start=(k == 0), stop=(hq != 0 and k == 3))
                        for k in range(4) for q in range(2)][-1],
                        waits=[(WG, wg3), (V_, h_v), (V_, y_v[ph]), (S_, ab_s[ph]),
                               (V_, st["pb_v"]) if ph else (V_, 0),
                               (V_, ab_v) if ph else (V_, 0)], inc=TT_)
                    h_t = tv
                    w3_t[hq % 2] = tv
                    if hq == 0:
                        tv = b.op("tensor", lambda e, Y=Y: [
                            e.matmul(Y[:, 512*q:512*(q+1)], ONESRB[:],
                                     GROWB[0:1, 512*q:512*(q+1)],
                                     start=False, stop=True)
                            for q in range(2)][-1], waits=[(WG, gv)], inc=TT_)
                        st["growb_t"] = tv
                    y_t[ph] = tv
                    y_v[ph] = b.op("vector", lambda e, ts_=ts_, Y=Y, ee=ee:
                                   e.scalar_tensor_tensor(
                        AZ[:, D*ts_:D*(ts_+1)], Y, SCOMB[:, 8*ts_+ee:8*ts_+ee+1],
                        AZ[:, D*ts_:D*(ts_+1)], ALU.mult, ALU.add),
                        waits=[(TT_, tv), (V_, comb_v), (V_, zv0)], inc=V_)
                    if ee == NE - 1 and hq == 3:
                        st["ztile_v"][ts_] = y_v[ph]
                    if ph == 1:
                        ab_v = y_v[ph]
        st["zdone"] = V_.n
        set_bars()
        st["bar_g"] = WG.n

    def emit_l1ffn():
        xnt_ts = dict(st["xnt_tile_s"])
        zv0 = st["zdone"]
        W1S = [AW[:, 0:1024], AW[:, 1024:2048]]
        W2S = AW[:, 2048:10240]
        W2Sv = W2S.rearrange("p (fs d) -> p fs d", fs=8)
        gv = b.op("gpsimd", lambda e: e.dma_start(GROWB[0:1, 0:1024], dr["g_b2"]),
                  waits=[(TT_, st.get("growb_t", 0))], inc=WG, dma=True)
        w1_t = {0: st["bar_t"], 1: st["bar_t"]}
        w2_t = st["bar_t"]
        h_v = 0
        h_t = st["bar_t"]
        hv_ph = {0: st["bar_v"], 1: st["bar_v"]}
        y_v = {0: st["bar_v"], 1: st["bar_v"]}
        for fq in range(4):
            w2q = list(range(8))

            def emit_w2(n, fq=fq):
                nonlocal wg2
                for _ in range(n):
                    if not w2q:
                        return
                    fs = w2q.pop(0)
                    gfs = 8*fq + fs
                    wg2 = b.op("gpsimd", lambda e, gfs=gfs, fs=fs: e.dma_start(
                        W2Sv[:, fs, :], dr["g_w2"][128*gfs:128*(gfs+1), :]),
                        waits=[(TT_, w2_t)], inc=WG, dma=True)
            wg2 = None
            for fs in range(8):
                gfs = 8*fq + fs
                par = fs % 2
                wg1 = b.op("gpsimd", lambda e, gfs=gfs, par=par: e.dma_start(
                    W1S[par].rearrange("p (j f) -> p j f", j=NC),
                    dr["g_w1"][gfs].rearrange("(j q) f -> q j f", q=128)),
                    waits=[(TT_, w1_t[par])], inc=WG, dma=True)
                if fs >= 2:
                    emit_w2(3)
                for th in range(2):
                    ph = th
                    A = PA[:, 1024*ph:1024*(ph+1)]
                    tv = b.op("tensor", lambda e, par=par, th=th, A=A: [
                        e.matmul(A[:, 512*q:512*(q+1)],
                                 W1S[par].rearrange("p (j f) -> p j f", j=NC)[:, j, :],
                                 AXNT[:, T*j+1024*th+512*q:T*j+1024*th+512*(q+1)],
                                 start=(j == 0), stop=(j == NC-1))
                        for q in range(2) for j in range(NC)][-1],
                        waits=[(WG, wg1), (S_, xnt_ts[8*th+7]), (V_, hv_ph[ph]),
                               (V_, y_v[ph])], inc=TT_)
                    w1_t[par] = tv
                    h_v = b.op("vector", lambda e, fs=fs, th=th, A=A, gfs=gfs:
                               e.tensor_scalar(
                        ABIG[:, 2048*fs+1024*th:2048*fs+1024*(th+1)], A,
                        B1L[:, gfs:gfs+1], 0.0, ALU.add, ALU.max),
                        waits=[(TT_, tv), (WX, consts_x), (TT_, h_t)], inc=V_)
                    hv_ph[ph] = h_v
            for ts_ in range(NT):
                ph = ts_ % 2
                Y = PA[:, 0:1024] if ph == 0 else PB[:, 0:1024]
                tv = b.op("tensor", lambda e, ts_=ts_, Y=Y, fq=fq: [
                    e.matmul(Y[:, 512*q:512*(q+1)],
                             ABIG[:, 2048*k+128*ts_:2048*k+128*(ts_+1)],
                             W2Sv[:, k, 512*q:512*(q+1)],
                             start=(k == 0), stop=(fq != 0 and k == 7))
                    for k in range(8) for q in range(2)][-1],
                    waits=[(WG, wg2), (V_, h_v), (V_, y_v[ph]), (V_, hv_ph[ph]),
                           (V_, st["pb_v"]) if ph else (V_, 0)], inc=TT_)
                h_t = tv
                w2_t = tv
                if fq == 0:
                    tv = b.op("tensor", lambda e, Y=Y: [
                        e.matmul(Y[:, 512*q:512*(q+1)], ONESRB[:],
                                 GROWB[0:1, 512*q:512*(q+1)], start=False,
                                 stop=True) for q in range(2)][-1],
                        waits=[(WG, gv)], inc=TT_)
                    st["growb_t"] = tv
                y_v[ph] = b.op("vector", lambda e, ts_=ts_, Y=Y: e.tensor_add(
                    AZ[:, D*ts_:D*(ts_+1)], AZ[:, D*ts_:D*(ts_+1)], Y),
                    waits=[(TT_, tv), (V_, zv0)], inc=V_)
                if fq == 3:
                    st["ztile_v"][ts_] = y_v[ph]
        st["zdone"] = V_.n
        set_bars()

    def emit_l1_final_ln(nrm_v):
        zt = st["ztile_v"]
        sqp = 0
        for i in range(NT):
            b.op("vector", lambda e, i=i: e.reduce_sum(
                SROW[:, i:i+1], AZ[:, D*i:D*(i+1)], axis=AX.X),
                waits=[(V_, zt[i])], inc=V_)
            sqp = b.op("scalar", lambda e, i=i: e.activation(
                SQD[:], AZ[:, D*i:D*(i+1)], AF.Square,
                accum_out=SROW[:, 16+i:17+i]),
                waits=[(V_, zt[i]), (S_, sqp)], inc=S_)
        v_st, s_st = V_.n, S_.n
        c1 = b.op("vector", lambda e: e.tensor_scalar_mul(
            SROW[:, 32:48], SROW[:, 0:16], 1.0 / D),
            waits=[(V_, v_st), (S_, s_st)], inc=V_)
        c2 = b.op("vector", lambda e: e.tensor_mul(
            SROW[:, 48:64], SROW[:, 32:48], SROW[:, 32:48]),
            waits=[(V_, c1)], inc=V_)
        vv = b.op("vector", lambda e: e.scalar_tensor_tensor(
            SROW[:, 48:64], SROW[:, 16:32], 1.0 / D, SROW[:, 48:64],
            ALU.mult, ALU.subtract), waits=[(V_, c2)], inc=V_)
        m1 = b.op("scalar", lambda e: e.activation(SROW[:, 96:112], SROW[:, 48:64],
             AF.Ln, bias=EPS5), waits=[(V_, vv), (V_, consts_v)], inc=S_)
        sv = b.op("scalar", lambda e: e.activation(SROW[:, 64:80], SROW[:, 96:112],
                  AF.Exp, scale=-0.5), waits=[(S_, m1)], inc=S_)
        nm = b.op("vector", lambda e: e.tensor_scalar_mul(
            SROW[:, 80:96], SROW[:, 32:48], -1.0), waits=[(V_, c1)], inc=V_)
        for i in range(NT):
            par = i % 2
            yt = AF32[:, 1024*par:1024*(par+1)]
            t1 = b.op("vector", lambda e, i=i, yt=yt: e.tensor_scalar(
                yt, AZ[:, D*i:D*(i+1)], SROW[:, 80+i:81+i], SROW[:, 64+i:65+i],
                ALU.add, ALU.mult), waits=[(S_, sv), (V_, nrm_v), (V_, nm)], inc=V_)
            t2 = b.op("vector", lambda e, i=i, yt=yt: e.tensor_mul(
                AZ[:, D*i:D*(i+1)], yt, NRMB[:]), waits=[(V_, t1)], inc=V_)
            t3 = b.op("vector", lambda e, i=i: e.tensor_add(
                AZ[:, D*i:D*(i+1)], AZ[:, D*i:D*(i+1)], NRMB2[:]),
                waits=[(V_, t2)], inc=V_)
            st["ztile_v"][i] = t3
        st["zdone"] = V_.n
        st["af32_free_v"] = V_.n
        set_bars()

    def dump_z():
        wv = st["zdone"]
        last = b.op("sync", lambda e: e.dma_start(
            xout.rearrange("(i p) d -> p i d", p=128),
            AZ[:].rearrange("p (i d) -> p i d", i=NT)),
            waits=[(V_, wv)] + bars(), inc=WX, dma=True)
        b.op("sync", lambda e: e.wait_ge(b.WX.h, last))
        b.finalize()
        return nc

    for si, s in enumerate(subs):
        kind = SUB_KIND[s]
        xin_ap, xo_ap = xs_seq[si], xs_seq[si + 1]
        if not st.get("z_is_ln", False):
            emit_ln(xin_ap)
        st["z_is_ln"] = False
        if debug == "stats":
            last = b.op("sync", lambda e: e.dma_start(
                xout[0:128, 0:128], SROW[:]),
                waits=[(V_, st["zdone"])] + bars(), inc=WX, dma=True)
            b.op("sync", lambda e: e.wait_ge(b.WX.h, last))
            b.finalize()
            return nc
        if debug == "ln":
            return dump_z()
        def dump_xnt():
            wv = st["xnt_s"]
            last = b.op("gpsimd", lambda e: e.dma_start(
                xout.rearrange("(a p) d -> p a d", p=128),
                AXNT[:].rearrange("p (a d) -> p a d", a=16)),
                waits=[(V_, wv)] + bars(), inc=WG, dma=True)
            b.op("sync", lambda e: e.wait_ge(b.WG.h, last))
            b.finalize()
            return nc

        if kind == "f":
            nv = row_bcast(NRMB, dr["f_n1w"])
            emit_to_xnt("rms_w", nv)
            if debug == "xnt":
                return dump_xnt()
            nv2 = row_bcast(NRMB2, dr["f_n2w"])
            if emit_attention("f") == "OT_DUMP":
                return nc
            emit_to_xnt("rms_w", nv2, nrmb=NRMB2)
            emit_moe()
        elif kind == "a":
            nv = row_bcast(NRMB, dr["a_gamma"], add=1.0, post_scale=32.0)
            emit_to_xnt("rms0", nv)
            if debug == "xnt":
                return dump_xnt()
            if emit_attention("a") == "OT_DUMP":
                return nc
        elif kind == "t":
            emit_to_xnt("none")
            if emit_attention("t") == "OT_DUMP":
                return nc
        else:
            emit_to_xnt("none")
            nv = row_bcast(NRMB, dr["g_gam"], scale=0.5)
            nv = row_bcast(NRMB2, dr["g_bet"], scale=0.5)
            emit_l1ffn()
            if debug == "body":
                return dump_z()
            emit_l1_final_ln(nv)
        if debug == "sub":
            return dump_z()
        fuse = si < len(subs) - 1
        emit_finale(xin_ap, xo_ap, SUB_ALPHA[s], fuse_ln=fuse)
        st["z_is_ln"] = fuse
    b.op("sync", lambda e: [e.wait_ge(b.WS0.h, b.WS0.n),
                            e.wait_ge(b.WS1.h, b.WS1.n)][-1] and None)
    b.finalize()
    return nc



_CACHE = {}

def _get_prog(key, subs):
    if key not in _CACHE:
        _CACHE[key] = build_program(subs)
    return _CACHE[key]


def kernel(**inputs):
    hp = host_prep(inputs)
    x = np.ascontiguousarray(np.asarray(inputs["L0_attn_x"], dtype=np.float32))
    xs = [np.ascontiguousarray(x[i]) for i in range(8)]
    for g in GROUPS:
        nc = _get_prog(tuple(g), list(g))
        names = [t for t in ("f", "a", "t", "g")
                 if t in {SUB_KIND[s] for s in g}]
        keys = ["c_ident", "c_ones_col", "c_ones_row"]
        if "f" in names:
            keys += ["f_wqkv", "f_bqkv", "f_bqkrot", "f_wout", "f_outb", "f_n1w",
                     "f_n2w", "f_gateT", "f_w1T", "f_w2T", "f_w3T", "f_b1T",
                     "f_b2T", "f_b3", "ropeC", "ropeS"]
        if "a" in names:
            keys += ["a_w", "a_wout", "a_gamma"]
        if "t" in names:
            keys += ["t_w", "t_wr", "stair"]
        if "g" in names:
            keys += ["g_w1", "g_b1T", "g_w2", "g_b2", "g_gam", "g_bet"]
        base = {k: hp[k] for k in keys}
        in_maps = [dict(base, xin=xs[c]) for c in range(8)]
        res = run_bass_kernel_spmd(nc, in_maps, core_ids=list(range(8)))
        xs = [np.asarray(res.results[c]["xout"]) for c in range(8)]
    return np.stack(xs).astype(np.float32)


# version 19 (final)
# speedup vs baseline: 1.0147x; 1.0147x over previous
import numpy as np
import concourse.bass as bass
import concourse.mybir as mybir
from concourse.bass_utils import run_bass_kernel_spmd

F32 = mybir.dt.float32
BF16 = mybir.dt.bfloat16
AF = mybir.ActivationFunctionType
ALU = mybir.AluOpType
AX = mybir.AxisListType

T = 2048
D = 1024
NT = 16
NC = 8
HD = 64
NE = 8
H0 = 2048
F1 = 4096
ENGINES = ("sync", "gpsimd", "tensor", "vector", "scalar")

GROUPS = [[0, 1, 2, 3, 4, 5]]

SUB_KIND = {0: "f", 1: "a", 2: "f", 3: "g", 4: "t", 5: "g"}
SUB_ALPHA = {0: 0.5, 1: 1.0, 2: 0.5, 3: 1.0, 4: 1.0, 5: 1.0}



def host_prep(inp):
    c = lambda a: np.ascontiguousarray(np.asarray(a), dtype=np.float32)
    out = {}
    _ = inp["L1_ffn_router_w"], inp["L1_ffn_router_b"]

    def per_ptile(wq, wk, wv):
        return np.stack([np.concatenate(
            [wq[:, 128*p:128*p+128], wk[:, 128*p:128*p+128], wv[:, 128*p:128*p+128]],
            axis=1) for p in range(8)])

    Wm = c(inp["L0_ffn_qkv_w"]).T
    out["f_wqkv"] = c(per_ptile(Wm[:, 0:D], Wm[:, D:2*D], Wm[:, 2*D:3*D]))
    bq = c(inp["L0_ffn_qkv_b"])
    bb = np.stack([np.stack([bq[128*p:128*p+128], bq[D+128*p:D+128*p+128],
                             bq[2*D+128*p:2*D+128*p+128]]) for p in range(8)])
    out["f_bqkv"] = c(bb)
    rot = lambda v: np.concatenate([v[32:64], v[0:32], v[96:128], v[64:96]])
    out["f_bqkrot"] = c(np.stack([np.stack([rot(bb[p, 0]), rot(bb[p, 1])])
                                  for p in range(8)]))
    out["f_wout"] = c(c(inp["L0_ffn_out_w"]).T)
    out["f_outb"] = c(inp["L0_ffn_out_b"]).reshape(1, D)
    out["f_n1w"] = c(inp["L0_ffn_norm1_w"]).reshape(1, D)
    out["f_n2w"] = c(inp["L0_ffn_norm2_w"]).reshape(1, D)
    out["f_gateT"] = c(c(inp["L0_ffn_gate_w"]).T)
    out["f_w1T"] = c(np.transpose(c(inp["L0_ffn_w1"]), (0, 2, 1)))
    out["f_w2T"] = c(np.transpose(c(inp["L0_ffn_w2"]), (0, 2, 1)))
    out["f_w3T"] = c(np.transpose(c(inp["L0_ffn_w3"]), (0, 2, 1)))
    out["f_b1T"] = c(np.transpose(c(inp["L0_ffn_b1"]).reshape(8, 16, 128), (0, 2, 1)))
    out["f_b2T"] = c(np.transpose(c(inp["L0_ffn_b2"]).reshape(8, 16, 128), (0, 2, 1)))
    out["f_b3"] = c(inp["L0_ffn_b3"])

    W = c(inp["L0_attn_w_qkv"])
    out["a_w"] = c(per_ptile(W[:, 0:D], W[:, D:2*D], W[:, 2*D:3*D]))
    out["a_wout"] = c(inp["L0_attn_w_out"])
    out["a_gamma"] = c(inp["L0_attn_gamma"]).reshape(1, D)

    out["t_w"] = c(per_ptile(c(inp["L1_attn_Wq"]), c(inp["L1_attn_Wk"]),
                             c(inp["L1_attn_Wv"])))
    out["t_wr"] = c(inp["L1_attn_Wr"])

    w1 = c(inp["L1_ffn_w1"][0])
    out["g_w1"] = c(np.transpose(w1.reshape(D, 32, 128), (1, 0, 2)))
    out["g_b1T"] = c(c(inp["L1_ffn_b1"][0]).reshape(32, 128).T)
    out["g_w2"] = c(inp["L1_ffn_w2"][0])
    out["g_b2"] = c(inp["L1_ffn_b2"][0]).reshape(1, D)
    out["g_gam"] = c(inp["L1_ffn_gamma"]).reshape(1, D)
    out["g_bet"] = c(inp["L1_ffn_beta"]).reshape(1, D)

    out["c_ident"] = np.eye(128, dtype=np.float32)
    out["c_ones_col"] = np.ones((128, 1), dtype=np.float32)
    out["c_ones_row"] = np.ones((1, 128), dtype=np.float32)
    invf = 1.0 / 10000.0 ** (np.arange(0, HD, 2, dtype=np.float64) / HD)
    fr = np.arange(T, dtype=np.float64)[None, :] * invf[:, None]
    c64 = np.concatenate([np.cos(fr), np.cos(fr)], axis=0)
    s64 = np.concatenate([np.sin(fr), np.sin(fr)], axis=0)
    ssgn = np.concatenate([-s64[0:32], s64[32:64]], axis=0)
    out["ropeC"] = c(np.concatenate([c64, c64], axis=0))
    out["ropeS"] = c(np.concatenate([ssgn, ssgn], axis=0))
    aa = np.arange(128)[:, None]
    cc = np.arange(1920)[None, :]
    out["stair"] = c(np.where(aa + 896 <= cc, 1.0, np.exp(-50.0)))
    return out



class Sem:
    def __init__(self, nc, name):
        self.h = nc.alloc_semaphore(name)
        self.n = 0

    def inc(self, k=1):
        self.n += k
        return self.n


class Bld:
    def __init__(self, nc):
        self.nc = nc
        self.ops = {e: [] for e in ENGINES}
        self.WX = Sem(nc, "WX")
        self.WG = Sem(nc, "WG")
        self.WSC = Sem(nc, "WSC")
        self.WS0 = Sem(nc, "WS0")
        self.WS1 = Sem(nc, "WS1")
        self.T = Sem(nc, "T")
        self.V = Sem(nc, "V")
        self.S = Sem(nc, "S")

    def op(self, eng, fn, waits=(), inc=None, dma=False):
        amt = 16 if dma else 1
        dma_sems = (self.WX, self.WG, self.WSC, self.WS0, self.WS1)
        waits = list(waits)
        wd = {}
        for s, v in waits:
            if v and v > 0:
                if s in dma_sems:
                    v = s.n
                wd[s.h] = max(wd.get(s.h, 0), v)
        if dma and inc in dma_sems and inc.n > 0:
            wd[inc.h] = max(wd.get(inc.h, 0), inc.n)
        tgt = inc.inc(amt) if inc is not None else None
        ih = inc.h if inc is not None else None

        def emitter(e, _wd=wd, _fn=fn, _ih=ih, _amt=amt):
            for h, v in _wd.items():
                e.wait_ge(h, v)
            ins = _fn(e)
            if _ih is not None:
                if ins is None:
                    ins = e.nop()
                ins.then_inc(_ih, _amt)
        self.ops[eng].append(emitter)
        return tgt

    def finalize(self):
        with self.nc.Block() as blk:
            for eng in ENGINES:
                def run(e, _l=self.ops[eng]):
                    for f in _l:
                        f(e)
                getattr(blk, eng)(run)



def build_program(subs, debug=None):
    nc = bass.Bass("TRN2", target_bir_lowering=False)
    b = Bld(nc)
    WX, WG, WSC, TT_, V_, S_ = b.WX, b.WG, b.WSC, b.T, b.V, b.S

    need = {SUB_KIND[s] for s in subs}
    dr = {}

    def dram(name, shape):
        dr[name] = nc.dram_tensor(name, list(shape), F32, kind="ExternalInput").ap()

    dram("xin", (T, D))
    xout = nc.dram_tensor("xout", [T, D], F32, kind="ExternalOutput").ap()
    xmid = [nc.dram_tensor(f"xmid{i}", [T, D], F32, kind="Internal").ap()
            for i in range(len(subs) - 1)]
    xs_seq = [dr["xin"]] + xmid + [xout]

    dram("c_ident", (128, 128))
    dram("c_ones_col", (128, 1))
    dram("c_ones_row", (1, 128))
    if "f" in need:
        for n, s in [("f_wqkv", (8, D, 384)), ("f_bqkv", (8, 3, 128)),
                     ("f_bqkrot", (8, 2, 128)), ("f_wout", (D, D)), ("f_outb", (1, D)),
                     ("f_n1w", (1, D)), ("f_n2w", (1, D)), ("f_gateT", (D, 8)),
                     ("f_w1T", (8, D, H0)), ("f_w2T", (8, D, H0)), ("f_w3T", (8, H0, D)),
                     ("f_b1T", (8, 128, 16)), ("f_b2T", (8, 128, 16)), ("f_b3", (8, D)),
                     ("ropeC", (128, T)), ("ropeS", (128, T))]:
            dram(n, s)
    if "a" in need:
        for n, s in [("a_w", (8, D, 384)), ("a_wout", (D, D)), ("a_gamma", (1, D))]:
            dram(n, s)
    if "t" in need:
        for n, s in [("t_w", (8, D, 384)), ("t_wr", (D, D)), ("stair", (128, 1920))]:
            dram(n, s)
    if "g" in need:
        for n, s in [("g_w1", (32, D, 128)), ("g_b1T", (128, 32)), ("g_w2", (F1, D)),
                     ("g_b2", (1, D)), ("g_gam", (1, D)), ("g_bet", (1, D))]:
            dram(n, s)

    sb = nc.alloc_sbuf_tensor
    AZ = sb("AZ", [128, NT * D], F32)
    AXNT = sb("AXNT", [128, NC * T], BF16)
    ABIG = sb("ABIG", [128, 16384], BF16)
    AW = sb("AW", [128, 10368], BF16)
    ATAB = sb("ATAB", [128, 4096], BF16) if ("f" in need or "t" in need) else None
    AF32 = sb("AF32", [128, 4096], F32)
    ASM = sb("ASM", [128, 4096], BF16)
    SROW = sb("SROW", [128, 128], F32)
    SCOMB = sb("SCOMB", [128, 128], F32)
    SMOE = sb("SMOE", [128, 48], F32)
    SBIAS = sb("SBIAS", [128, 64], F32)
    SGATE = sb("SGATE", [128, 64], BF16) if "f" in need else None
    B1L = sb("B1L", [128, 32], F32) if "g" in need else None
    IDENT = sb("IDENT", [128, 128], BF16)
    ONESC = sb("ONESC", [128, 1], BF16)
    ONESR = sb("ONESR", [1, 128], F32)
    ONESRB = sb("ONESRB", [1, 128], BF16)
    ROWT = sb("ROWT", [1, 1024], F32)
    RROW = sb("RROW", [1, 1024], BF16)
    GROW = sb("GROW", [1, 1024], F32)
    GROWB = sb("GROWB", [1, 1024], BF16)
    NRMB = sb("NRMB", [128, 1024], F32)
    NRMB2 = sb("NRMB2", [128, 1024], F32)
    SQD = sb("SQD", [128, 1024], BF16)

    PA = nc.alloc_psum_tensor("PA", [128, 2048], F32)
    PB = nc.alloc_psum_tensor("PB", [128, 1024], F32)
    PC = nc.alloc_psum_tensor("PC", [128, 512], F32)
    PT = nc.alloc_psum_tensor("PT", [128, 1024], BF16)

    EPS5 = SBIAS[:, 0:1]
    EPS6 = SBIAS[:, 1:2]
    EPS24 = SBIAS[:, 2:3]
    CNEG = SBIAS[:, 3:4]
    BQ, BK = SBIAS[:, 4:5], SBIAS[:, 5:6]
    BQR, BKR = SBIAS[:, 6:7], SBIAS[:, 7:8]
    B1C = SBIAS[:, 16:32]
    B2C = SBIAS[:, 32:48]
    XNTI = [ASM[:, 2048:3072], ASM[:, 3072:4096]]
    QT = ASM[:, 0:2048]
    KT = ASM[:, 2048:4096]

    def col128(ap1d):
        return ap1d.rearrange("(q a) -> q a", a=1)

    st = {"bar_t": 0, "bar_v": 0, "bar_s": 0, "bar_g": 0,
          "zdone": 0, "pb_v": 0, "grow_t": 0,
          "ztile_v": {i: 0 for i in range(NT)},
          "xnt_tile_s": {i: 0 for i in range(NT)},
          "xnt_last_s": 0,
          "af32_free_v": 0}

    def bars():
        return [(TT_, st["bar_t"]), (V_, st["bar_v"]), (S_, st["bar_s"])]

    def set_bars():
        st["bar_t"], st["bar_v"], st["bar_s"] = TT_.n, V_.n, S_.n

    b.op("gpsimd", lambda e: e.dma_start(IDENT[:], dr["c_ident"]), inc=WG, dma=True)
    b.op("gpsimd", lambda e: e.dma_start(ONESC[:], dr["c_ones_col"]), inc=WG, dma=True)
    b.op("sync", lambda e: e.dma_start(ONESR[:], dr["c_ones_row"]), inc=WX, dma=True)
    b.op("gpsimd", lambda e: e.dma_start(ONESRB[:], dr["c_ones_row"]), inc=WG, dma=True)
    if "f" in need:
        b.op("gpsimd", lambda e: e.dma_start(
            SGATE[:].rearrange("p (j g) -> p j g", j=NC),
            dr["f_gateT"].rearrange("(j q) g -> q j g", q=128)), inc=WG, dma=True)
    if "g" in need:
        b.op("sync", lambda e: e.dma_start(B1L[:], dr["g_b1T"]), inc=WX, dma=True)
    consts_g = WG.n
    consts_x = WX.n
    cm = b.op("vector", lambda e: e.memset(SBIAS[:, 0:16], 0.0), inc=V_)
    cm = b.op("vector", lambda e: e.memset(EPS5, 1e-5), waits=[(V_, cm)], inc=V_)
    cm = b.op("vector", lambda e: e.memset(EPS6, 1e-6), waits=[(V_, cm)], inc=V_)
    cm = b.op("vector", lambda e: e.memset(EPS24, 1e-24), waits=[(V_, cm)], inc=V_)
    cm = b.op("vector", lambda e: e.memset(SBIAS[:, 8:9], float(np.log(0.1275))),
              waits=[(V_, cm)], inc=V_)
    consts_v = V_.n
    st["bar_v"] = consts_v


    def row_bcast(dst, src, width=D, scale=None, add=None, post_scale=None):
        wx = b.op("sync", lambda e: e.dma_start(GROW[:, 0:width], src),
                  waits=[(TT_, st["grow_t"])] + bars(), inc=WX, dma=True)
        vv = 0
        if scale is not None:
            vv = b.op("vector", lambda e: e.tensor_scalar_mul(
                GROW[:, 0:width], GROW[:, 0:width], scale), waits=[(WX, wx)], inc=V_)
        if add is not None:
            vv = b.op("vector", lambda e: e.tensor_scalar_add(
                GROW[:, 0:width], GROW[:, 0:width], add), waits=[(WX, wx)], inc=V_)
        if post_scale is not None:
            vv = b.op("vector", lambda e: e.tensor_scalar_mul(
                GROW[:, 0:width], GROW[:, 0:width], post_scale),
                waits=[(WX, wx), (V_, vv)], inc=V_)
        tv = b.op("tensor", lambda e: [
            e.matmul(PB[:, 512*q:512*(q+1)], ONESR[:], GROW[0:1, 512*q:512*(q+1)],
                     start=True, stop=True) for q in range(width // 512)][-1],
            waits=[(WX, max(wx, consts_x)), (V_, vv), (V_, st["pb_v"]),
                   (TT_, TT_.n)], inc=TT_)
        st["grow_t"] = tv
        vv = b.op("vector", lambda e: e.tensor_copy(dst[:, 0:width], PB[:, 0:width]),
                  waits=[(TT_, tv), (V_, st["bar_v"])], inc=V_)
        st["pb_v"] = vv
        return vv

    def emit_ln(xin_ap):
        wx = b.op("sync", lambda e: e.dma_start(
            AZ[:].rearrange("p (i d) -> p i d", i=NT),
            xin_ap.rearrange("(i p) d -> p i d", p=128)),
            waits=bars() + [(b.WS0, b.WS0.n), (b.WS1, b.WS1.n)], inc=WX, dma=True)
        sqp = 0
        for i in range(NT):
            b.op("vector", lambda e, i=i: e.reduce_sum(
                SROW[:, i:i+1], AZ[:, D*i:D*(i+1)], axis=AX.X),
                waits=[(WX, wx)] + bars(), inc=V_)
            sqp = b.op("scalar", lambda e, i=i: e.activation(
                SQD[:], AZ[:, D*i:D*(i+1)], AF.Square,
                accum_out=SROW[:, 16+i:17+i]),
                waits=[(WX, wx), (S_, sqp)] + bars(), inc=S_)
        v_st, s_st = V_.n, S_.n
        c1 = b.op("vector", lambda e: e.tensor_scalar_mul(
            SROW[:, 32:48], SROW[:, 0:16], 1.0 / D),
            waits=[(V_, v_st), (S_, s_st)], inc=V_)
        c2 = b.op("vector", lambda e: e.tensor_mul(
            SROW[:, 48:64], SROW[:, 32:48], SROW[:, 32:48]),
            waits=[(V_, c1)], inc=V_)
        vv = b.op("vector", lambda e: e.scalar_tensor_tensor(
            SROW[:, 48:64], SROW[:, 16:32], 1.0 / D, SROW[:, 48:64],
            ALU.mult, ALU.subtract), waits=[(V_, c2)], inc=V_)
        m1 = b.op("scalar", lambda e: e.activation(SROW[:, 96:112], SROW[:, 48:64],
             AF.Ln, bias=EPS5), waits=[(V_, vv), (V_, consts_v)], inc=S_)
        sv = b.op("scalar", lambda e: e.activation(SROW[:, 64:80], SROW[:, 96:112],
                  AF.Exp, scale=-0.5), waits=[(S_, m1)], inc=S_)
        nm = b.op("vector", lambda e: e.tensor_scalar_mul(
            SROW[:, 80:96], SROW[:, 32:48], -1.0), waits=[(V_, c1)], inc=V_)
        for i in range(NT):
            tsm = b.op("vector", lambda e, i=i: e.tensor_scalar(
                AZ[:, D*i:D*(i+1)], AZ[:, D*i:D*(i+1)],
                SROW[:, 80+i:81+i], SROW[:, 64+i:65+i],
                ALU.add, ALU.mult), waits=[(S_, sv), (V_, nm)], inc=V_)
            st["ztile_v"][i] = tsm
        st["zdone"] = V_.n
        set_bars()

    def emit_finale(xin_ap, xo_ap, alpha, fuse_ln=False):
        fin_v = {}
        ws_after = {}
        WS = [b.WS0, b.WS1]
        sqp = 0
        afv = st["af32_free_v"]
        for i in range(NT):
            par = i % 2
            xld = AF32[:, 1024*par:1024*(par+1)]
            w_extra = [(b.WS0, b.WS0.n), (b.WS1, b.WS1.n)] if i < 2 else []
            wx = b.op("sync", lambda e, i=i, xld=xld: e.dma_start(
                xld, xin_ap[128*i:128*(i+1), :]),
                waits=[(V_, afv), (V_, fin_v.get(i-2, 0))] + w_extra,
                inc=WX, dma=True)
            vv = b.op("vector", lambda e, i=i, xld=xld: e.scalar_tensor_tensor(
                AZ[:, D*i:D*(i+1)], AZ[:, D*i:D*(i+1)], alpha, xld,
                ALU.mult, ALU.add),
                waits=[(WX, wx), (WS[par], WS[par].n),
                       (V_, st["ztile_v"][i])], inc=V_)
            fin_v[i] = vv
            ws_after[i] = b.op("sync", lambda e, i=i: e.dma_start(
                xo_ap[128*i:128*(i+1), :], AZ[:, D*i:D*(i+1)]), waits=[(V_, vv)],
                inc=WS[par], dma=True)
            if fuse_ln:
                b.op("vector", lambda e, i=i: e.reduce_sum(
                    SROW[:, i:i+1], AZ[:, D*i:D*(i+1)], axis=AX.X),
                    waits=[(V_, vv)], inc=V_)
                sqp = b.op("scalar", lambda e, i=i: e.activation(
                    SQD[:], AZ[:, D*i:D*(i+1)], AF.Square,
                    accum_out=SROW[:, 16+i:17+i]),
                    waits=[(V_, vv), (S_, sqp)], inc=S_)
        st["bar_v"] = V_.n
        if not fuse_ln:
            st["zdone"] = 0
            return 0
        v_st, s_st = V_.n, S_.n
        c1 = b.op("vector", lambda e: e.tensor_scalar_mul(
            SROW[:, 32:48], SROW[:, 0:16], 1.0 / D),
            waits=[(V_, v_st), (S_, s_st)], inc=V_)
        c2 = b.op("vector", lambda e: e.tensor_mul(
            SROW[:, 48:64], SROW[:, 32:48], SROW[:, 32:48]),
            waits=[(V_, c1)], inc=V_)
        vv = b.op("vector", lambda e: e.scalar_tensor_tensor(
            SROW[:, 48:64], SROW[:, 16:32], 1.0 / D, SROW[:, 48:64],
            ALU.mult, ALU.subtract), waits=[(V_, c2)], inc=V_)
        m1 = b.op("scalar", lambda e: e.activation(SROW[:, 96:112], SROW[:, 48:64],
             AF.Ln, bias=EPS5), waits=[(V_, vv), (V_, consts_v)], inc=S_)
        sv = b.op("scalar", lambda e: e.activation(SROW[:, 64:80], SROW[:, 96:112],
                  AF.Exp, scale=-0.5), waits=[(S_, m1)], inc=S_)
        nm = b.op("vector", lambda e: e.tensor_scalar_mul(
            SROW[:, 80:96], SROW[:, 32:48], -1.0), waits=[(V_, c1)], inc=V_)
        for i in range(NT):
            par = i % 2
            tsm = b.op("vector", lambda e, i=i: e.tensor_scalar(
                AZ[:, D*i:D*(i+1)], AZ[:, D*i:D*(i+1)],
                SROW[:, 80+i:81+i], SROW[:, 64+i:65+i],
                ALU.add, ALU.mult),
                waits=[(S_, sv), (V_, nm), (WS[par], ws_after[i])], inc=V_)
            st["ztile_v"][i] = tsm
        st["zdone"] = V_.n
        set_bars()
        return 0

    def emit_to_xnt(mode, nrm_v=0, nrmb=None):
        if nrmb is None:
            nrmb = NRMB
        zt = st["ztile_v"]
        sv = 0
        if mode != "none":
            sqp = 0
            for i in range(NT):
                sqp = b.op("scalar", lambda e, i=i: e.activation(
                    SQD[:], AZ[:, D*i:D*(i+1)], AF.Square,
                    accum_out=SROW[:, i:i+1]),
                    waits=[(V_, zt[i]), (S_, sqp)], inc=S_)
            s_st = S_.n
            if mode == "rms_w":
                vv = b.op("vector", lambda e: e.tensor_scalar_mul(
                    SROW[:, 16:32], SROW[:, 0:16], 1.0 / D), waits=[(S_, s_st)], inc=V_)
                m1 = b.op("scalar", lambda e: e.activation(SROW[:, 96:112], SROW[:, 16:32],
                     AF.Ln, bias=EPS6), waits=[(V_, vv), (V_, consts_v)], inc=S_)
                sv = b.op("scalar", lambda e: e.activation(
                    SROW[:, 32:48], SROW[:, 96:112], AF.Exp, scale=-0.5),
                    waits=[(S_, m1)], inc=S_)
            else:
                vv = b.op("vector", lambda e: e.tensor_scalar(
                    SROW[:, 16:32], SROW[:, 0:16], EPS24, None, ALU.max),
                    waits=[(S_, s_st), (V_, consts_v)], inc=V_)
                m1 = b.op("scalar", lambda e: e.activation(SROW[:, 96:112], SROW[:, 16:32],
                     AF.Ln), waits=[(V_, vv)], inc=S_)
                sv = b.op("scalar", lambda e: e.activation(SROW[:, 32:48], SROW[:, 96:112],
                     AF.Exp, scale=-0.5), waits=[(S_, m1)], inc=S_)
        prev_evac = {}
        last_prev = st["xnt_last_s"]
        for i in range(NT):
            par = i % 2
            xt = XNTI[par]
            guard = prev_evac.get(i-2, last_prev)
            if mode == "none":
                vv = b.op("vector", lambda e, xt=xt, i=i: e.tensor_copy(
                    xt, AZ[:, D*i:D*(i+1)]),
                    waits=[(V_, zt[i]), (V_, guard)], inc=V_)
            else:
                vv = b.op("vector", lambda e, i=i, xt=xt, nrmb=nrmb:
                          e.scalar_tensor_tensor(
                    xt, AZ[:, D*i:D*(i+1)], SROW[:, 32+i:33+i], nrmb[:],
                    ALU.mult, ALU.mult),
                    waits=[(S_, sv), (V_, nrm_v), (V_, zt[i]), (V_, guard)], inc=V_)
            tv = b.op("tensor", lambda e, xt=xt: [
                e.transpose(PT[:, 128*j:128*(j+1)], xt[:, 128*j:128*(j+1)], IDENT[:])
                for j in range(8)][-1],
                waits=[(V_, vv), (V_, prev_evac.get(i-1, last_prev)),
                       (WG, consts_g)], inc=TT_)
            prev_evac[i] = b.op("vector", lambda e, i=i: e.tensor_copy(
                AXNT[:].rearrange("p (j c) -> p j c", j=NC)[:, :, 128*i:128*(i+1)],
                PT[:].rearrange("p (j c) -> p j c", j=NC)),
                waits=[(TT_, tv)], inc=V_)
            st["xnt_tile_s"][i] = prev_evac[i]
        st["xnt_s"] = prev_evac[NT-1]
        st["xnt_last_s"] = prev_evac[NT-1]
        set_bars()

    def emit_attention(kind):
        wname = {"f": "f_wqkv", "a": "a_w", "t": "t_w"}[kind]
        woname = {"f": "f_wout", "a": "a_wout", "t": "t_wr"}[kind]
        xnt_ts = dict(st["xnt_tile_s"])
        causal = kind == "t"
        rope = kind == "f"
        VAUG = AW[:, 8192:10272]
        VAUG4 = VAUG.rearrange("p (k h c) -> p k h c", k=NT, c=65)
        EB = [AW[:, 6144:7168], AW[:, 7168:8192]]

        st["pend_evac"] = None
        ones_v = b.op("vector", lambda e: e.memset(
            VAUG.rearrange("p (h c) -> p h c", c=65)[:, :, 64:65], 1.0),
            waits=bars(), inc=V_)

        def flush_evac():
            nonlocal b_evac_v, rb_v, ot_v
            pe_ = st.get("pend_evac")
            if pe_ is None:
                return
            pp_, hl_, qh_, dm, as0, as1 = pe_
            st["pend_evac"] = None
            base_ = 64 * hl_
            tvs = []
            for q in range(2):
                tvr = b.op("tensor", lambda e, q=q: e.matmul(
                    PC[0:64, 0:512], ONESRB[0:1, 0:64],
                    RROW[0:1, 512*q:512*(q+1)], start=True, stop=True),
                    waits=[(S_, dm), (V_, rb_v), (WG, consts_g),
                           (V_, st.get("csqm_l", 0))], inc=TT_)
                rb_v = b.op("vector", lambda e, q=q: e.tensor_copy(
                    AF32[0:64, 512*q:512*(q+1)], PC[0:64, 0:512]),
                    waits=[(TT_, tvr)], inc=V_)
            st["rb_t"] = TT_.n
            if debug == "tagot":
                b_evac_v = b.op("vector",
                                lambda e, pp_=pp_, base_=base_, qh_=qh_: e.memset(
                    ABIG[:, 2048*pp_:2048*(pp_+1)][base_:base_+64,
                                                   1024*qh_:1024*(qh_+1)],
                    float(100*pp_ + 10*(base_//64) + qh_ + 1)),
                    waits=[(V_, rb_v), (TT_, csq_t)], inc=V_)
            else:
                b_evac_v = b.op("vector",
                                lambda e, pp_=pp_, base_=base_, qh_=qh_: e.tensor_mul(
                    ABIG[:, 2048*pp_:2048*(pp_+1)][base_:base_+64,
                                                   1024*qh_:1024*(qh_+1)],
                    PB[0:64, 0:1024], AF32[0:64, 0:1024]),
                    waits=[(V_, rb_v), (TT_, csq_t)], inc=V_)
            ot_v = b_evac_v
        tab_g = st.get("tab_g", 0)
        if rope and st.get("tab") != "rope":
            tab_g = b.op("gpsimd", lambda e: e.dma_start(
                ATAB[:, 0:2048], dr["ropeC"]), waits=bars(), inc=WG, dma=True)
            tab_g = b.op("gpsimd", lambda e: e.dma_start(
                ATAB[:, 2048:4096], dr["ropeS"]), inc=WG, dma=True)
            st["tab"] = "rope"
            st["tab_g"] = tab_g
        if causal and st.get("tab") != "stair":
            tab_g = b.op("gpsimd", lambda e: e.dma_start(
                ATAB[:, 0:1920], dr["stair"]), waits=bars(), inc=WG, dma=True)
            st["tab"] = "stair"
            st["tab_g"] = tab_g
        wslab_t = {0: st["bar_t"], 1: st["bar_t"]}
        vaug_t = 0
        e_t = {0: 0, 1: 0}
        b_evac_v = 0
        rb_v = 0
        cneg_s = 0
        asm_t = st["bar_t"]
        bias_v = 0
        vb_t = 0
        csqm_v = 0
        pa_s = 0
        ot_v = 0
        zv0 = st["zdone"]

        for p in range(8):
            par = p % 2
            WSL = AW[:, 3072*par:3072*(par+1)]
            WSLv = WSL.rearrange("p (j f) -> p j f", j=NC)
            wg = b.op("gpsimd", lambda e, p=p, WSL=WSL: e.dma_start(
                WSL.rearrange("p (j f) -> p j f", j=NC),
                dr[wname][p].rearrange("(j q) f -> q j f", q=128)),
                waits=[(TT_, wslab_t[par])], inc=WG, dma=True)
            wb = 0
            if rope:
                for (dst, src) in ((BQ, dr["f_bqkv"][p, 0]), (BK, dr["f_bqkv"][p, 1]),
                                   (BQR, dr["f_bqkrot"][p, 0]),
                                   (BKR, dr["f_bqkrot"][p, 1])):
                    wb = b.op("sync", lambda e, dst=dst, src=src: e.dma_start(
                        dst, col128(src)), waits=[(V_, bias_v)], inc=WX, dma=True)
            qk_v = 0
            sq_s = {}
            DUMP = ABIG[:, 2048*p:2048*(p+1)]

            def rope_or_copy(DST, bcol, bcolr, sqm, tv):
                if rope:
                    b.op("vector", lambda e, bcol=bcol: e.scalar_tensor_tensor(
                        AF32[:, 0:2048], PA[:], bcol, ATAB[:, 0:2048],
                        ALU.add, ALU.mult),
                        waits=[(TT_, tv), (WX, wb), (TT_, asm_t),
                               (V_, b_evac_v), (WG, tab_g), (S_, sqm)], inc=V_)
                    r2 = 0
                    for (od, sd) in ((0, 32), (32, 0), (64, 96), (96, 64)):
                        r2 = b.op("vector", lambda e, od=od, sd=sd, bcolr=bcolr:
                             e.scalar_tensor_tensor(
                                 AF32[od:od+32, 2048:4096], PA[sd:sd+32, :],
                                 bcolr[od:od+32, :], ATAB[od:od+32, 2048:4096],
                                 ALU.add, ALU.mult), inc=V_)
                    return b.op("vector", lambda e, DST=DST: e.tensor_add(
                        DST, AF32[:, 0:2048], AF32[:, 2048:4096]),
                        waits=[(V_, r2)], inc=V_)
                return b.op("vector", lambda e, DST=DST: e.tensor_copy(DST, PA[:]),
                            waits=[(TT_, tv), (TT_, asm_t), (V_, b_evac_v),
                                   (S_, sqm)], inc=V_)

            def emit_v_half(h0, h1, gv):
                for tt in range(h0, h1):
                    ph = tt % 2
                    tv = b.op("tensor", lambda e, tt=tt, ph=ph, WSLv=WSLv,
                              last=(kind != "f"): [
                        e.matmul(PB[:, 512*ph:512*ph+128],
                                 AXNT[:, T*j+128*tt:T*j+128*(tt+1)],
                                 WSLv[:, j, 256:384],
                                 start=(j == 0), stop=(last and j == NC-1))
                        for j in range(NC)][-1],
                        waits=[(S_, vev.get(tt-2, 0)), (WG, wg),
                               (V_, xnt_ts[tt]),
                               (V_, ones_v), (TT_, vaug_t), (V_, st["pb_v"]),
                               (V_, b_evac_v)], inc=TT_)
                    if kind == "f":
                        tv = b.op("tensor", lambda e, ph=ph: e.matmul(
                            PB[:, 512*ph:512*ph+128], ONESRB[:], GROWB[0:1, 0:128],
                            start=False, stop=True), waits=[(WG, gv)], inc=TT_)
                    vev[tt] = b.op("scalar", lambda e, tt=tt, ph=ph: e.copy(
                        VAUG4[:, tt, :, 0:64],
                        PB[:, 512*ph:512*ph+128].rearrange("p (h c) -> p h c", h=2)),
                        waits=[(TT_, tv), (V_, ones_v)], inc=S_)

            def emit_csq(which, sqm):
                cm = 0
                for q in range(4):
                    tv2 = b.op("tensor", lambda e, q=q, DUMP=DUMP: e.matmul(
                        PC[0:1, 0:512], ONESC[:], DUMP[:, 512*q:512*(q+1)],
                        start=True, stop=True),
                        waits=[(S_, sqm), (V_, st.get("csqm_l", 0)), (V_, rb_v),
                               (WG, consts_g)], inc=TT_)
                    cm = b.op("vector", lambda e, which=which, q=q: e.reduce_max(
                        SROW[0:1, 96+4*which+q:97+4*which+q], PC[0:1, 0:512],
                        axis=AX.X), waits=[(TT_, tv2)], inc=V_)
                    st["csqm_l"] = cm
                return cm

            gv = 0
            if kind == "f":
                gv = b.op("gpsimd", lambda e, p=p: e.dma_start(
                    GROWB[0:1, 0:128],
                    dr["f_bqkv"][p, 2].rearrange("(a f) -> a f", a=1)),
                    waits=[(TT_, vb_t)], inc=WG, dma=True)
            vev = {}
            for q in range(4):
                tvq = b.op("tensor", lambda e, WSLv=WSLv, q=q: [
                    e.matmul(PA[:, 512*q:512*(q+1)], WSLv[:, j, 0:128],
                             AXNT[:, T*j+512*q:T*j+512*(q+1)],
                             start=(j == 0), stop=(j == NC-1))
                    for j in range(NC)][-1],
                    waits=[(WG, wg), (V_, xnt_ts[4*q+3]), (TT_, asm_t),
                           (S_, pa_s), (V_, rb_v)], inc=TT_)
            flush_evac()
            sq_s[0] = b.op(
                "scalar",
                (lambda e, DUMP=DUMP: e.activation(DUMP, PA[:], AF.Square, bias=BQ))
                if rope else
                (lambda e, DUMP=DUMP: e.activation(DUMP, PA[:], AF.Square)),
                waits=[(TT_, tvq), (WX, wb), (S_, cneg_s), (V_, ot_v),
                       (TT_, st["bar_t"]), (TT_, st.get("sqd_t", 0))], inc=S_)
            qk_v = rope_or_copy(QT, BQ, BQR, sq_s[0], tvq)
            bias_v = qk_v
            emit_v_half(0, 8, gv)
            cmq = emit_csq(0, sq_s[0])
            for q in range(4):
                tvk = b.op("tensor", lambda e, WSLv=WSLv, q=q: [
                    e.matmul(PA[:, 512*q:512*(q+1)], WSLv[:, j, 128:256],
                             AXNT[:, T*j+512*q:T*j+512*(q+1)],
                             start=(j == 0), stop=(j == NC-1))
                    for j in range(NC)][-1],
                    waits=[(WG, wg), (V_, xnt_ts[4*q+3]), (S_, sq_s[0]),
                           (V_, qk_v)], inc=TT_)
            sq_s[1] = b.op("scalar", (lambda e, DUMP=DUMP: e.activation(
                DUMP, PA[:], AF.Square, bias=BK)) if rope else
                (lambda e, DUMP=DUMP: e.activation(DUMP, PA[:], AF.Square)),
                waits=[(TT_, tvk), (WX, wb), (S_, sq_s[0]), (V_, cmq)], inc=S_)
            qk_v = rope_or_copy(KT, BK, BKR, sq_s[1], tvk)
            bias_v = qk_v
            emit_v_half(8, NT, gv)
            emit_csq(1, sq_s[1])
            v_ready = vev[NT-1]
            if kind == "f":
                vb_t = TT_.n
            csq_t = TT_.n
            st["sqd_t"] = TT_.n
            if debug == "qt" and p == 0:
                last = b.op("gpsimd", lambda e: e.dma_start(
                    xout.rearrange("(a p) d -> p a d", p=128)[:, 0:4, :],
                    ASM[:, 0:4096].rearrange("p (a d) -> p a d", a=4)),
                    waits=[(V_, qk_v)], inc=WG, dma=True)
                b.op("sync", lambda e: e.wait_ge(b.WG.h, last))
                b.finalize()
                return "OT_DUMP"
            vv = b.op("vector", lambda e: e.reduce_max(
                SROW[0:1, 104:105], SROW[0:1, 96:100], axis=AX.X),
                waits=[(V_, st.get("csqm_l", 0))], inc=V_)
            vv = b.op("vector", lambda e: e.reduce_max(
                SROW[0:1, 105:106], SROW[0:1, 100:104], axis=AX.X),
                waits=[(V_, st.get("csqm_l", 0))], inc=V_)
            b.op("scalar", lambda e: e.activation(SROW[0:1, 106:107],
                 SROW[0:1, 104:105], AF.Ln), waits=[(V_, vv)], inc=S_)
            sv = b.op("scalar", lambda e: e.activation(SROW[0:1, 107:108],
                      SROW[0:1, 105:106], AF.Ln), inc=S_)
            vv = b.op("vector", lambda e: e.tensor_add(
                SROW[0:1, 106:107], SROW[0:1, 106:107], SROW[0:1, 107:108]),
                waits=[(S_, sv)], inc=V_)
            sv = b.op("scalar", lambda e: e.activation(SROW[0:1, 108:109],
                      SROW[0:1, 106:107], AF.Exp, scale=0.5,
                      bias=SBIAS[0:1, 8:9]), waits=[(V_, vv)], inc=S_)
            tv = b.op("tensor", lambda e: e.matmul(
                PC[:, 0:1], ONESR[:], SROW[0:1, 108:109], start=True, stop=True),
                waits=[(S_, sv), (V_, st.get("csqm_l", 0))], inc=TT_)
            cneg_v = b.op("vector", lambda e: e.tensor_scalar_mul(
                CNEG, PC[:, 0:1], -1.0), waits=[(TT_, tv), (S_, cneg_s)], inc=V_)
            wslab_t[par] = TT_.n

            for hl in range(2):
                base = 64 * hl
                for qh in range(2):
                    kts = [kt for kt in range(NT)
                           if not (causal and 128*kt > 1024*qh + 1023)]
                    a_s = {0: pa_s, 1: pa_s}
                    pend_av = None
                    first = True

                    def emit_av(pend, last):
                        nonlocal vaug_t
                        kt, ap_, E, av_w, fst = pend
                        tv = b.op("tensor", lambda e, kt=kt, hl=hl, E=E,
                                  fst=fst, last=last: [
                            e.matmul(PB[0:65, 512*q:512*(q+1)],
                                     VAUG[:, 130*kt+65*hl:130*kt+65*hl+65],
                                     E[:, 512*q:512*(q+1)],
                                     start=fst, stop=last) for q in range(2)][-1],
                            waits=[av_w, (S_, v_ready), (V_, b_evac_v)], inc=TT_)
                        vaug_t = tv
                        e_t[ap_] = tv

                    for idx, kt in enumerate(kts):
                        ap_ = idx % 2
                        A = PA[:, 1024*ap_:1024*(ap_+1)]
                        E = EB[ap_]
                        tv = b.op("tensor",
                                  lambda e, kt=kt, qh=qh, base=base, A=A: [
                            e.matmul(A[:, 512*q:512*(q+1)],
                                     KT[base:base+64, 128*kt:128*(kt+1)],
                                     QT[base:base+64,
                                        1024*qh+512*q:1024*qh+512*(q+1)],
                                     start=True, stop=True) for q in range(2)][-1],
                            waits=[(S_, a_s[ap_]), (V_, qk_v)], inc=TT_)
                        asm_t = tv
                        sv = b.op("scalar", lambda e, A=A, E=E: e.activation(
                            E, A, AF.Exp, bias=CNEG, scale=0.125),
                            waits=[(TT_, tv), (V_, cneg_v), (TT_, e_t[ap_]),
                                   (TT_, st.get("rb_t", 0))], inc=S_)
                        a_s[ap_] = sv
                        cneg_s = sv
                        pa_s = sv
                        av_w = (S_, sv)
                        if causal:
                            s_ = 128*kt - 1024*qh
                            if 0 <= s_ <= 896:
                                mv = b.op("vector", lambda e, E=E, s_=s_: e.tensor_mul(
                                    E, E, ATAB[:, 896-s_:1920-s_]),
                                    waits=[(S_, sv), (WG, tab_g)], inc=V_)
                                av_w = (V_, mv)
                        if idx == 1 and st.get("pend_evac") is not None:
                            flush_evac()
                        if pend_av is not None:
                            emit_av(pend_av, False)
                        pend_av = (kt, ap_, E, av_w, first)
                        first = False
                    emit_av(pend_av, True)
                    if debug == "att1" and p == 0 and hl == 0 and qh == 0:
                        cv = b.op("vector", lambda e: e.tensor_copy(
                            AF32[0:65, 0:1024], PB[0:65, 0:1024]),
                            waits=[(TT_, vaug_t)], inc=V_)
                        last = b.op("sync", lambda e: e.dma_start(
                            xout[0:65, 0:1024], AF32[0:65, 0:1024]),
                            waits=[(V_, cv)], inc=WX, dma=True)
                        b.op("sync", lambda e: e.wait_ge(b.WX.h, last))
                        b.finalize()
                        return "OT_DUMP"
                    m1 = b.op("scalar", lambda e: e.activation(
                        ROWT[:, 0:1024], PB[64:65, 0:1024], AF.Ln),
                        waits=[(TT_, vaug_t)], inc=S_)
                    dm = b.op("scalar", lambda e: e.activation(
                        RROW[:, 0:1024], ROWT[:, 0:1024], AF.Exp, scale=-1.0),
                        waits=[(S_, m1), (TT_, st.get("rb_t", 0))], inc=S_)
                    st["pend_evac"] = (p, hl, qh, dm, a_s[0], a_s[1])
        flush_evac()
        st["af32_free_v"] = V_.n

        if debug in ("ot", "tagot", "tagb"):
            ot_all = ot_v
            last = b.op("gpsimd", lambda e: e.dma_start(
                xout.rearrange("(a p) d -> p a d", p=128),
                ABIG[:].rearrange("p (a d) -> p a d", a=16)),
                waits=[(V_, ot_all)] + [(TT_, TT_.n), (S_, S_.n)], inc=WG, dma=True)
            b.op("sync", lambda e: e.wait_ge(b.WG.h, last))
            b.finalize()
            return "OT_DUMP"
        wgo = b.op("gpsimd", lambda e: e.dma_start(
            AW[:, 0:8192].rearrange("p (j f) -> p j f", j=NC),
            dr[woname].rearrange("(j q) f -> q j f", q=128)),
            waits=[(TT_, asm_t), (TT_, wslab_t[0]), (TT_, wslab_t[1]),
                   (TT_, TT_.n)], inc=WG, dma=True)
        gv = 0
        if kind == "f":
            gv = b.op("gpsimd", lambda e: e.dma_start(GROWB[0:1, 0:1024], dr["f_outb"]),
                      waits=[(TT_, vb_t), (TT_, st.get("growb_t", 0))], inc=WG, dma=True)
        opev = {}
        for tt in range(NT):
            ph = tt % 2
            PAH = PA[:, 1024*ph:1024*(ph+1)]
            tv = b.op("tensor", lambda e, tt=tt, PAH=PAH, isf=(kind == "f"): [
                e.matmul(PAH[:, 512*q:512*(q+1)],
                         ABIG[:, 2048*j+128*tt:2048*j+128*(tt+1)],
                         AW[:, 1024*j+512*q:1024*j+512*(q+1)],
                         start=(j == 0), stop=(not isf) and (j == NC-1))
                for j in range(NC) for q in range(2)][-1],
                waits=[(WG, wgo), (V_, ot_v), (V_, opev.get(tt-2, 0)),
                       (S_, pa_s), (V_, rb_v), (V_, b_evac_v)], inc=TT_)
            if kind == "f":
                tv = b.op("tensor", lambda e, PAH=PAH: [
                    e.matmul(PAH[:, 512*q:512*(q+1)], ONESRB[:],
                             GROWB[0:1, 512*q:512*(q+1)], start=False, stop=True)
                    for q in range(2)][-1], waits=[(WG, gv)], inc=TT_)
                st["growb_t"] = tv
            if kind == "f":
                opev[tt] = b.op("vector", lambda e, tt=tt, PAH=PAH: e.tensor_add(
                    AZ[:, D*tt:D*(tt+1)], AZ[:, D*tt:D*(tt+1)], PAH),
                    waits=[(TT_, tv), (V_, zv0)], inc=V_)
            else:
                opev[tt] = b.op("vector", lambda e, tt=tt, PAH=PAH: e.tensor_copy(
                    AZ[:, D*tt:D*(tt+1)], PAH), waits=[(TT_, tv), (V_, zv0)], inc=V_)
            st["ztile_v"][tt] = opev[tt]
        st["zdone"] = V_.n
        set_bars()

    def emit_moe():
        xnt_ts = dict(st["xnt_tile_s"])
        zv0 = st["zdone"]
        gl_v = 0
        for tt in range(NT):
            tv = b.op("tensor", lambda e, tt=tt: [
                e.matmul(PC[:, 8*tt:8*(tt+1)],
                         AXNT[:, T*j+128*tt:T*j+128*(tt+1)], SGATE[:, 8*j:8*(j+1)],
                         start=(j == 0), stop=(j == NC-1)) for j in range(NC)][-1],
                waits=[(V_, xnt_ts[tt]), (WG, consts_g), (TT_, st["bar_t"]),
                       (V_, gl_v)], inc=TT_)
            gl_v = b.op("vector", lambda e, tt=tt: e.tensor_copy(
                SCOMB[:, 8*tt:8*(tt+1)], PC[:, 8*tt:8*(tt+1)]),
                waits=[(TT_, tv), (V_, st["bar_v"])], inc=V_)
        comb_v = 0
        for tt in range(NT):
            LG = SCOMB[:, 8*tt:8*(tt+1)]
            m = b.op("vector", lambda e, LG=LG: e.reduce_max(
                SMOE[:, 0:1], LG, axis=AX.X), waits=[(V_, gl_v), (S_, st["bar_s"])],
                inc=V_)
            m = b.op("vector", lambda e, LG=LG: e.tensor_scalar(
                SMOE[:, 8:16], LG, SMOE[:, 0:1], None, ALU.is_ge),
                waits=[(V_, m)], inc=V_)
            m = b.op("vector", lambda e, LG=LG: e.scalar_tensor_tensor(
                SMOE[:, 16:24], SMOE[:, 8:16], -1e30, LG, ALU.mult, ALU.add),
                waits=[(V_, m)], inc=V_)
            m = b.op("vector", lambda e: e.reduce_max(
                SMOE[:, 1:2], SMOE[:, 16:24], axis=AX.X), waits=[(V_, m)], inc=V_)
            m2 = b.op("vector", lambda e: e.tensor_scalar(
                SMOE[:, 24:32], SMOE[:, 16:24], SMOE[:, 1:2], None, ALU.is_ge),
                waits=[(V_, m)], inc=V_)
            vv = b.op("vector", lambda e: e.tensor_sub(
                SMOE[:, 1:2], SMOE[:, 1:2], SMOE[:, 0:1]), waits=[(V_, m2)], inc=V_)
            sv = b.op("scalar", lambda e: e.activation(
                SMOE[:, 2:3], SMOE[:, 1:2], AF.Exp), waits=[(V_, vv)], inc=S_)
            m = b.op("vector", lambda e: e.tensor_scalar_add(
                SMOE[:, 3:4], SMOE[:, 2:3], 1.0), waits=[(S_, sv)], inc=V_)
            m = b.op("vector", lambda e: e.reciprocal(SMOE[:, 4:5], SMOE[:, 3:4]),
                waits=[(V_, m)], inc=V_)
            m = b.op("vector", lambda e: e.tensor_mul(
                SMOE[:, 5:6], SMOE[:, 2:3], SMOE[:, 4:5]), waits=[(V_, m)], inc=V_)
            m = b.op("vector", lambda e: e.tensor_scalar_mul(
                SMOE[:, 24:32], SMOE[:, 24:32], SMOE[:, 5:6]), waits=[(V_, m)], inc=V_)
            comb_v = b.op("vector", lambda e, LG=LG: e.scalar_tensor_tensor(
                LG, SMOE[:, 8:16], SMOE[:, 4:5], SMOE[:, 24:32],
                ALU.mult, ALU.add), waits=[(V_, m)], inc=V_)
        w12_t = {0: st["bar_t"], 1: st["bar_t"]}
        w3_t = {0: st["bar_t"], 1: st["bar_t"]}
        h_v = 0
        h_t = st["bar_t"]
        sil_s = {0: 0, 1: 0}
        ab_s = {0: st["bar_s"], 1: st["bar_s"]}
        ab_v = 0
        y_t = {0: 0, 1: 0}
        y_v = {0: 0, 1: 0}
        bias_x = 0
        W1C = [AW[:, 0:2048].rearrange("p (j h) -> p j h", j=NC),
               AW[:, 2048:4096].rearrange("p (j h) -> p j h", j=NC)]
        W2C = [AW[:, 4096:6144].rearrange("p (j h) -> p j h", j=NC),
               AW[:, 6144:8192].rearrange("p (j h) -> p j h", j=NC)]
        W3Q = [ABIG[:, 8192:12288].rearrange("p (k d) -> p k d", k=4),
               ABIG[:, 12288:16384].rearrange("p (k d) -> p k d", k=4)]
        aw_t0 = st["bar_t"]
        for ee in range(NE):
            bx = b.op("sync", lambda e, ee=ee: e.dma_start(B1C, dr["f_b1T"][ee]),
                      waits=[(S_, ab_s[0]), (S_, ab_s[1]), (V_, ab_v)], inc=WX,
                      dma=True)
            bx = b.op("sync", lambda e, ee=ee: e.dma_start(B2C, dr["f_b2T"][ee]),
                      inc=WX, dma=True)
            gv = b.op("gpsimd", lambda e, ee=ee: e.dma_start(
                GROWB[0:1, 0:1024], dr["f_b3"][ee].rearrange("(a f) -> a f", a=1)),
                waits=[(TT_, st.get("growb_t", 0))], inc=WG, dma=True)
            for hq in range(4):
                wg3 = b.op("gpsimd", lambda e, ee=ee, hq=hq: e.dma_start(
                    W3Q[hq % 2], dr["f_w3T"][ee][512*hq:512*(hq+1), :]
                    .rearrange("(k q) d -> q k d", q=128)),
                    waits=[(TT_, w3_t[hq % 2]), (S_, st["bar_s"])], inc=WG, dma=True)
                for hc in range(2):
                    cpar = (2*hq + hc) % 2
                    wg1 = b.op("gpsimd", lambda e, ee=ee, hq=hq, hc=hc, cpar=cpar:
                               e.dma_start(
                        W1C[cpar], dr["f_w1T"][ee][:, 512*hq+256*hc:512*hq+256*(hc+1)]
                        .rearrange("(j q) h -> q j h", q=128)),
                        waits=[(TT_, w12_t[cpar]), (TT_, aw_t0)], inc=WG, dma=True)
                    wg2 = b.op("gpsimd", lambda e, ee=ee, hq=hq, hc=hc, cpar=cpar:
                               e.dma_start(
                        W2C[cpar], dr["f_w2T"][ee][:, 512*hq+256*hc:512*hq+256*(hc+1)]
                        .rearrange("(j q) h -> q j h", q=128)), inc=WG, dma=True)
                    for ht2 in range(2):
                        ht = 2*hc + ht2
                        bcol1 = B1C[:, 4*hq+ht:4*hq+ht+1]
                        bcol2 = B2C[:, 4*hq+ht:4*hq+ht+1]
                        for th in range(2):
                            ph = (2*ht + th) % 2
                            A = PA[:, 1024*ph:1024*(ph+1)]
                            SSI = ASM[:, 1024*ph:1024*(ph+1)]
                            ta = b.op("tensor", lambda e, ht2=ht2, th=th, A=A,
                                      cpar=cpar, wg1=wg1: [
                                e.matmul(A[:, 512*q:512*(q+1)],
                                         W1C[cpar][:, j, 128*ht2:128*(ht2+1)],
                                         AXNT[:, T*j+1024*th+512*q:T*j+1024*th+512*(q+1)],
                                         start=(j == 0), stop=(j == NC-1))
                                for q in range(2) for j in range(NC)][-1],
                                waits=[(WG, wg1), (V_, xnt_ts[8*th+7]),
                                       (S_, ab_s[ph]), (V_, y_v[ph])], inc=TT_)
                            tb = b.op("tensor", lambda e, ht2=ht2, th=th,
                                      cpar=cpar, wg2=wg2: [
                                e.matmul(PB[:, 512*q:512*(q+1)],
                                         W2C[cpar][:, j, 128*ht2:128*(ht2+1)],
                                         AXNT[:, T*j+1024*th+512*q:T*j+1024*th+512*(q+1)],
                                         start=(j == 0), stop=(j == NC-1))
                                for q in range(2) for j in range(NC)][-1],
                                waits=[(WG, wg2), (V_, ab_v), (V_, st["pb_v"])], inc=TT_)
                            w12_t[cpar] = tb
                            sv = b.op("scalar", lambda e, A=A, SSI=SSI, bcol1=bcol1:
                                      e.activation(SSI, A, AF.Silu, bias=bcol1),
                                      waits=[(TT_, ta), (WX, bx), (V_, h_v)], inc=S_)
                            ab_s[ph] = sv
                            h_v = b.op("vector", lambda e, ht=ht, th=th, SSI=SSI,
                                       bcol2=bcol2: e.scalar_tensor_tensor(
                                ABIG[:, 2048*ht+1024*th:2048*ht+1024*(th+1)],
                                PB[:], bcol2, SSI, ALU.add, ALU.mult),
                                waits=[(TT_, tb), (S_, sv), (WX, bx), (TT_, h_t)],
                                inc=V_)
                            ab_v = h_v
                for ts_ in range(NT):
                    ph = ts_ % 2
                    Y = PA[:, 0:1024] if ph == 0 else PB[:, 0:1024]
                    tv = b.op("tensor", lambda e, ts_=ts_, Y=Y, hq=hq: [
                        e.matmul(Y[:, 512*q:512*(q+1)],
                                 ABIG[:, 2048*k+128*ts_:2048*k+128*(ts_+1)],
                                 W3Q[hq % 2][:, k, 512*q:512*(q+1)],
                                 start=(k == 0), stop=(hq != 0 and k == 3))
                        for k in range(4) for q in range(2)][-1],
                        waits=[(WG, wg3), (V_, h_v), (V_, y_v[ph]), (S_, ab_s[ph]),
                               (V_, st["pb_v"]) if ph else (V_, 0),
                               (V_, ab_v) if ph else (V_, 0)], inc=TT_)
                    h_t = tv
                    w3_t[hq % 2] = tv
                    if hq == 0:
                        tv = b.op("tensor", lambda e, Y=Y: [
                            e.matmul(Y[:, 512*q:512*(q+1)], ONESRB[:],
                                     GROWB[0:1, 512*q:512*(q+1)],
                                     start=False, stop=True)
                            for q in range(2)][-1], waits=[(WG, gv)], inc=TT_)
                        st["growb_t"] = tv
                    y_t[ph] = tv
                    y_v[ph] = b.op("vector", lambda e, ts_=ts_, Y=Y, ee=ee:
                                   e.scalar_tensor_tensor(
                        AZ[:, D*ts_:D*(ts_+1)], Y, SCOMB[:, 8*ts_+ee:8*ts_+ee+1],
                        AZ[:, D*ts_:D*(ts_+1)], ALU.mult, ALU.add),
                        waits=[(TT_, tv), (V_, comb_v), (V_, zv0)], inc=V_)
                    if ee == NE - 1 and hq == 3:
                        st["ztile_v"][ts_] = y_v[ph]
                    if ph == 1:
                        ab_v = y_v[ph]
        st["zdone"] = V_.n
        set_bars()
        st["bar_g"] = WG.n

    def emit_l1ffn():
        xnt_ts = dict(st["xnt_tile_s"])
        zv0 = st["zdone"]
        W1S = [AW[:, 0:1024], AW[:, 1024:2048]]
        W2S = AW[:, 2048:10240]
        W2Sv = W2S.rearrange("p (fs d) -> p fs d", fs=8)
        gv = b.op("gpsimd", lambda e: e.dma_start(GROWB[0:1, 0:1024], dr["g_b2"]),
                  waits=[(TT_, st.get("growb_t", 0))], inc=WG, dma=True)
        w1_t = {0: st["bar_t"], 1: st["bar_t"]}
        w2_t = st["bar_t"]
        h_v = 0
        h_t = st["bar_t"]
        hv_ph = {0: st["bar_v"], 1: st["bar_v"]}
        y_v = {0: st["bar_v"], 1: st["bar_v"]}
        for fq in range(4):
            w2q = list(range(8))

            def emit_w2(n, fq=fq):
                nonlocal wg2
                for _ in range(n):
                    if not w2q:
                        return
                    fs = w2q.pop(0)
                    gfs = 8*fq + fs
                    wg2 = b.op("gpsimd", lambda e, gfs=gfs, fs=fs: e.dma_start(
                        W2Sv[:, fs, :], dr["g_w2"][128*gfs:128*(gfs+1), :]),
                        waits=[(TT_, w2_t)], inc=WG, dma=True)
            wg2 = None
            for fs in range(8):
                gfs = 8*fq + fs
                par = fs % 2
                wg1 = b.op("gpsimd", lambda e, gfs=gfs, par=par: e.dma_start(
                    W1S[par].rearrange("p (j f) -> p j f", j=NC),
                    dr["g_w1"][gfs].rearrange("(j q) f -> q j f", q=128)),
                    waits=[(TT_, w1_t[par])], inc=WG, dma=True)
                if fs >= 2:
                    emit_w2(3)
                for th in range(2):
                    ph = th
                    A = PA[:, 1024*ph:1024*(ph+1)]
                    tv = b.op("tensor", lambda e, par=par, th=th, A=A: [
                        e.matmul(A[:, 512*q:512*(q+1)],
                                 W1S[par].rearrange("p (j f) -> p j f", j=NC)[:, j, :],
                                 AXNT[:, T*j+1024*th+512*q:T*j+1024*th+512*(q+1)],
                                 start=(j == 0), stop=(j == NC-1))
                        for q in range(2) for j in range(NC)][-1],
                        waits=[(WG, wg1), (V_, xnt_ts[8*th+7]), (V_, hv_ph[ph]),
                               (V_, y_v[ph])], inc=TT_)
                    w1_t[par] = tv
                    h_v = b.op("vector", lambda e, fs=fs, th=th, A=A, gfs=gfs:
                               e.tensor_scalar(
                        ABIG[:, 2048*fs+1024*th:2048*fs+1024*(th+1)], A,
                        B1L[:, gfs:gfs+1], 0.0, ALU.add, ALU.max),
                        waits=[(TT_, tv), (WX, consts_x), (TT_, h_t)], inc=V_)
                    hv_ph[ph] = h_v
            for ts_ in range(NT):
                ph = ts_ % 2
                Y = PA[:, 0:1024] if ph == 0 else PB[:, 0:1024]
                tv = b.op("tensor", lambda e, ts_=ts_, Y=Y, fq=fq: [
                    e.matmul(Y[:, 512*q:512*(q+1)],
                             ABIG[:, 2048*k+128*ts_:2048*k+128*(ts_+1)],
                             W2Sv[:, k, 512*q:512*(q+1)],
                             start=(k == 0), stop=(fq != 0 and k == 7))
                    for k in range(8) for q in range(2)][-1],
                    waits=[(WG, wg2), (V_, h_v), (V_, y_v[ph]), (V_, hv_ph[ph]),
                           (V_, st["pb_v"]) if ph else (V_, 0)], inc=TT_)
                h_t = tv
                w2_t = tv
                if fq == 0:
                    tv = b.op("tensor", lambda e, Y=Y: [
                        e.matmul(Y[:, 512*q:512*(q+1)], ONESRB[:],
                                 GROWB[0:1, 512*q:512*(q+1)], start=False,
                                 stop=True) for q in range(2)][-1],
                        waits=[(WG, gv)], inc=TT_)
                    st["growb_t"] = tv
                y_v[ph] = b.op("vector", lambda e, ts_=ts_, Y=Y: e.tensor_add(
                    AZ[:, D*ts_:D*(ts_+1)], AZ[:, D*ts_:D*(ts_+1)], Y),
                    waits=[(TT_, tv), (V_, zv0)], inc=V_)
                if fq == 3:
                    st["ztile_v"][ts_] = y_v[ph]
        st["zdone"] = V_.n
        set_bars()

    def emit_l1_final_ln(nrm_v):
        zt = st["ztile_v"]
        sqp = 0
        for i in range(NT):
            b.op("vector", lambda e, i=i: e.reduce_sum(
                SROW[:, i:i+1], AZ[:, D*i:D*(i+1)], axis=AX.X),
                waits=[(V_, zt[i])], inc=V_)
            sqp = b.op("scalar", lambda e, i=i: e.activation(
                SQD[:], AZ[:, D*i:D*(i+1)], AF.Square,
                accum_out=SROW[:, 16+i:17+i]),
                waits=[(V_, zt[i]), (S_, sqp)], inc=S_)
        v_st, s_st = V_.n, S_.n
        c1 = b.op("vector", lambda e: e.tensor_scalar_mul(
            SROW[:, 32:48], SROW[:, 0:16], 1.0 / D),
            waits=[(V_, v_st), (S_, s_st)], inc=V_)
        c2 = b.op("vector", lambda e: e.tensor_mul(
            SROW[:, 48:64], SROW[:, 32:48], SROW[:, 32:48]),
            waits=[(V_, c1)], inc=V_)
        vv = b.op("vector", lambda e: e.scalar_tensor_tensor(
            SROW[:, 48:64], SROW[:, 16:32], 1.0 / D, SROW[:, 48:64],
            ALU.mult, ALU.subtract), waits=[(V_, c2)], inc=V_)
        m1 = b.op("scalar", lambda e: e.activation(SROW[:, 96:112], SROW[:, 48:64],
             AF.Ln, bias=EPS5), waits=[(V_, vv), (V_, consts_v)], inc=S_)
        sv = b.op("scalar", lambda e: e.activation(SROW[:, 64:80], SROW[:, 96:112],
                  AF.Exp, scale=-0.5), waits=[(S_, m1)], inc=S_)
        nm = b.op("vector", lambda e: e.tensor_scalar_mul(
            SROW[:, 80:96], SROW[:, 32:48], -1.0), waits=[(V_, c1)], inc=V_)
        for i in range(NT):
            par = i % 2
            yt = AF32[:, 1024*par:1024*(par+1)]
            t1 = b.op("vector", lambda e, i=i, yt=yt: e.tensor_scalar(
                yt, AZ[:, D*i:D*(i+1)], SROW[:, 80+i:81+i], SROW[:, 64+i:65+i],
                ALU.add, ALU.mult), waits=[(S_, sv), (V_, nrm_v), (V_, nm)], inc=V_)
            t2 = b.op("vector", lambda e, i=i, yt=yt: e.tensor_mul(
                AZ[:, D*i:D*(i+1)], yt, NRMB[:]), waits=[(V_, t1)], inc=V_)
            t3 = b.op("vector", lambda e, i=i: e.tensor_add(
                AZ[:, D*i:D*(i+1)], AZ[:, D*i:D*(i+1)], NRMB2[:]),
                waits=[(V_, t2)], inc=V_)
            st["ztile_v"][i] = t3
        st["zdone"] = V_.n
        st["af32_free_v"] = V_.n
        set_bars()

    def dump_z():
        wv = st["zdone"]
        last = b.op("sync", lambda e: e.dma_start(
            xout.rearrange("(i p) d -> p i d", p=128),
            AZ[:].rearrange("p (i d) -> p i d", i=NT)),
            waits=[(V_, wv)] + bars(), inc=WX, dma=True)
        b.op("sync", lambda e: e.wait_ge(b.WX.h, last))
        b.finalize()
        return nc

    for si, s in enumerate(subs):
        kind = SUB_KIND[s]
        xin_ap, xo_ap = xs_seq[si], xs_seq[si + 1]
        if not st.get("z_is_ln", False):
            emit_ln(xin_ap)
        st["z_is_ln"] = False
        if debug == "stats":
            last = b.op("sync", lambda e: e.dma_start(
                xout[0:128, 0:128], SROW[:]),
                waits=[(V_, st["zdone"])] + bars(), inc=WX, dma=True)
            b.op("sync", lambda e: e.wait_ge(b.WX.h, last))
            b.finalize()
            return nc
        if debug == "ln":
            return dump_z()
        def dump_xnt():
            wv = st["xnt_s"]
            last = b.op("gpsimd", lambda e: e.dma_start(
                xout.rearrange("(a p) d -> p a d", p=128),
                AXNT[:].rearrange("p (a d) -> p a d", a=16)),
                waits=[(V_, wv)] + bars(), inc=WG, dma=True)
            b.op("sync", lambda e: e.wait_ge(b.WG.h, last))
            b.finalize()
            return nc

        if kind == "f":
            nv = row_bcast(NRMB, dr["f_n1w"])
            emit_to_xnt("rms_w", nv)
            if debug == "xnt":
                return dump_xnt()
            nv2 = row_bcast(NRMB2, dr["f_n2w"])
            if emit_attention("f") == "OT_DUMP":
                return nc
            emit_to_xnt("rms_w", nv2, nrmb=NRMB2)
            emit_moe()
        elif kind == "a":
            nv = row_bcast(NRMB, dr["a_gamma"], add=1.0, post_scale=32.0)
            emit_to_xnt("rms0", nv)
            if debug == "xnt":
                return dump_xnt()
            if emit_attention("a") == "OT_DUMP":
                return nc
        elif kind == "t":
            emit_to_xnt("none")
            if emit_attention("t") == "OT_DUMP":
                return nc
        else:
            emit_to_xnt("none")
            nv = row_bcast(NRMB, dr["g_gam"], scale=0.5)
            nv = row_bcast(NRMB2, dr["g_bet"], scale=0.5)
            emit_l1ffn()
            if debug == "body":
                return dump_z()
            emit_l1_final_ln(nv)
        if debug == "sub":
            return dump_z()
        fuse = si < len(subs) - 1
        emit_finale(xin_ap, xo_ap, SUB_ALPHA[s], fuse_ln=fuse)
        st["z_is_ln"] = fuse
    b.op("sync", lambda e: [e.wait_ge(b.WS0.h, b.WS0.n),
                            e.wait_ge(b.WS1.h, b.WS1.n)][-1] and None)
    b.finalize()
    return nc



_CACHE = {}

def _get_prog(key, subs):
    if key not in _CACHE:
        _CACHE[key] = build_program(subs)
    return _CACHE[key]


def kernel(**inputs):
    hp = host_prep(inputs)
    x = np.ascontiguousarray(np.asarray(inputs["L0_attn_x"], dtype=np.float32))
    xs = [np.ascontiguousarray(x[i]) for i in range(8)]
    for g in GROUPS:
        nc = _get_prog(tuple(g), list(g))
        names = [t for t in ("f", "a", "t", "g")
                 if t in {SUB_KIND[s] for s in g}]
        keys = ["c_ident", "c_ones_col", "c_ones_row"]
        if "f" in names:
            keys += ["f_wqkv", "f_bqkv", "f_bqkrot", "f_wout", "f_outb", "f_n1w",
                     "f_n2w", "f_gateT", "f_w1T", "f_w2T", "f_w3T", "f_b1T",
                     "f_b2T", "f_b3", "ropeC", "ropeS"]
        if "a" in names:
            keys += ["a_w", "a_wout", "a_gamma"]
        if "t" in names:
            keys += ["t_w", "t_wr", "stair"]
        if "g" in names:
            keys += ["g_w1", "g_b1T", "g_w2", "g_b2", "g_gam", "g_bet"]
        base = {k: hp[k] for k in keys}
        in_maps = [dict(base, xin=xs[c]) for c in range(8)]
        res = run_bass_kernel_spmd(nc, in_maps, core_ids=list(range(8)))
        xs = [np.asarray(res.results[c]["xout"]) for c in range(8)]
    return np.stack(xs).astype(np.float32)


# version 20
# speedup vs baseline: 1.0328x; 1.0179x over previous
import numpy as np
import concourse.bass as bass
import concourse.mybir as mybir
from concourse.bass_utils import run_bass_kernel_spmd

F32 = mybir.dt.float32
BF16 = mybir.dt.bfloat16
AF = mybir.ActivationFunctionType
ALU = mybir.AluOpType
AX = mybir.AxisListType

T = 2048
D = 1024
NT = 16
NC = 8
HD = 64
NE = 8
H0 = 2048
F1 = 4096
ENGINES = ("sync", "gpsimd", "tensor", "vector", "scalar")

GROUPS = [[0, 1, 2, 3, 4, 5]]

SUB_KIND = {0: "f", 1: "a", 2: "f", 3: "g", 4: "t", 5: "g"}
SUB_ALPHA = {0: 0.5, 1: 1.0, 2: 0.5, 3: 1.0, 4: 1.0, 5: 1.0}



def host_prep(inp):
    c = lambda a: np.ascontiguousarray(np.asarray(a), dtype=np.float32)
    out = {}
    _ = inp["L1_ffn_router_w"], inp["L1_ffn_router_b"]

    def per_ptile(wq, wk, wv):
        return np.stack([np.concatenate(
            [wq[:, 128*p:128*p+128], wk[:, 128*p:128*p+128], wv[:, 128*p:128*p+128]],
            axis=1) for p in range(8)])

    Wm = c(inp["L0_ffn_qkv_w"]).T
    out["f_wqkv"] = c(per_ptile(Wm[:, 0:D], Wm[:, D:2*D], Wm[:, 2*D:3*D]))
    bq = c(inp["L0_ffn_qkv_b"])
    bb = np.stack([np.stack([bq[128*p:128*p+128], bq[D+128*p:D+128*p+128],
                             bq[2*D+128*p:2*D+128*p+128]]) for p in range(8)])
    out["f_bqkv"] = c(bb)
    rot = lambda v: np.concatenate([v[32:64], v[0:32], v[96:128], v[64:96]])
    out["f_bqkrot"] = c(np.stack([np.stack([rot(bb[p, 0]), rot(bb[p, 1])])
                                  for p in range(8)]))
    out["f_wout"] = c(c(inp["L0_ffn_out_w"]).T)
    out["f_outb"] = c(inp["L0_ffn_out_b"]).reshape(1, D)
    out["f_n1w"] = c(inp["L0_ffn_norm1_w"]).reshape(1, D)
    out["f_n2w"] = c(inp["L0_ffn_norm2_w"]).reshape(1, D)
    out["f_gateT"] = c(c(inp["L0_ffn_gate_w"]).T)
    out["f_w1T"] = c(np.transpose(c(inp["L0_ffn_w1"]), (0, 2, 1)))
    out["f_w2T"] = c(np.transpose(c(inp["L0_ffn_w2"]), (0, 2, 1)))
    out["f_w3T"] = c(np.transpose(c(inp["L0_ffn_w3"]), (0, 2, 1)))
    out["f_b1T"] = c(np.transpose(c(inp["L0_ffn_b1"]).reshape(8, 16, 128), (0, 2, 1)))
    out["f_b2T"] = c(np.transpose(c(inp["L0_ffn_b2"]).reshape(8, 16, 128), (0, 2, 1)))
    out["f_b3"] = c(inp["L0_ffn_b3"])

    W = c(inp["L0_attn_w_qkv"])
    out["a_w"] = c(per_ptile(W[:, 0:D], W[:, D:2*D], W[:, 2*D:3*D]))
    out["a_wout"] = c(inp["L0_attn_w_out"])
    out["a_gamma"] = c(inp["L0_attn_gamma"]).reshape(1, D)

    out["t_w"] = c(per_ptile(c(inp["L1_attn_Wq"]), c(inp["L1_attn_Wk"]),
                             c(inp["L1_attn_Wv"])))
    out["t_wr"] = c(inp["L1_attn_Wr"])

    w1 = c(inp["L1_ffn_w1"][0])
    out["g_w1"] = c(np.transpose(w1.reshape(D, 32, 128), (1, 0, 2)))
    out["g_b1T"] = c(c(inp["L1_ffn_b1"][0]).reshape(32, 128).T)
    out["g_w2"] = c(inp["L1_ffn_w2"][0])
    out["g_b2"] = c(inp["L1_ffn_b2"][0]).reshape(1, D)
    out["g_gam"] = c(inp["L1_ffn_gamma"]).reshape(1, D)
    out["g_bet"] = c(inp["L1_ffn_beta"]).reshape(1, D)

    out["c_ident"] = np.eye(128, dtype=np.float32)
    out["c_ones_col"] = np.ones((128, 1), dtype=np.float32)
    out["c_ones_row"] = np.ones((1, 128), dtype=np.float32)
    invf = 1.0 / 10000.0 ** (np.arange(0, HD, 2, dtype=np.float64) / HD)
    fr = np.arange(T, dtype=np.float64)[None, :] * invf[:, None]
    c64 = np.concatenate([np.cos(fr), np.cos(fr)], axis=0)
    s64 = np.concatenate([np.sin(fr), np.sin(fr)], axis=0)
    ssgn = np.concatenate([-s64[0:32], s64[32:64]], axis=0)
    out["ropeC"] = c(np.concatenate([c64, c64], axis=0))
    out["ropeS"] = c(np.concatenate([ssgn, ssgn], axis=0))
    aa = np.arange(128)[:, None]
    cc = np.arange(1920)[None, :]
    out["stair"] = c(np.where(aa + 896 <= cc, 1.0, np.exp(-50.0)))
    return out



class Sem:
    def __init__(self, nc, name):
        self.h = nc.alloc_semaphore(name)
        self.n = 0

    def inc(self, k=1):
        self.n += k
        return self.n


class Bld:
    def __init__(self, nc):
        self.nc = nc
        self.ops = {e: [] for e in ENGINES}
        self.WX = Sem(nc, "WX")
        self.WG = Sem(nc, "WG")
        self.WSC = Sem(nc, "WSC")
        self.WS0 = Sem(nc, "WS0")
        self.WS1 = Sem(nc, "WS1")
        self.T = Sem(nc, "T")
        self.V = Sem(nc, "V")
        self.S = Sem(nc, "S")

    def op(self, eng, fn, waits=(), inc=None, dma=False):
        amt = 16 if dma else 1
        dma_sems = (self.WX, self.WG, self.WSC, self.WS0, self.WS1)
        waits = list(waits)
        wd = {}
        for s, v in waits:
            if v and v > 0:
                if s in dma_sems:
                    v = s.n
                wd[s.h] = max(wd.get(s.h, 0), v)
        if dma and inc in dma_sems and inc.n > 0:
            wd[inc.h] = max(wd.get(inc.h, 0), inc.n)
        tgt = inc.inc(amt) if inc is not None else None
        ih = inc.h if inc is not None else None

        def emitter(e, _wd=wd, _fn=fn, _ih=ih, _amt=amt):
            for h, v in _wd.items():
                e.wait_ge(h, v)
            ins = _fn(e)
            if _ih is not None:
                if ins is None:
                    ins = e.nop()
                ins.then_inc(_ih, _amt)
        self.ops[eng].append(emitter)
        return tgt

    def finalize(self):
        with self.nc.Block() as blk:
            for eng in ENGINES:
                def run(e, _l=self.ops[eng]):
                    for f in _l:
                        f(e)
                getattr(blk, eng)(run)



def build_program(subs, debug=None):
    nc = bass.Bass("TRN2", target_bir_lowering=False)
    b = Bld(nc)
    WX, WG, WSC, TT_, V_, S_ = b.WX, b.WG, b.WSC, b.T, b.V, b.S

    need = {SUB_KIND[s] for s in subs}
    dr = {}

    def dram(name, shape):
        dr[name] = nc.dram_tensor(name, list(shape), F32, kind="ExternalInput").ap()

    dram("xin", (T, D))
    xout = nc.dram_tensor("xout", [T, D], F32, kind="ExternalOutput").ap()
    xmid = [nc.dram_tensor(f"xmid{i}", [T, D], F32, kind="Internal").ap()
            for i in range(len(subs) - 1)]
    xs_seq = [dr["xin"]] + xmid + [xout]

    dram("c_ident", (128, 128))
    dram("c_ones_col", (128, 1))
    dram("c_ones_row", (1, 128))
    if "f" in need:
        for n, s in [("f_wqkv", (8, D, 384)), ("f_bqkv", (8, 3, 128)),
                     ("f_bqkrot", (8, 2, 128)), ("f_wout", (D, D)), ("f_outb", (1, D)),
                     ("f_n1w", (1, D)), ("f_n2w", (1, D)), ("f_gateT", (D, 8)),
                     ("f_w1T", (8, D, H0)), ("f_w2T", (8, D, H0)), ("f_w3T", (8, H0, D)),
                     ("f_b1T", (8, 128, 16)), ("f_b2T", (8, 128, 16)), ("f_b3", (8, D)),
                     ("ropeC", (128, T)), ("ropeS", (128, T))]:
            dram(n, s)
    if "a" in need:
        for n, s in [("a_w", (8, D, 384)), ("a_wout", (D, D)), ("a_gamma", (1, D))]:
            dram(n, s)
    if "t" in need:
        for n, s in [("t_w", (8, D, 384)), ("t_wr", (D, D)), ("stair", (128, 1920))]:
            dram(n, s)
    if "g" in need:
        for n, s in [("g_w1", (32, D, 128)), ("g_b1T", (128, 32)), ("g_w2", (F1, D)),
                     ("g_b2", (1, D)), ("g_gam", (1, D)), ("g_bet", (1, D))]:
            dram(n, s)

    sb = nc.alloc_sbuf_tensor
    AZ = sb("AZ", [128, NT * D], F32)
    AXNT = sb("AXNT", [128, NC * T], BF16)
    ABIG = sb("ABIG", [128, 16384], BF16)
    AW = sb("AW", [128, 10368], BF16)
    ATAB = sb("ATAB", [128, 4096], BF16) if ("f" in need or "t" in need) else None
    AF32 = sb("AF32", [128, 4096], F32)
    ASM = sb("ASM", [128, 4096], BF16)
    SROW = sb("SROW", [128, 128], F32)
    SCOMB = sb("SCOMB", [128, 128], F32)
    SMOE = sb("SMOE", [128, 48], F32)
    SBIAS = sb("SBIAS", [128, 64], F32)
    SGATE = sb("SGATE", [128, 64], BF16) if "f" in need else None
    B1L = sb("B1L", [128, 32], F32) if "g" in need else None
    IDENT = sb("IDENT", [128, 128], BF16)
    ONESC = sb("ONESC", [128, 1], BF16)
    ONESR = sb("ONESR", [1, 128], F32)
    ONESRB = sb("ONESRB", [1, 128], BF16)
    ROWT = sb("ROWT", [1, 1024], F32)
    RROW = sb("RROW", [1, 1024], BF16)
    GROW = sb("GROW", [1, 1024], F32)
    GROWB = sb("GROWB", [1, 1024], BF16)
    NRMB = sb("NRMB", [128, 1024], F32)
    NRMB2 = sb("NRMB2", [128, 1024], F32)
    SQD = sb("SQD", [128, 1024], BF16)

    PA = nc.alloc_psum_tensor("PA", [128, 2048], F32)
    PB = nc.alloc_psum_tensor("PB", [128, 1024], F32)
    PC = nc.alloc_psum_tensor("PC", [128, 512], F32)
    PT = nc.alloc_psum_tensor("PT", [128, 1024], BF16)

    EPS5 = SBIAS[:, 0:1]
    EPS6 = SBIAS[:, 1:2]
    EPS24 = SBIAS[:, 2:3]
    CNEG = SBIAS[:, 3:4]
    BQ, BK = SBIAS[:, 4:5], SBIAS[:, 5:6]
    BQR, BKR = SBIAS[:, 6:7], SBIAS[:, 7:8]
    B1C = SBIAS[:, 16:32]
    B2C = SBIAS[:, 32:48]
    XNTI = [ASM[:, 2048:3072], ASM[:, 3072:4096]]
    QT = ASM[:, 0:2048]
    KT = ASM[:, 2048:4096]

    def col128(ap1d):
        return ap1d.rearrange("(q a) -> q a", a=1)

    st = {"bar_t": 0, "bar_v": 0, "bar_s": 0, "bar_g": 0,
          "zdone": 0, "pb_v": 0, "grow_t": 0,
          "ztile_v": {i: 0 for i in range(NT)},
          "xnt_tile_s": {i: 0 for i in range(NT)},
          "xnt_last_s": 0,
          "af32_free_v": 0}

    def bars():
        return [(TT_, st["bar_t"]), (V_, st["bar_v"]), (S_, st["bar_s"])]

    def set_bars():
        st["bar_t"], st["bar_v"], st["bar_s"] = TT_.n, V_.n, S_.n

    b.op("gpsimd", lambda e: e.dma_start(IDENT[:], dr["c_ident"]), inc=WG, dma=True)
    b.op("gpsimd", lambda e: e.dma_start(ONESC[:], dr["c_ones_col"]), inc=WG, dma=True)
    b.op("sync", lambda e: e.dma_start(ONESR[:], dr["c_ones_row"]), inc=WX, dma=True)
    b.op("gpsimd", lambda e: e.dma_start(ONESRB[:], dr["c_ones_row"]), inc=WG, dma=True)
    if "f" in need:
        b.op("gpsimd", lambda e: e.dma_start(
            SGATE[:].rearrange("p (j g) -> p j g", j=NC),
            dr["f_gateT"].rearrange("(j q) g -> q j g", q=128)), inc=WG, dma=True)
    if "g" in need:
        b.op("sync", lambda e: e.dma_start(B1L[:], dr["g_b1T"]), inc=WX, dma=True)
    consts_g = WG.n
    consts_x = WX.n
    cm = b.op("vector", lambda e: e.memset(SBIAS[:, 0:16], 0.0), inc=V_)
    cm = b.op("vector", lambda e: e.memset(EPS5, 1e-5), waits=[(V_, cm)], inc=V_)
    cm = b.op("vector", lambda e: e.memset(EPS6, 1e-6), waits=[(V_, cm)], inc=V_)
    cm = b.op("vector", lambda e: e.memset(EPS24, 1e-24), waits=[(V_, cm)], inc=V_)
    cm = b.op("vector", lambda e: e.memset(SBIAS[:, 8:9], float(np.log(0.1275))),
              waits=[(V_, cm)], inc=V_)
    consts_v = V_.n
    st["bar_v"] = consts_v


    def row_bcast(dst, src, width=D, scale=None, add=None, post_scale=None):
        wx = b.op("sync", lambda e: e.dma_start(GROW[:, 0:width], src),
                  waits=[(TT_, st["grow_t"])] + bars(), inc=WX, dma=True)
        vv = 0
        if scale is not None:
            vv = b.op("vector", lambda e: e.tensor_scalar_mul(
                GROW[:, 0:width], GROW[:, 0:width], scale), waits=[(WX, wx)], inc=V_)
        if add is not None:
            vv = b.op("vector", lambda e: e.tensor_scalar_add(
                GROW[:, 0:width], GROW[:, 0:width], add), waits=[(WX, wx)], inc=V_)
        if post_scale is not None:
            vv = b.op("vector", lambda e: e.tensor_scalar_mul(
                GROW[:, 0:width], GROW[:, 0:width], post_scale),
                waits=[(WX, wx), (V_, vv)], inc=V_)
        tv = b.op("tensor", lambda e: [
            e.matmul(PB[:, 512*q:512*(q+1)], ONESR[:], GROW[0:1, 512*q:512*(q+1)],
                     start=True, stop=True) for q in range(width // 512)][-1],
            waits=[(WX, max(wx, consts_x)), (V_, vv), (V_, st["pb_v"]),
                   (TT_, TT_.n)], inc=TT_)
        st["grow_t"] = tv
        vv = b.op("vector", lambda e: e.tensor_copy(dst[:, 0:width], PB[:, 0:width]),
                  waits=[(TT_, tv), (V_, st["bar_v"])], inc=V_)
        st["pb_v"] = vv
        return vv

    def emit_ln(xin_ap):
        wx = b.op("sync", lambda e: e.dma_start(
            AZ[:].rearrange("p (i d) -> p i d", i=NT),
            xin_ap.rearrange("(i p) d -> p i d", p=128)),
            waits=bars() + [(b.WS0, b.WS0.n), (b.WS1, b.WS1.n)], inc=WX, dma=True)
        sqp = 0
        for i in range(NT):
            b.op("vector", lambda e, i=i: e.reduce_sum(
                SROW[:, i:i+1], AZ[:, D*i:D*(i+1)], axis=AX.X),
                waits=[(WX, wx)] + bars(), inc=V_)
            sqp = b.op("scalar", lambda e, i=i: e.activation(
                SQD[:], AZ[:, D*i:D*(i+1)], AF.Square,
                accum_out=SROW[:, 16+i:17+i]),
                waits=[(WX, wx), (S_, sqp)] + bars(), inc=S_)
        v_st, s_st = V_.n, S_.n
        c1 = b.op("vector", lambda e: e.tensor_scalar_mul(
            SROW[:, 32:48], SROW[:, 0:16], 1.0 / D),
            waits=[(V_, v_st), (S_, s_st)], inc=V_)
        c2 = b.op("vector", lambda e: e.tensor_mul(
            SROW[:, 48:64], SROW[:, 32:48], SROW[:, 32:48]),
            waits=[(V_, c1)], inc=V_)
        vv = b.op("vector", lambda e: e.scalar_tensor_tensor(
            SROW[:, 48:64], SROW[:, 16:32], 1.0 / D, SROW[:, 48:64],
            ALU.mult, ALU.subtract), waits=[(V_, c2)], inc=V_)
        m1 = b.op("scalar", lambda e: e.activation(SROW[:, 96:112], SROW[:, 48:64],
             AF.Ln, bias=EPS5), waits=[(V_, vv), (V_, consts_v)], inc=S_)
        sv = b.op("scalar", lambda e: e.activation(SROW[:, 64:80], SROW[:, 96:112],
                  AF.Exp, scale=-0.5), waits=[(S_, m1)], inc=S_)
        nm = b.op("vector", lambda e: e.tensor_scalar_mul(
            SROW[:, 80:96], SROW[:, 32:48], -1.0), waits=[(V_, c1)], inc=V_)
        for i in range(NT):
            tsm = b.op("vector", lambda e, i=i: e.tensor_scalar(
                AZ[:, D*i:D*(i+1)], AZ[:, D*i:D*(i+1)],
                SROW[:, 80+i:81+i], SROW[:, 64+i:65+i],
                ALU.add, ALU.mult), waits=[(S_, sv), (V_, nm)], inc=V_)
            st["ztile_v"][i] = tsm
        st["zdone"] = V_.n
        set_bars()

    def emit_finale(xin_ap, xo_ap, alpha, fuse_ln=False):
        fin_v = {}
        ws_after = {}
        WS = [b.WS0, b.WS1]
        sqp = 0
        afv = st["af32_free_v"]
        for i in range(NT):
            par = i % 2
            xld = AF32[:, 2048+1024*par:2048+1024*(par+1)]
            w_extra = [(b.WS0, b.WS0.n), (b.WS1, b.WS1.n)] if i < 2 else []
            wx = b.op("sync", lambda e, i=i, xld=xld: e.dma_start(
                xld, xin_ap[128*i:128*(i+1), :]),
                waits=[(V_, afv), (V_, fin_v.get(i-2, 0))] + w_extra,
                inc=WX, dma=True)
            vv = b.op("vector", lambda e, i=i, xld=xld: e.scalar_tensor_tensor(
                AZ[:, D*i:D*(i+1)], AZ[:, D*i:D*(i+1)], alpha, xld,
                ALU.mult, ALU.add),
                waits=[(WX, wx), (WS[par], WS[par].n),
                       (V_, st["ztile_v"][i])], inc=V_)
            fin_v[i] = vv
            ws_after[i] = b.op("sync", lambda e, i=i: e.dma_start(
                xo_ap[128*i:128*(i+1), :], AZ[:, D*i:D*(i+1)]), waits=[(V_, vv)],
                inc=WS[par], dma=True)
            if fuse_ln:
                b.op("vector", lambda e, i=i: e.reduce_sum(
                    SROW[:, i:i+1], AZ[:, D*i:D*(i+1)], axis=AX.X),
                    waits=[(V_, vv)], inc=V_)
                sqp = b.op("scalar", lambda e, i=i: e.activation(
                    SQD[:], AZ[:, D*i:D*(i+1)], AF.Square,
                    accum_out=SROW[:, 16+i:17+i]),
                    waits=[(V_, vv), (S_, sqp)], inc=S_)
        st["bar_v"] = V_.n
        st["af32_free_v"] = V_.n
        if not fuse_ln:
            st["zdone"] = 0
            return 0
        v_st, s_st = V_.n, S_.n
        c1 = b.op("vector", lambda e: e.tensor_scalar_mul(
            SROW[:, 32:48], SROW[:, 0:16], 1.0 / D),
            waits=[(V_, v_st), (S_, s_st)], inc=V_)
        c2 = b.op("vector", lambda e: e.tensor_mul(
            SROW[:, 48:64], SROW[:, 32:48], SROW[:, 32:48]),
            waits=[(V_, c1)], inc=V_)
        vv = b.op("vector", lambda e: e.scalar_tensor_tensor(
            SROW[:, 48:64], SROW[:, 16:32], 1.0 / D, SROW[:, 48:64],
            ALU.mult, ALU.subtract), waits=[(V_, c2)], inc=V_)
        m1 = b.op("scalar", lambda e: e.activation(SROW[:, 96:112], SROW[:, 48:64],
             AF.Ln, bias=EPS5), waits=[(V_, vv), (V_, consts_v)], inc=S_)
        sv = b.op("scalar", lambda e: e.activation(SROW[:, 64:80], SROW[:, 96:112],
                  AF.Exp, scale=-0.5), waits=[(S_, m1)], inc=S_)
        nm = b.op("vector", lambda e: e.tensor_scalar_mul(
            SROW[:, 80:96], SROW[:, 32:48], -1.0), waits=[(V_, c1)], inc=V_)
        for i in range(NT):
            par = i % 2
            tsm = b.op("vector", lambda e, i=i: e.tensor_scalar(
                AZ[:, D*i:D*(i+1)], AZ[:, D*i:D*(i+1)],
                SROW[:, 80+i:81+i], SROW[:, 64+i:65+i],
                ALU.add, ALU.mult),
                waits=[(S_, sv), (V_, nm), (WS[par], ws_after[i])], inc=V_)
            st["ztile_v"][i] = tsm
        st["zdone"] = V_.n
        set_bars()
        return 0

    def emit_to_xnt(mode, nrm_v=0, nrmb=None):
        if nrmb is None:
            nrmb = NRMB
        zt = st["ztile_v"]
        sv = 0
        if mode != "none":
            sqp = 0
            for i in range(NT):
                sqp = b.op("scalar", lambda e, i=i: e.activation(
                    SQD[:], AZ[:, D*i:D*(i+1)], AF.Square,
                    accum_out=SROW[:, i:i+1]),
                    waits=[(V_, zt[i]), (S_, sqp)], inc=S_)
            s_st = S_.n
            if mode == "rms_w":
                vv = b.op("vector", lambda e: e.tensor_scalar_mul(
                    SROW[:, 16:32], SROW[:, 0:16], 1.0 / D), waits=[(S_, s_st)], inc=V_)
                m1 = b.op("scalar", lambda e: e.activation(SROW[:, 96:112], SROW[:, 16:32],
                     AF.Ln, bias=EPS6), waits=[(V_, vv), (V_, consts_v)], inc=S_)
                sv = b.op("scalar", lambda e: e.activation(
                    SROW[:, 32:48], SROW[:, 96:112], AF.Exp, scale=-0.5),
                    waits=[(S_, m1)], inc=S_)
            else:
                vv = b.op("vector", lambda e: e.tensor_scalar(
                    SROW[:, 16:32], SROW[:, 0:16], EPS24, None, ALU.max),
                    waits=[(S_, s_st), (V_, consts_v)], inc=V_)
                m1 = b.op("scalar", lambda e: e.activation(SROW[:, 96:112], SROW[:, 16:32],
                     AF.Ln), waits=[(V_, vv)], inc=S_)
                sv = b.op("scalar", lambda e: e.activation(SROW[:, 32:48], SROW[:, 96:112],
                     AF.Exp, scale=-0.5), waits=[(S_, m1)], inc=S_)
        prev_evac = {}
        last_prev = st["xnt_last_s"]
        for i in range(NT):
            par = i % 2
            xt = XNTI[par]
            guard = prev_evac.get(i-2, last_prev)
            if mode == "none":
                vv = b.op("vector", lambda e, xt=xt, i=i: e.tensor_copy(
                    xt, AZ[:, D*i:D*(i+1)]),
                    waits=[(V_, zt[i]), (V_, guard)], inc=V_)
            else:
                vv = b.op("vector", lambda e, i=i, xt=xt, nrmb=nrmb:
                          e.scalar_tensor_tensor(
                    xt, AZ[:, D*i:D*(i+1)], SROW[:, 32+i:33+i], nrmb[:],
                    ALU.mult, ALU.mult),
                    waits=[(S_, sv), (V_, nrm_v), (V_, zt[i]), (V_, guard)], inc=V_)
            tv = b.op("tensor", lambda e, xt=xt: [
                e.transpose(PT[:, 128*j:128*(j+1)], xt[:, 128*j:128*(j+1)], IDENT[:])
                for j in range(8)][-1],
                waits=[(V_, vv), (V_, prev_evac.get(i-1, last_prev)),
                       (WG, consts_g)], inc=TT_)
            prev_evac[i] = b.op("vector", lambda e, i=i: e.tensor_copy(
                AXNT[:].rearrange("p (j c) -> p j c", j=NC)[:, :, 128*i:128*(i+1)],
                PT[:].rearrange("p (j c) -> p j c", j=NC)),
                waits=[(TT_, tv)], inc=V_)
            st["xnt_tile_s"][i] = prev_evac[i]
        st["xnt_s"] = prev_evac[NT-1]
        st["xnt_last_s"] = prev_evac[NT-1]
        set_bars()

    def emit_attention(kind):
        wname = {"f": "f_wqkv", "a": "a_w", "t": "t_w"}[kind]
        woname = {"f": "f_wout", "a": "a_wout", "t": "t_wr"}[kind]
        xnt_ts = dict(st["xnt_tile_s"])
        causal = kind == "t"
        rope = kind == "f"
        VAUG = AW[:, 8192:10272]
        VAUG4 = VAUG.rearrange("p (k h c) -> p k h c", k=NT, c=65)
        EB = [AW[:, 6144:7168], AW[:, 7168:8192]]

        st["pend_evac"] = None
        ones_v = b.op("vector", lambda e: e.memset(
            VAUG.rearrange("p (h c) -> p h c", c=65)[:, :, 64:65], 1.0),
            waits=bars(), inc=V_)

        def flush_evac():
            nonlocal b_evac_v, rb_v, ot_v
            pe_ = st.get("pend_evac")
            if pe_ is None:
                return
            pp_, hl_, qh_, dm, as0, as1 = pe_
            st["pend_evac"] = None
            base_ = 64 * hl_
            tvs = []
            for q in range(2):
                tvr = b.op("tensor", lambda e, q=q: e.matmul(
                    PC[0:64, 0:512], ONESRB[0:1, 0:64],
                    RROW[0:1, 512*q:512*(q+1)], start=True, stop=True),
                    waits=[(S_, dm), (V_, rb_v), (WG, consts_g),
                           (V_, st.get("csqm_l", 0))], inc=TT_)
                rb_v = b.op("vector", lambda e, q=q: e.tensor_copy(
                    AF32[0:64, 512*q:512*(q+1)], PC[0:64, 0:512]),
                    waits=[(TT_, tvr)], inc=V_)
            st["rb_t"] = TT_.n
            if debug == "tagot":
                b_evac_v = b.op("vector",
                                lambda e, pp_=pp_, base_=base_, qh_=qh_: e.memset(
                    ABIG[:, 2048*pp_:2048*(pp_+1)][base_:base_+64,
                                                   1024*qh_:1024*(qh_+1)],
                    float(100*pp_ + 10*(base_//64) + qh_ + 1)),
                    waits=[(V_, rb_v), (TT_, csq_t)], inc=V_)
            else:
                b_evac_v = b.op("vector",
                                lambda e, pp_=pp_, base_=base_, qh_=qh_: e.tensor_mul(
                    ABIG[:, 2048*pp_:2048*(pp_+1)][base_:base_+64,
                                                   1024*qh_:1024*(qh_+1)],
                    PB[0:64, 0:1024], AF32[0:64, 0:1024]),
                    waits=[(V_, rb_v), (TT_, csq_t)], inc=V_)
            ot_v = b_evac_v
        tab_g = st.get("tab_g", 0)
        if rope and st.get("tab") != "rope":
            tab_g = b.op("gpsimd", lambda e: e.dma_start(
                ATAB[:, 0:2048], dr["ropeC"]), waits=bars(), inc=WG, dma=True)
            tab_g = b.op("gpsimd", lambda e: e.dma_start(
                ATAB[:, 2048:4096], dr["ropeS"]), inc=WG, dma=True)
            st["tab"] = "rope"
            st["tab_g"] = tab_g
        if causal and st.get("tab") != "stair":
            tab_g = b.op("gpsimd", lambda e: e.dma_start(
                ATAB[:, 0:1920], dr["stair"]), waits=bars(), inc=WG, dma=True)
            st["tab"] = "stair"
            st["tab_g"] = tab_g
        wslab_t = {0: st["bar_t"], 1: st["bar_t"]}
        vaug_t = 0
        e_t = {0: 0, 1: 0}
        b_evac_v = 0
        rb_v = 0
        cneg_s = 0
        asm_t = st["bar_t"]
        bias_v = 0
        vb_t = 0
        csqm_v = 0
        pa_s = 0
        ot_v = 0
        zv0 = st["zdone"]

        for p in range(8):
            par = p % 2
            WSL = AW[:, 3072*par:3072*(par+1)]
            WSLv = WSL.rearrange("p (j f) -> p j f", j=NC)
            wg = b.op("gpsimd", lambda e, p=p, WSL=WSL: e.dma_start(
                WSL.rearrange("p (j f) -> p j f", j=NC),
                dr[wname][p].rearrange("(j q) f -> q j f", q=128)),
                waits=[(TT_, wslab_t[par])], inc=WG, dma=True)
            wb = 0
            if rope:
                for (dst, src) in ((BQ, dr["f_bqkv"][p, 0]), (BK, dr["f_bqkv"][p, 1]),
                                   (BQR, dr["f_bqkrot"][p, 0]),
                                   (BKR, dr["f_bqkrot"][p, 1])):
                    wb = b.op("sync", lambda e, dst=dst, src=src: e.dma_start(
                        dst, col128(src)), waits=[(V_, bias_v)], inc=WX, dma=True)
            qk_v = 0
            sq_s = {}
            DUMP = ABIG[:, 2048*p:2048*(p+1)]

            def rope_or_copy(DST, bcol, bcolr, sqm, tv):
                if rope:
                    b.op("vector", lambda e, bcol=bcol: e.scalar_tensor_tensor(
                        AF32[:, 0:2048], PA[:], bcol, ATAB[:, 0:2048],
                        ALU.add, ALU.mult),
                        waits=[(TT_, tv), (WX, wb), (TT_, asm_t),
                               (V_, b_evac_v), (WG, tab_g), (S_, sqm)], inc=V_)
                    r2 = 0
                    for (od, sd) in ((0, 32), (32, 0), (64, 96), (96, 64)):
                        r2 = b.op("vector", lambda e, od=od, sd=sd, bcolr=bcolr:
                             e.scalar_tensor_tensor(
                                 AF32[od:od+32, 2048:4096], PA[sd:sd+32, :],
                                 bcolr[od:od+32, :], ATAB[od:od+32, 2048:4096],
                                 ALU.add, ALU.mult), inc=V_)
                    return b.op("vector", lambda e, DST=DST: e.tensor_add(
                        DST, AF32[:, 0:2048], AF32[:, 2048:4096]),
                        waits=[(V_, r2)], inc=V_)
                return b.op("vector", lambda e, DST=DST: e.tensor_copy(DST, PA[:]),
                            waits=[(TT_, tv), (TT_, asm_t), (V_, b_evac_v),
                                   (S_, sqm)], inc=V_)

            def emit_v_half(h0, h1, gv):
                for tt in range(h0, h1):
                    ph = tt % 2
                    tv = b.op("tensor", lambda e, tt=tt, ph=ph, WSLv=WSLv,
                              last=(kind != "f"): [
                        e.matmul(PB[:, 512*ph:512*ph+128],
                                 AXNT[:, T*j+128*tt:T*j+128*(tt+1)],
                                 WSLv[:, j, 256:384],
                                 start=(j == 0), stop=(last and j == NC-1))
                        for j in range(NC)][-1],
                        waits=[(S_, vev.get(tt-2, 0)), (WG, wg),
                               (V_, xnt_ts[tt]),
                               (V_, ones_v), (TT_, vaug_t), (V_, st["pb_v"]),
                               (V_, b_evac_v)], inc=TT_)
                    if kind == "f":
                        tv = b.op("tensor", lambda e, ph=ph: e.matmul(
                            PB[:, 512*ph:512*ph+128], ONESRB[:], GROWB[0:1, 0:128],
                            start=False, stop=True), waits=[(WG, gv)], inc=TT_)
                    vev[tt] = b.op("scalar", lambda e, tt=tt, ph=ph: e.copy(
                        VAUG4[:, tt, :, 0:64],
                        PB[:, 512*ph:512*ph+128].rearrange("p (h c) -> p h c", h=2)),
                        waits=[(TT_, tv), (V_, ones_v)], inc=S_)

            def emit_csq(which, sqm):
                cm = 0
                for q in range(4):
                    tv2 = b.op("tensor", lambda e, q=q, DUMP=DUMP: e.matmul(
                        PC[0:1, 0:512], ONESC[:], DUMP[:, 512*q:512*(q+1)],
                        start=True, stop=True),
                        waits=[(S_, sqm), (V_, st.get("csqm_l", 0)), (V_, rb_v),
                               (WG, consts_g)], inc=TT_)
                    cm = b.op("vector", lambda e, which=which, q=q: e.reduce_max(
                        SROW[0:1, 96+4*which+q:97+4*which+q], PC[0:1, 0:512],
                        axis=AX.X), waits=[(TT_, tv2)], inc=V_)
                    st["csqm_l"] = cm
                return cm

            gv = 0
            if kind == "f":
                gv = b.op("gpsimd", lambda e, p=p: e.dma_start(
                    GROWB[0:1, 0:128],
                    dr["f_bqkv"][p, 2].rearrange("(a f) -> a f", a=1)),
                    waits=[(TT_, vb_t)], inc=WG, dma=True)
            vev = {}
            for q in range(4):
                tvq = b.op("tensor", lambda e, WSLv=WSLv, q=q: [
                    e.matmul(PA[:, 512*q:512*(q+1)], WSLv[:, j, 0:128],
                             AXNT[:, T*j+512*q:T*j+512*(q+1)],
                             start=(j == 0), stop=(j == NC-1))
                    for j in range(NC)][-1],
                    waits=[(WG, wg), (V_, xnt_ts[4*q+3]), (TT_, asm_t),
                           (S_, pa_s), (V_, rb_v)], inc=TT_)
            flush_evac()
            sq_s[0] = b.op(
                "scalar",
                (lambda e, DUMP=DUMP: e.activation(DUMP, PA[:], AF.Square, bias=BQ))
                if rope else
                (lambda e, DUMP=DUMP: e.activation(DUMP, PA[:], AF.Square)),
                waits=[(TT_, tvq), (WX, wb), (S_, cneg_s), (V_, ot_v),
                       (TT_, st["bar_t"]), (TT_, st.get("sqd_t", 0))], inc=S_)
            qk_v = rope_or_copy(QT, BQ, BQR, sq_s[0], tvq)
            bias_v = qk_v
            emit_v_half(0, 8, gv)
            cmq = emit_csq(0, sq_s[0])
            for q in range(4):
                tvk = b.op("tensor", lambda e, WSLv=WSLv, q=q: [
                    e.matmul(PA[:, 512*q:512*(q+1)], WSLv[:, j, 128:256],
                             AXNT[:, T*j+512*q:T*j+512*(q+1)],
                             start=(j == 0), stop=(j == NC-1))
                    for j in range(NC)][-1],
                    waits=[(WG, wg), (V_, xnt_ts[4*q+3]), (S_, sq_s[0]),
                           (V_, qk_v)], inc=TT_)
            sq_s[1] = b.op("scalar", (lambda e, DUMP=DUMP: e.activation(
                DUMP, PA[:], AF.Square, bias=BK)) if rope else
                (lambda e, DUMP=DUMP: e.activation(DUMP, PA[:], AF.Square)),
                waits=[(TT_, tvk), (WX, wb), (S_, sq_s[0]), (V_, cmq)], inc=S_)
            qk_v = rope_or_copy(KT, BK, BKR, sq_s[1], tvk)
            bias_v = qk_v
            emit_v_half(8, NT, gv)
            emit_csq(1, sq_s[1])
            v_ready = vev[NT-1]
            if kind == "f":
                vb_t = TT_.n
            csq_t = TT_.n
            st["sqd_t"] = TT_.n
            if debug == "qt" and p == 0:
                last = b.op("gpsimd", lambda e: e.dma_start(
                    xout.rearrange("(a p) d -> p a d", p=128)[:, 0:4, :],
                    ASM[:, 0:4096].rearrange("p (a d) -> p a d", a=4)),
                    waits=[(V_, qk_v)], inc=WG, dma=True)
                b.op("sync", lambda e: e.wait_ge(b.WG.h, last))
                b.finalize()
                return "OT_DUMP"
            vv = b.op("vector", lambda e: e.reduce_max(
                SROW[0:1, 104:105], SROW[0:1, 96:100], axis=AX.X),
                waits=[(V_, st.get("csqm_l", 0))], inc=V_)
            vv = b.op("vector", lambda e: e.reduce_max(
                SROW[0:1, 105:106], SROW[0:1, 100:104], axis=AX.X),
                waits=[(V_, st.get("csqm_l", 0))], inc=V_)
            b.op("scalar", lambda e: e.activation(SROW[0:1, 106:107],
                 SROW[0:1, 104:105], AF.Ln), waits=[(V_, vv)], inc=S_)
            sv = b.op("scalar", lambda e: e.activation(SROW[0:1, 107:108],
                      SROW[0:1, 105:106], AF.Ln), inc=S_)
            vv = b.op("vector", lambda e: e.tensor_add(
                SROW[0:1, 106:107], SROW[0:1, 106:107], SROW[0:1, 107:108]),
                waits=[(S_, sv)], inc=V_)
            sv = b.op("scalar", lambda e: e.activation(SROW[0:1, 108:109],
                      SROW[0:1, 106:107], AF.Exp, scale=0.5,
                      bias=SBIAS[0:1, 8:9]), waits=[(V_, vv)], inc=S_)
            tv = b.op("tensor", lambda e: e.matmul(
                PC[:, 0:1], ONESR[:], SROW[0:1, 108:109], start=True, stop=True),
                waits=[(S_, sv), (V_, st.get("csqm_l", 0))], inc=TT_)
            cneg_v = b.op("vector", lambda e: e.tensor_scalar_mul(
                CNEG, PC[:, 0:1], -1.0), waits=[(TT_, tv), (S_, cneg_s)], inc=V_)
            wslab_t[par] = TT_.n

            for hl in range(2):
                base = 64 * hl
                for qh in range(2):
                    kts = [kt for kt in range(NT)
                           if not (causal and 128*kt > 1024*qh + 1023)]
                    a_s = {0: pa_s, 1: pa_s}
                    pend_av = None
                    first = True

                    def emit_av(pend, last):
                        nonlocal vaug_t
                        kt, ap_, E, av_w, fst = pend
                        tv = b.op("tensor", lambda e, kt=kt, hl=hl, E=E,
                                  fst=fst, last=last: [
                            e.matmul(PB[0:65, 512*q:512*(q+1)],
                                     VAUG[:, 130*kt+65*hl:130*kt+65*hl+65],
                                     E[:, 512*q:512*(q+1)],
                                     start=fst, stop=last) for q in range(2)][-1],
                            waits=[av_w, (S_, v_ready), (V_, b_evac_v)], inc=TT_)
                        vaug_t = tv
                        e_t[ap_] = tv

                    for idx, kt in enumerate(kts):
                        ap_ = idx % 2
                        A = PA[:, 1024*ap_:1024*(ap_+1)]
                        E = EB[ap_]
                        tv = b.op("tensor",
                                  lambda e, kt=kt, qh=qh, base=base, A=A: [
                            e.matmul(A[:, 512*q:512*(q+1)],
                                     KT[base:base+64, 128*kt:128*(kt+1)],
                                     QT[base:base+64,
                                        1024*qh+512*q:1024*qh+512*(q+1)],
                                     start=True, stop=True) for q in range(2)][-1],
                            waits=[(S_, a_s[ap_]), (V_, qk_v)], inc=TT_)
                        asm_t = tv
                        sv = b.op("scalar", lambda e, A=A, E=E: e.activation(
                            E, A, AF.Exp, bias=CNEG, scale=0.125),
                            waits=[(TT_, tv), (V_, cneg_v), (TT_, e_t[ap_]),
                                   (TT_, st.get("rb_t", 0))], inc=S_)
                        a_s[ap_] = sv
                        cneg_s = sv
                        pa_s = sv
                        av_w = (S_, sv)
                        if causal:
                            s_ = 128*kt - 1024*qh
                            if 0 <= s_ <= 896:
                                mv = b.op("vector", lambda e, E=E, s_=s_: e.tensor_mul(
                                    E, E, ATAB[:, 896-s_:1920-s_]),
                                    waits=[(S_, sv), (WG, tab_g)], inc=V_)
                                av_w = (V_, mv)
                        if idx == 1 and st.get("pend_evac") is not None:
                            flush_evac()
                        if pend_av is not None:
                            emit_av(pend_av, False)
                        pend_av = (kt, ap_, E, av_w, first)
                        first = False
                    emit_av(pend_av, True)
                    if debug == "att1" and p == 0 and hl == 0 and qh == 0:
                        cv = b.op("vector", lambda e: e.tensor_copy(
                            AF32[0:65, 0:1024], PB[0:65, 0:1024]),
                            waits=[(TT_, vaug_t)], inc=V_)
                        last = b.op("sync", lambda e: e.dma_start(
                            xout[0:65, 0:1024], AF32[0:65, 0:1024]),
                            waits=[(V_, cv)], inc=WX, dma=True)
                        b.op("sync", lambda e: e.wait_ge(b.WX.h, last))
                        b.finalize()
                        return "OT_DUMP"
                    m1 = b.op("scalar", lambda e: e.activation(
                        ROWT[:, 0:1024], PB[64:65, 0:1024], AF.Ln),
                        waits=[(TT_, vaug_t)], inc=S_)
                    dm = b.op("scalar", lambda e: e.activation(
                        RROW[:, 0:1024], ROWT[:, 0:1024], AF.Exp, scale=-1.0),
                        waits=[(S_, m1), (TT_, st.get("rb_t", 0))], inc=S_)
                    st["pend_evac"] = (p, hl, qh, dm, a_s[0], a_s[1])
        flush_evac()
        if rope:
            st["af32_free_v"] = V_.n

        if debug in ("ot", "tagot", "tagb"):
            ot_all = ot_v
            last = b.op("gpsimd", lambda e: e.dma_start(
                xout.rearrange("(a p) d -> p a d", p=128),
                ABIG[:].rearrange("p (a d) -> p a d", a=16)),
                waits=[(V_, ot_all)] + [(TT_, TT_.n), (S_, S_.n)], inc=WG, dma=True)
            b.op("sync", lambda e: e.wait_ge(b.WG.h, last))
            b.finalize()
            return "OT_DUMP"
        wgo = b.op("gpsimd", lambda e: e.dma_start(
            AW[:, 0:8192].rearrange("p (j f) -> p j f", j=NC),
            dr[woname].rearrange("(j q) f -> q j f", q=128)),
            waits=[(TT_, asm_t), (TT_, wslab_t[0]), (TT_, wslab_t[1]),
                   (TT_, TT_.n)], inc=WG, dma=True)
        gv = 0
        if kind == "f":
            gv = b.op("gpsimd", lambda e: e.dma_start(GROWB[0:1, 0:1024], dr["f_outb"]),
                      waits=[(TT_, vb_t), (TT_, st.get("growb_t", 0))], inc=WG, dma=True)
        opev = {}
        for tt in range(NT):
            ph = tt % 2
            PAH = PA[:, 1024*ph:1024*(ph+1)]
            tv = b.op("tensor", lambda e, tt=tt, PAH=PAH, isf=(kind == "f"): [
                e.matmul(PAH[:, 512*q:512*(q+1)],
                         ABIG[:, 2048*j+128*tt:2048*j+128*(tt+1)],
                         AW[:, 1024*j+512*q:1024*j+512*(q+1)],
                         start=(j == 0), stop=(not isf) and (j == NC-1))
                for j in range(NC) for q in range(2)][-1],
                waits=[(WG, wgo), (V_, ot_v), (V_, opev.get(tt-2, 0)),
                       (S_, pa_s), (V_, rb_v), (V_, b_evac_v)], inc=TT_)
            if kind == "f":
                tv = b.op("tensor", lambda e, PAH=PAH: [
                    e.matmul(PAH[:, 512*q:512*(q+1)], ONESRB[:],
                             GROWB[0:1, 512*q:512*(q+1)], start=False, stop=True)
                    for q in range(2)][-1], waits=[(WG, gv)], inc=TT_)
                st["growb_t"] = tv
            if kind == "f":
                opev[tt] = b.op("vector", lambda e, tt=tt, PAH=PAH: e.tensor_add(
                    AZ[:, D*tt:D*(tt+1)], AZ[:, D*tt:D*(tt+1)], PAH),
                    waits=[(TT_, tv), (V_, zv0)], inc=V_)
            else:
                opev[tt] = b.op("vector", lambda e, tt=tt, PAH=PAH: e.tensor_copy(
                    AZ[:, D*tt:D*(tt+1)], PAH), waits=[(TT_, tv), (V_, zv0)], inc=V_)
            st["ztile_v"][tt] = opev[tt]
        st["zdone"] = V_.n
        set_bars()

    def emit_moe():
        xnt_ts = dict(st["xnt_tile_s"])
        zv0 = st["zdone"]
        gl_v = 0
        for tt in range(NT):
            tv = b.op("tensor", lambda e, tt=tt: [
                e.matmul(PC[:, 8*tt:8*(tt+1)],
                         AXNT[:, T*j+128*tt:T*j+128*(tt+1)], SGATE[:, 8*j:8*(j+1)],
                         start=(j == 0), stop=(j == NC-1)) for j in range(NC)][-1],
                waits=[(V_, xnt_ts[tt]), (WG, consts_g), (TT_, st["bar_t"]),
                       (V_, gl_v)], inc=TT_)
            gl_v = b.op("vector", lambda e, tt=tt: e.tensor_copy(
                SCOMB[:, 8*tt:8*(tt+1)], PC[:, 8*tt:8*(tt+1)]),
                waits=[(TT_, tv), (V_, st["bar_v"])], inc=V_)
        comb_v = 0
        for tt in range(NT):
            LG = SCOMB[:, 8*tt:8*(tt+1)]
            m = b.op("vector", lambda e, LG=LG: e.reduce_max(
                SMOE[:, 0:1], LG, axis=AX.X), waits=[(V_, gl_v), (S_, st["bar_s"])],
                inc=V_)
            m = b.op("vector", lambda e, LG=LG: e.tensor_scalar(
                SMOE[:, 8:16], LG, SMOE[:, 0:1], None, ALU.is_ge),
                waits=[(V_, m)], inc=V_)
            m = b.op("vector", lambda e, LG=LG: e.scalar_tensor_tensor(
                SMOE[:, 16:24], SMOE[:, 8:16], -1e30, LG, ALU.mult, ALU.add),
                waits=[(V_, m)], inc=V_)
            m = b.op("vector", lambda e: e.reduce_max(
                SMOE[:, 1:2], SMOE[:, 16:24], axis=AX.X), waits=[(V_, m)], inc=V_)
            m2 = b.op("vector", lambda e: e.tensor_scalar(
                SMOE[:, 24:32], SMOE[:, 16:24], SMOE[:, 1:2], None, ALU.is_ge),
                waits=[(V_, m)], inc=V_)
            vv = b.op("vector", lambda e: e.tensor_sub(
                SMOE[:, 1:2], SMOE[:, 1:2], SMOE[:, 0:1]), waits=[(V_, m2)], inc=V_)
            sv = b.op("scalar", lambda e: e.activation(
                SMOE[:, 2:3], SMOE[:, 1:2], AF.Exp), waits=[(V_, vv)], inc=S_)
            m = b.op("vector", lambda e: e.tensor_scalar_add(
                SMOE[:, 3:4], SMOE[:, 2:3], 1.0), waits=[(S_, sv)], inc=V_)
            m = b.op("vector", lambda e: e.reciprocal(SMOE[:, 4:5], SMOE[:, 3:4]),
                waits=[(V_, m)], inc=V_)
            m = b.op("vector", lambda e: e.tensor_mul(
                SMOE[:, 5:6], SMOE[:, 2:3], SMOE[:, 4:5]), waits=[(V_, m)], inc=V_)
            m = b.op("vector", lambda e: e.tensor_scalar_mul(
                SMOE[:, 24:32], SMOE[:, 24:32], SMOE[:, 5:6]), waits=[(V_, m)], inc=V_)
            comb_v = b.op("vector", lambda e, LG=LG: e.scalar_tensor_tensor(
                LG, SMOE[:, 8:16], SMOE[:, 4:5], SMOE[:, 24:32],
                ALU.mult, ALU.add), waits=[(V_, m)], inc=V_)
        w12_t = {0: st["bar_t"], 1: st["bar_t"]}
        w3_t = {0: st["bar_t"], 1: st["bar_t"]}
        h_v = 0
        h_t = st["bar_t"]
        sil_s = {0: 0, 1: 0}
        ab_s = {0: st["bar_s"], 1: st["bar_s"]}
        ab_v = 0
        y_t = {0: 0, 1: 0}
        y_v = {0: 0, 1: 0}
        bias_x = 0
        W1C = [AW[:, 0:2048].rearrange("p (j h) -> p j h", j=NC),
               AW[:, 2048:4096].rearrange("p (j h) -> p j h", j=NC)]
        W2C = [AW[:, 4096:6144].rearrange("p (j h) -> p j h", j=NC),
               AW[:, 6144:8192].rearrange("p (j h) -> p j h", j=NC)]
        W3Q = [ABIG[:, 8192:12288].rearrange("p (k d) -> p k d", k=4),
               ABIG[:, 12288:16384].rearrange("p (k d) -> p k d", k=4)]
        aw_t0 = st["bar_t"]
        for ee in range(NE):
            bx = b.op("sync", lambda e, ee=ee: e.dma_start(B1C, dr["f_b1T"][ee]),
                      waits=[(S_, ab_s[0]), (S_, ab_s[1]), (V_, ab_v)], inc=WX,
                      dma=True)
            bx = b.op("sync", lambda e, ee=ee: e.dma_start(B2C, dr["f_b2T"][ee]),
                      inc=WX, dma=True)
            gv = b.op("gpsimd", lambda e, ee=ee: e.dma_start(
                GROWB[0:1, 0:1024], dr["f_b3"][ee].rearrange("(a f) -> a f", a=1)),
                waits=[(TT_, st.get("growb_t", 0))], inc=WG, dma=True)
            for hq in range(4):
                wg3 = b.op("gpsimd", lambda e, ee=ee, hq=hq: e.dma_start(
                    W3Q[hq % 2], dr["f_w3T"][ee][512*hq:512*(hq+1), :]
                    .rearrange("(k q) d -> q k d", q=128)),
                    waits=[(TT_, w3_t[hq % 2]), (S_, st["bar_s"])], inc=WG, dma=True)
                for hc in range(2):
                    cpar = (2*hq + hc) % 2
                    wg1 = b.op("gpsimd", lambda e, ee=ee, hq=hq, hc=hc, cpar=cpar:
                               e.dma_start(
                        W1C[cpar], dr["f_w1T"][ee][:, 512*hq+256*hc:512*hq+256*(hc+1)]
                        .rearrange("(j q) h -> q j h", q=128)),
                        waits=[(TT_, w12_t[cpar]), (TT_, aw_t0)], inc=WG, dma=True)
                    wg2 = b.op("gpsimd", lambda e, ee=ee, hq=hq, hc=hc, cpar=cpar:
                               e.dma_start(
                        W2C[cpar], dr["f_w2T"][ee][:, 512*hq+256*hc:512*hq+256*(hc+1)]
                        .rearrange("(j q) h -> q j h", q=128)), inc=WG, dma=True)
                    for ht2 in range(2):
                        ht = 2*hc + ht2
                        bcol1 = B1C[:, 4*hq+ht:4*hq+ht+1]
                        bcol2 = B2C[:, 4*hq+ht:4*hq+ht+1]
                        for th in range(2):
                            ph = (2*ht + th) % 2
                            A = PA[:, 1024*ph:1024*(ph+1)]
                            SSI = ASM[:, 1024*ph:1024*(ph+1)]
                            ta = b.op("tensor", lambda e, ht2=ht2, th=th, A=A,
                                      cpar=cpar, wg1=wg1: [
                                e.matmul(A[:, 512*q:512*(q+1)],
                                         W1C[cpar][:, j, 128*ht2:128*(ht2+1)],
                                         AXNT[:, T*j+1024*th+512*q:T*j+1024*th+512*(q+1)],
                                         start=(j == 0), stop=(j == NC-1))
                                for q in range(2) for j in range(NC)][-1],
                                waits=[(WG, wg1), (V_, xnt_ts[8*th+7]),
                                       (S_, ab_s[ph]), (V_, y_v[ph])], inc=TT_)
                            tb = b.op("tensor", lambda e, ht2=ht2, th=th,
                                      cpar=cpar, wg2=wg2: [
                                e.matmul(PB[:, 512*q:512*(q+1)],
                                         W2C[cpar][:, j, 128*ht2:128*(ht2+1)],
                                         AXNT[:, T*j+1024*th+512*q:T*j+1024*th+512*(q+1)],
                                         start=(j == 0), stop=(j == NC-1))
                                for q in range(2) for j in range(NC)][-1],
                                waits=[(WG, wg2), (V_, ab_v), (V_, st["pb_v"])], inc=TT_)
                            w12_t[cpar] = tb
                            sv = b.op("scalar", lambda e, A=A, SSI=SSI, bcol1=bcol1:
                                      e.activation(SSI, A, AF.Silu, bias=bcol1),
                                      waits=[(TT_, ta), (WX, bx), (V_, h_v)], inc=S_)
                            ab_s[ph] = sv
                            h_v = b.op("vector", lambda e, ht=ht, th=th, SSI=SSI,
                                       bcol2=bcol2: e.scalar_tensor_tensor(
                                ABIG[:, 2048*ht+1024*th:2048*ht+1024*(th+1)],
                                PB[:], bcol2, SSI, ALU.add, ALU.mult),
                                waits=[(TT_, tb), (S_, sv), (WX, bx), (TT_, h_t)],
                                inc=V_)
                            ab_v = h_v
                for ts_ in range(NT):
                    ph = ts_ % 2
                    Y = PA[:, 0:1024] if ph == 0 else PB[:, 0:1024]
                    tv = b.op("tensor", lambda e, ts_=ts_, Y=Y, hq=hq: [
                        e.matmul(Y[:, 512*q:512*(q+1)],
                                 ABIG[:, 2048*k+128*ts_:2048*k+128*(ts_+1)],
                                 W3Q[hq % 2][:, k, 512*q:512*(q+1)],
                                 start=(k == 0), stop=(hq != 0 and k == 3))
                        for k in range(4) for q in range(2)][-1],
                        waits=[(WG, wg3), (V_, h_v), (V_, y_v[ph]), (S_, ab_s[ph]),
                               (V_, st["pb_v"]) if ph else (V_, 0),
                               (V_, ab_v) if ph else (V_, 0)], inc=TT_)
                    h_t = tv
                    w3_t[hq % 2] = tv
                    if hq == 0:
                        tv = b.op("tensor", lambda e, Y=Y: [
                            e.matmul(Y[:, 512*q:512*(q+1)], ONESRB[:],
                                     GROWB[0:1, 512*q:512*(q+1)],
                                     start=False, stop=True)
                            for q in range(2)][-1], waits=[(WG, gv)], inc=TT_)
                        st["growb_t"] = tv
                    y_t[ph] = tv
                    y_v[ph] = b.op("vector", lambda e, ts_=ts_, Y=Y, ee=ee:
                                   e.scalar_tensor_tensor(
                        AZ[:, D*ts_:D*(ts_+1)], Y, SCOMB[:, 8*ts_+ee:8*ts_+ee+1],
                        AZ[:, D*ts_:D*(ts_+1)], ALU.mult, ALU.add),
                        waits=[(TT_, tv), (V_, comb_v), (V_, zv0)], inc=V_)
                    if ee == NE - 1 and hq == 3:
                        st["ztile_v"][ts_] = y_v[ph]
                    if ph == 1:
                        ab_v = y_v[ph]
        st["zdone"] = V_.n
        set_bars()
        st["bar_g"] = WG.n

    def emit_l1ffn():
        xnt_ts = dict(st["xnt_tile_s"])
        zv0 = st["zdone"]
        W1S = [AW[:, 0:1024], AW[:, 1024:2048]]
        W2S = AW[:, 2048:10240]
        W2Sv = W2S.rearrange("p (fs d) -> p fs d", fs=8)
        gv = b.op("gpsimd", lambda e: e.dma_start(GROWB[0:1, 0:1024], dr["g_b2"]),
                  waits=[(TT_, st.get("growb_t", 0))], inc=WG, dma=True)
        w1_t = {0: st["bar_t"], 1: st["bar_t"]}
        w2_t = st["bar_t"]
        h_v = 0
        h_t = st["bar_t"]
        hv_ph = {0: st["bar_v"], 1: st["bar_v"]}
        y_v = {0: st["bar_v"], 1: st["bar_v"]}
        for fq in range(4):
            w2q = list(range(8))

            def emit_w2(n, fq=fq):
                nonlocal wg2
                for _ in range(n):
                    if not w2q:
                        return
                    fs = w2q.pop(0)
                    gfs = 8*fq + fs
                    wg2 = b.op("gpsimd", lambda e, gfs=gfs, fs=fs: e.dma_start(
                        W2Sv[:, fs, :], dr["g_w2"][128*gfs:128*(gfs+1), :]),
                        waits=[(TT_, w2_t)], inc=WG, dma=True)
            wg2 = None
            for fs in range(8):
                gfs = 8*fq + fs
                par = fs % 2
                wg1 = b.op("gpsimd", lambda e, gfs=gfs, par=par: e.dma_start(
                    W1S[par].rearrange("p (j f) -> p j f", j=NC),
                    dr["g_w1"][gfs].rearrange("(j q) f -> q j f", q=128)),
                    waits=[(TT_, w1_t[par])], inc=WG, dma=True)
                if fs >= 2:
                    emit_w2(3)
                for th in range(2):
                    ph = th
                    A = PA[:, 1024*ph:1024*(ph+1)]
                    tv = b.op("tensor", lambda e, par=par, th=th, A=A: [
                        e.matmul(A[:, 512*q:512*(q+1)],
                                 W1S[par].rearrange("p (j f) -> p j f", j=NC)[:, j, :],
                                 AXNT[:, T*j+1024*th+512*q:T*j+1024*th+512*(q+1)],
                                 start=(j == 0), stop=(j == NC-1))
                        for q in range(2) for j in range(NC)][-1],
                        waits=[(WG, wg1), (V_, xnt_ts[8*th+7]), (V_, hv_ph[ph]),
                               (V_, y_v[ph])], inc=TT_)
                    w1_t[par] = tv
                    h_v = b.op("vector", lambda e, fs=fs, th=th, A=A, gfs=gfs:
                               e.tensor_scalar(
                        ABIG[:, 2048*fs+1024*th:2048*fs+1024*(th+1)], A,
                        B1L[:, gfs:gfs+1], 0.0, ALU.add, ALU.max),
                        waits=[(TT_, tv), (WX, consts_x), (TT_, h_t)], inc=V_)
                    hv_ph[ph] = h_v
            for ts_ in range(NT):
                ph = ts_ % 2
                Y = PA[:, 0:1024] if ph == 0 else PB[:, 0:1024]
                tv = b.op("tensor", lambda e, ts_=ts_, Y=Y, fq=fq: [
                    e.matmul(Y[:, 512*q:512*(q+1)],
                             ABIG[:, 2048*k+128*ts_:2048*k+128*(ts_+1)],
                             W2Sv[:, k, 512*q:512*(q+1)],
                             start=(k == 0), stop=(fq != 0 and k == 7))
                    for k in range(8) for q in range(2)][-1],
                    waits=[(WG, wg2), (V_, h_v), (V_, y_v[ph]), (V_, hv_ph[ph]),
                           (V_, st["pb_v"]) if ph else (V_, 0)], inc=TT_)
                h_t = tv
                w2_t = tv
                if fq == 0:
                    tv = b.op("tensor", lambda e, Y=Y: [
                        e.matmul(Y[:, 512*q:512*(q+1)], ONESRB[:],
                                 GROWB[0:1, 512*q:512*(q+1)], start=False,
                                 stop=True) for q in range(2)][-1],
                        waits=[(WG, gv)], inc=TT_)
                    st["growb_t"] = tv
                y_v[ph] = b.op("vector", lambda e, ts_=ts_, Y=Y: e.tensor_add(
                    AZ[:, D*ts_:D*(ts_+1)], AZ[:, D*ts_:D*(ts_+1)], Y),
                    waits=[(TT_, tv), (V_, zv0)], inc=V_)
                if fq == 3:
                    st["ztile_v"][ts_] = y_v[ph]
        st["zdone"] = V_.n
        set_bars()

    def emit_l1_final_ln(nrm_v):
        zt = st["ztile_v"]
        sqp = 0
        for i in range(NT):
            b.op("vector", lambda e, i=i: e.reduce_sum(
                SROW[:, i:i+1], AZ[:, D*i:D*(i+1)], axis=AX.X),
                waits=[(V_, zt[i])], inc=V_)
            sqp = b.op("scalar", lambda e, i=i: e.activation(
                SQD[:], AZ[:, D*i:D*(i+1)], AF.Square,
                accum_out=SROW[:, 16+i:17+i]),
                waits=[(V_, zt[i]), (S_, sqp)], inc=S_)
        v_st, s_st = V_.n, S_.n
        c1 = b.op("vector", lambda e: e.tensor_scalar_mul(
            SROW[:, 32:48], SROW[:, 0:16], 1.0 / D),
            waits=[(V_, v_st), (S_, s_st)], inc=V_)
        c2 = b.op("vector", lambda e: e.tensor_mul(
            SROW[:, 48:64], SROW[:, 32:48], SROW[:, 32:48]),
            waits=[(V_, c1)], inc=V_)
        vv = b.op("vector", lambda e: e.scalar_tensor_tensor(
            SROW[:, 48:64], SROW[:, 16:32], 1.0 / D, SROW[:, 48:64],
            ALU.mult, ALU.subtract), waits=[(V_, c2)], inc=V_)
        m1 = b.op("scalar", lambda e: e.activation(SROW[:, 96:112], SROW[:, 48:64],
             AF.Ln, bias=EPS5), waits=[(V_, vv), (V_, consts_v)], inc=S_)
        sv = b.op("scalar", lambda e: e.activation(SROW[:, 64:80], SROW[:, 96:112],
                  AF.Exp, scale=-0.5), waits=[(S_, m1)], inc=S_)
        nm = b.op("vector", lambda e: e.tensor_scalar_mul(
            SROW[:, 80:96], SROW[:, 32:48], -1.0), waits=[(V_, c1)], inc=V_)
        for i in range(NT):
            par = i % 2
            yt = AF32[:, 1024*par:1024*(par+1)]
            t1 = b.op("vector", lambda e, i=i, yt=yt: e.tensor_scalar(
                yt, AZ[:, D*i:D*(i+1)], SROW[:, 80+i:81+i], SROW[:, 64+i:65+i],
                ALU.add, ALU.mult), waits=[(S_, sv), (V_, nrm_v), (V_, nm)], inc=V_)
            t2 = b.op("vector", lambda e, i=i, yt=yt: e.tensor_mul(
                AZ[:, D*i:D*(i+1)], yt, NRMB[:]), waits=[(V_, t1)], inc=V_)
            t3 = b.op("vector", lambda e, i=i: e.tensor_add(
                AZ[:, D*i:D*(i+1)], AZ[:, D*i:D*(i+1)], NRMB2[:]),
                waits=[(V_, t2)], inc=V_)
            st["ztile_v"][i] = t3
        st["zdone"] = V_.n
        set_bars()

    def dump_z():
        wv = st["zdone"]
        last = b.op("sync", lambda e: e.dma_start(
            xout.rearrange("(i p) d -> p i d", p=128),
            AZ[:].rearrange("p (i d) -> p i d", i=NT)),
            waits=[(V_, wv)] + bars(), inc=WX, dma=True)
        b.op("sync", lambda e: e.wait_ge(b.WX.h, last))
        b.finalize()
        return nc

    for si, s in enumerate(subs):
        kind = SUB_KIND[s]
        xin_ap, xo_ap = xs_seq[si], xs_seq[si + 1]
        if not st.get("z_is_ln", False):
            emit_ln(xin_ap)
        st["z_is_ln"] = False
        if debug == "stats":
            last = b.op("sync", lambda e: e.dma_start(
                xout[0:128, 0:128], SROW[:]),
                waits=[(V_, st["zdone"])] + bars(), inc=WX, dma=True)
            b.op("sync", lambda e: e.wait_ge(b.WX.h, last))
            b.finalize()
            return nc
        if debug == "ln":
            return dump_z()
        def dump_xnt():
            wv = st["xnt_s"]
            last = b.op("gpsimd", lambda e: e.dma_start(
                xout.rearrange("(a p) d -> p a d", p=128),
                AXNT[:].rearrange("p (a d) -> p a d", a=16)),
                waits=[(V_, wv)] + bars(), inc=WG, dma=True)
            b.op("sync", lambda e: e.wait_ge(b.WG.h, last))
            b.finalize()
            return nc

        if kind == "f":
            nv = row_bcast(NRMB, dr["f_n1w"])
            emit_to_xnt("rms_w", nv)
            if debug == "xnt":
                return dump_xnt()
            nv2 = row_bcast(NRMB2, dr["f_n2w"])
            if emit_attention("f") == "OT_DUMP":
                return nc
            emit_to_xnt("rms_w", nv2, nrmb=NRMB2)
            emit_moe()
        elif kind == "a":
            nv = row_bcast(NRMB, dr["a_gamma"], add=1.0, post_scale=32.0)
            emit_to_xnt("rms0", nv)
            if debug == "xnt":
                return dump_xnt()
            if emit_attention("a") == "OT_DUMP":
                return nc
        elif kind == "t":
            emit_to_xnt("none")
            if emit_attention("t") == "OT_DUMP":
                return nc
        else:
            emit_to_xnt("none")
            nv = row_bcast(NRMB, dr["g_gam"], scale=0.5)
            nv = row_bcast(NRMB2, dr["g_bet"], scale=0.5)
            emit_l1ffn()
            if debug == "body":
                return dump_z()
            emit_l1_final_ln(nv)
        if debug == "sub":
            return dump_z()
        fuse = si < len(subs) - 1
        emit_finale(xin_ap, xo_ap, SUB_ALPHA[s], fuse_ln=fuse)
        st["z_is_ln"] = fuse
    b.op("sync", lambda e: [e.wait_ge(b.WS0.h, b.WS0.n),
                            e.wait_ge(b.WS1.h, b.WS1.n)][-1] and None)
    b.finalize()
    return nc



_CACHE = {}

def _get_prog(key, subs):
    if key not in _CACHE:
        _CACHE[key] = build_program(subs)
    return _CACHE[key]


def kernel(**inputs):
    hp = host_prep(inputs)
    x = np.ascontiguousarray(np.asarray(inputs["L0_attn_x"], dtype=np.float32))
    xs = [np.ascontiguousarray(x[i]) for i in range(8)]
    for g in GROUPS:
        nc = _get_prog(tuple(g), list(g))
        names = [t for t in ("f", "a", "t", "g")
                 if t in {SUB_KIND[s] for s in g}]
        keys = ["c_ident", "c_ones_col", "c_ones_row"]
        if "f" in names:
            keys += ["f_wqkv", "f_bqkv", "f_bqkrot", "f_wout", "f_outb", "f_n1w",
                     "f_n2w", "f_gateT", "f_w1T", "f_w2T", "f_w3T", "f_b1T",
                     "f_b2T", "f_b3", "ropeC", "ropeS"]
        if "a" in names:
            keys += ["a_w", "a_wout", "a_gamma"]
        if "t" in names:
            keys += ["t_w", "t_wr", "stair"]
        if "g" in names:
            keys += ["g_w1", "g_b1T", "g_w2", "g_b2", "g_gam", "g_bet"]
        base = {k: hp[k] for k in keys}
        in_maps = [dict(base, xin=xs[c]) for c in range(8)]
        res = run_bass_kernel_spmd(nc, in_maps, core_ids=list(range(8)))
        xs = [np.asarray(res.results[c]["xout"]) for c in range(8)]
    return np.stack(xs).astype(np.float32)
